# version 17
# baseline (speedup 1.0000x reference)
import math
from contextlib import ExitStack

import numpy as np

import concourse.bass as bass
import concourse.mybir as mybir
from concourse.bass_utils import run_bass_kernel_spmd

F32 = mybir.dt.float32
F32R = mybir.dt.float32r
BF16 = mybir.dt.bfloat16
F8 = mybir.dt.float8e4
I32 = mybir.dt.int32
AF = mybir.ActivationFunctionType
ALU = mybir.AluOpType
AX = mybir.AxisListType

T = 2048
NB = 8
LN2 = math.log(2.0)
INV_LN2 = 1.0 / LN2
LAMBDA_INIT = 0.8 - 0.6 * math.exp(-0.3 * 4)
RMS_EPS = float(np.finfo(np.float32).eps)
LN_EPS = 1e-5
SCHED_LEVEL = 4
N_HEADS_L0 = 8
ATTN_STAGE = 4
PROJ_ON = 1
LAMBDA_ON = 1
PREP_STAGE = 3



def pack_lhsT(W):
    I, O = W.shape
    return np.ascontiguousarray(
        W.reshape(I // 128, 128, O // 128, 128)
        .transpose(2, 1, 0, 3)
        .reshape(O // 128, 128, I).astype(np.float32))


def pack_bias_cols(b):
    n = b.shape[0] // 128
    return np.ascontiguousarray(np.asarray(b, np.float32).reshape(n, 128).T)


def rope_tables_l0():
    HD = 64
    inv_freq = 1.0 / 10000.0 ** (np.arange(0, HD, 2, dtype=np.float32) / HD)
    fr = np.outer(np.arange(T, dtype=np.float32), inv_freq)
    cos, sin = np.cos(fr).T, np.sin(fr).T
    Cf = np.empty((128, T), np.float32)
    Sf = np.empty((128, T), np.float32)
    for blk in range(4):
        Cf[blk * 32:(blk + 1) * 32] = cos
        Sf[blk * 32:(blk + 1) * 32] = sin if blk % 2 == 0 else -sin
    return Cf, Sf


def rope_tables_l1():
    DR = 128
    inv_freq = 1.0 / 10000.0 ** (np.arange(0, DR, 2, dtype=np.float32) / DR)
    fr = np.outer(np.arange(T, dtype=np.float32), inv_freq)
    cos, sin = np.cos(fr).T, np.sin(fr).T
    Cf = np.empty((128, T), np.float32)
    Sf = np.empty((128, T), np.float32)
    Cf[0:64] = cos
    Cf[64:128] = cos
    Sf[0:64] = -sin
    Sf[64:128] = sin
    return Cf, Sf


def perm_mat(n):
    P = np.zeros((128, 128), np.float32)
    for a in range(128):
        P[(a // n ^ 1) * n + (a % n), a] = 1.0
    return P


def causal_maskbias():
    m = np.zeros((128, 128), np.float32)
    m[np.tril_indices(128, -1)] = -60.0
    return m


class Prog:
    ENGINES = ("tensor", "vector", "scalar", "gpsimd", "sync")

    def __init__(self):
        self.th = {e: [] for e in self.ENGINES}
        self.sem_counts = {}
        self.hwm = {e: {} for e in self.ENGINES}
        self.default_waits = []

    def op(self, eng, fn, waits=(), inc=None, inc_step=1):
        tok = None
        if inc is not None:
            v = self.sem_counts.get(inc, 0) + inc_step
            self.sem_counts[inc] = v
            tok = (inc, v)
        rw = []
        for w in list(self.default_waits) + list(waits):
            if w is None:
                continue
            s, v = w
            if self.hwm[eng].get(s, 0) < v:
                self.hwm[eng][s] = v
                rw.append((s, v))
        step = inc_step

        def thunk(engine, H):
            for (s, v) in rw:
                engine.wait_ge(H[s], v)
            ins = fn(engine, H)
            if inc is not None:
                if ins is None:
                    engine.sem_inc(H[inc], step)
                else:
                    ins.then_inc(H[inc], step)
        self.th[eng].append(thunk)
        return tok


def build_program():
    nc = bass.Bass("TRN2", target_bir_lowering=False)
    P = Prog()

    def din(name, shape, dtype=F32):
        return nc.dram_tensor(name, list(shape), dtype,
                              kind="ExternalInput").ap()

    x_in = din("x", [T, 1024])
    y_out = nc.dram_tensor("y", [T, 1024], F32, kind="ExternalOutput").ap()

    wq0 = din("wq0", [8, 128, 1024])
    wk0 = din("wk0", [8, 128, 1024])
    wv0 = din("wv0", [8, 128, 1024])
    wproj0 = din("wproj0", [8, 128, 1024])
    wff = []
    for l in range(2):
        wff.append(dict(
            w1=din(f"w1_{l}", [32, 128, 1024]),
            w2=din(f"w2_{l}", [32, 128, 1024]),
            b1=din(f"b1c_{l}", [128, 32]),
            b2=din(f"b2c_{l}", [128, 8])))
    wq1 = din("wq1", [2, 128, 1024])
    bq1 = din("bq1", [128, 2])
    wqc1 = din("wqc1", [7, 128, 256])
    bqc1 = din("bqc1", [128, 7])
    wqr1 = din("wqr1", [1, 128, 1024])
    bqr1 = din("bqr1", [128, 1])
    wkv1 = din("wkv1", [2, 128, 1024])
    bkv1 = din("bkv1", [128, 2])
    wkr1 = din("wkr1", [1, 128, 1024])
    bkr1 = din("bkr1", [128, 1])
    wkc1 = din("wkc1", [7, 128, 256])
    bkc1 = din("bkc1", [128, 7])
    wv1 = din("wv1", [8, 128, 256])
    bv1 = din("bv1", [128, 8])
    wo1 = din("wo1", [8, 128, 1024])
    bo1 = din("bo1", [128, 8])
    lam_in = din("lamv", [1, 256])
    identc = din("identc", [128, 128])
    permc0 = din("permc0", [128, 128])
    permc1 = din("permc1", [128, 128])
    maskc = din("maskc", [128, 128])
    ropec0 = din("ropec0", [2, 128, T])
    ropec1 = din("ropec1", [2, 128, T])
    w2q_sc = [nc.dram_tensor(f"w2qsc_{l}", [32, 128, 1024], F8,
                             kind="Internal").ap() for l in range(2)]

    es = ExitStack()

    def sbt(name, shape, dtype):
        return es.enter_context(nc.sbuf_tensor(name, shape, dtype))

    x_res = sbt("x_res", [128, 16 * 1024], F32)
    xnT = sbt("xnT", [128, 8 * T], BF16)
    arA = sbt("arA", [128, 8 * T], BF16)
    aT = arA[:, 0:8 * T]
    hqv = arA.bitcast(F8)
    arB = sbt("arB", [128, 6 * T], BF16)
    cqb = arB[:, 0:2 * T]
    ckvb = arB[:, 2 * T:4 * T]
    qh = sbt("qh", [128, T], BF16)
    kh = sbt("kh", [128, T], BF16)
    VS = 136
    vtok = sbt("vtok", [128, 16 * VS], BF16)
    p0 = sbt("p0", [128, 2 * T], BF16)
    p1 = sbt("p1", [128, 2 * T], BF16)
    pcb = sbt("pcb", [128, 1024], BF16)
    NSCR = 8
    scr = arB.bitcast(F32)
    scri = arB.bitcast(I32)[:, 4096:5120]
    scrb = arB[:, 10240:11264]
    rmsr = sbt("rmsr", [128, 512], F32)
    rmsb = sbt("rmsb", [128, 512], F32)
    wbA = sbt("wbA", [128, 1024], BF16)
    wbB = sbt("wbB", [128, 1024], BF16)
    wbC = sbt("wbC", [128, 1024], BF16)
    w2qs = sbt("w2qs", [128, 2 * 1024], F8)
    ident = sbt("ident", [128, 128], BF16)
    perm0 = sbt("perm0", [128, 128], BF16)
    perm1 = sbt("perm1", [128, 128], BF16)
    maskb = sbt("maskb", [128, 128], BF16)
    ropeC = sbt("ropeC", [128, T], BF16)
    ropeS = sbt("ropeS", [128, T], BF16)
    onesq = sbt("onesq", [128, 128], BF16)
    b1cols = [sbt(f"b1col{l}", [128, 32], F32) for l in range(2)]
    b2cols = [sbt(f"b2col{l}", [128, 8], F32) for l in range(2)]
    b2qcols = [sbt(f"b2qcol{l}", [128, 8], F32) for l in range(2)]
    bias1 = sbt("bias1", [128, 12], F32)
    bias2 = sbt("bias2", [128, 12], F32)
    bvcol = sbt("bvcol", [128, 8], F32)
    bocol = sbt("bocol", [128, 8], F32)
    lamt = sbt("lamt", [1, 256], F32)
    smlb = sbt("smlb", [1, 8], BF16)
    sml = sbt("sml", [128, 128], F32)
    sml2 = sbt("sml2", [128, 32], F32)
    cst = sbt("cst", [128, 16], F32)
    stats = sbt("stats", [128, 12], F32)
    aggr = sbt("aggr", [128, 2 * 16], F32)
    rstd = sbt("rstd", [128, 16], F32)
    bcols = sbt("bcols", [128, 16], F32)

    psA = es.enter_context(nc.psum_tensor("psA", [128, 512], F32))
    psB = es.enter_context(nc.psum_tensor("psB", [128, 512], F32))
    psC = es.enter_context(nc.psum_tensor("psC", [128, 512], F32))
    psD = es.enter_context(nc.psum_tensor("psD", [128, 512], F32))
    psT = es.enter_context(nc.psum_tensor("psT", [128, 1024], BF16))
    psU = es.enter_context(nc.psum_tensor("psU", [128, 1024], BF16))
    psS = es.enter_context(nc.psum_tensor("psS", [128, 512], F32))
    psR = es.enter_context(nc.psum_tensor("psR", [128, 512], F32))
    PSAB = [psA, psB]
    PSCD = [psC, psD]
    PS4 = [psA, psB, psC, psD]

    sem_names = ["T", "V", "S", "G", "Dc", "Dg", "DA", "DB", "DC",
                 "W2A", "W2B", "DZ0", "DZ1", "DS0", "DS1",
                 "Dr", "Do"]
    H = {n: es.enter_context(nc.semaphore(n)) for n in sem_names}

    def T_(fn, waits=()):
        return P.op("tensor", fn, waits, inc="T")

    def T0_(fn, waits=()):
        return P.op("tensor", fn, waits, inc=None)

    def V_(fn, waits=()):
        return P.op("vector", fn, waits, inc="V")

    def S_(fn, waits=()):
        return P.op("scalar", fn, waits, inc="S")

    def G_(fn, waits=()):
        return P.op("gpsimd", fn, waits, inc="G")

    def DMA(eng, sem, fn, waits=()):
        return P.op(eng, fn, waits, inc=sem, inc_step=16)

    def DMAw(fn, waits=(), sem="Dc"):
        return P.op("sync", fn, waits, inc=sem, inc_step=16)

    def DMAg(fn, waits=(), sem="Dg"):
        return P.op("gpsimd", fn, waits, inc=sem, inc_step=16)

    def DMAo(fn, waits=()):
        return P.op("sync", fn, waits, inc="Do", inc_step=16)

    last_use = {}

    def ruse(*keys):
        out = []
        for k in keys:
            out.extend(last_use.get(k, []))
        return out

    def rdone(key, toks):
        if not isinstance(toks, list):
            toks = [toks]
        last_use[key] = [t for t in toks if t is not None]

    def radd(key, tok):
        if tok is not None:
            last_use.setdefault(key, []).append(tok)

    def emit_bcast2(srcs, dsts, waits):
        tm = None
        for i, s2 in enumerate(srcs):
            tm = T_(lambda e, h, s2=s2, i=i: e.matmul(
                psS[:, 0:2], onesq[0:1, 0:128], s2,
                start=(i == 0), stop=(i == len(srcs) - 1)),
                waits=list(waits) + [t_blkr] + (ruse("psS")
                                               if i == 0 else []))
        out = []
        for j, d in enumerate(dsts):
            if d is None:
                continue
            out.append(V_(lambda e, h, j=j, d=d: e.tensor_copy(
                d, psS[:, j:j + 1]), waits=[tm]))
        rdone("psS", out)
        return out[-1]

    CSTV = {}
    CSTV_FROZEN = [False]

    def cstcol(val):
        val = float(val)
        if val not in CSTV:
            assert not CSTV_FROZEN[0], f"const col {val} not preregistered"
            i = len(CSTV)
            assert i < 16, "too many const cols"
            CSTV[val] = i
            V_(lambda e, h, i=i, val=val: e.memset(cst[:, i:i + 1], val))
        return cst[:, CSTV[val]:CSTV[val] + 1]

    t_ident = DMAg(lambda e, h: e.dma_start(ident[:], identc))
    t_perm0 = DMAg(lambda e, h: e.dma_start(perm0[:], permc0))
    t_perm1 = DMAg(lambda e, h: e.dma_start(perm1[:], permc1))
    t_ropeC = DMAg(lambda e, h: e.dma_start(ropeC[:], ropec0[0]))
    t_ropeS = DMAg(lambda e, h: e.dma_start(ropeS[:], ropec0[1]))
    t_mask = DMAg(lambda e, h: e.dma_start(maskb[:], maskc))
    t_blkr = V_(lambda e, h: e.memset(onesq[:], 1.0))
    t_blkc = t_blkr
    V_(lambda e, h: e.memset(smlb[:], 0.0))
    t_lam = DMAw(lambda e, h: e.dma_start(lamt[:], lam_in))
    t_b1 = [DMAw(lambda e, h, l=l: e.dma_start(b1cols[l][:], wff[l]["b1"]))
            for l in range(2)]
    t_b2 = [DMAw(lambda e, h, l=l: e.dma_start(b2cols[l][:], wff[l]["b2"]))
            for l in range(2)]
    x3 = x_in.rearrange("(n p) c -> n p c", p=128)
    y3 = y_out.rearrange("(n p) c -> n p c", p=128)
    t_x = [DMAo(lambda e, h, i=i: e.dma_start(
        x_res[:, i * 1024:(i + 1) * 1024], x3[i])) for i in range(16)]
    t_bias_l = [
        DMAw(lambda e, h: e.dma_start(bias1[:, 0:2], bq1)),
        DMAw(lambda e, h: e.dma_start(bias1[:, 2:9], bqc1)),
        DMAw(lambda e, h: e.dma_start(bias1[:, 9:10], bqr1)),
        DMAw(lambda e, h: e.dma_start(bias2[:, 0:2], bkv1)),
        DMAw(lambda e, h: e.dma_start(bias2[:, 2:9], bkc1)),
        DMAw(lambda e, h: e.dma_start(bias2[:, 9:10], bkr1)),
        DMAw(lambda e, h: e.dma_start(bvcol[:], bv1)),
        DMAw(lambda e, h: e.dma_start(bocol[:], bo1)),
    ]
    for kb in range(16):
        V_(lambda e, h, kb=kb: e.memset(vtok[:, kb * VS + 64:kb * VS + 65],
                                        1.0))
    for _v in (0.0, 2.0, 4096.0, LN_EPS, RMS_EPS, 2.0 ** -60, -8.0,
               -math.log(8.0), math.log(1.0 - LAMBDA_INIT),
               3 * LN2, -3 * LN2, 4 * LN2, -4 * LN2):
        cstcol(_v)
    CSTV_FROZEN[0] = True
    tok_c = P.op("vector", lambda e, h: None,
                 waits=[("Dc", P.sem_counts["Dc"]),
                        ("Dg", P.sem_counts.get("Dg", 0))], inc="V")
    P.default_waits = [tok_c]
    t_ident = t_perm0 = t_perm1 = t_ropeC = t_ropeS = tok_c
    t_mask = t_blkc = t_blkr = t_lam = t_ones = tok_c
    t_b1 = [tok_c, tok_c]
    t_b2 = [tok_c, tok_c]
    t_x = [t_x[-1]]
    t_bias = [tok_c]

    def emit_layernorm(dep):
        tA = []
        for i in range(16):
            xt = x_res[:, i * 1024:(i + 1) * 1024]
            tsa = V_(lambda e, h, xt=xt: e.bn_stats(stats[:, 0:6],
                                                    xt[:, 0:512]),
                     waits=list(dep) + ruse("aggr", "stats"))
            tsb = V_(lambda e, h, xt=xt: e.bn_stats(stats[:, 6:12],
                                                    xt[:, 512:1024]),
                     waits=[tsa])
            ta_ = V_(lambda e, h, i=i: e.bn_aggr(
                aggr[:, 2 * i:2 * i + 2], stats[:, 0:12]), waits=[tsb])
            rdone("stats", ta_)
            tA.append(ta_)
        var_view = aggr[:, 1:32:2]
        tl = S_(lambda e, h: e.activation(rstd[:], var_view, AF.Ln,
                                          bias=cstcol(LN_EPS)),
                waits=[tA[-1]])
        tl = S_(lambda e, h: e.activation(rstd[:], rstd[:], AF.Exp,
                                          scale=-0.5), waits=[tl])
        rdone("aggr", tl)
        toks = []
        for i in range(16):
            xt = x_res[:, i * 1024:(i + 1) * 1024]
            tv = V_(lambda e, h, xt=xt, i=i: e.tensor_scalar(
                pcb[:, 0:1024], xt, aggr[:, 2 * i:2 * i + 1],
                rstd[:, i:i + 1], ALU.subtract, ALU.mult),
                waits=[tl] + ruse("pcb"))
            last = None
            for c in range(8):
                pd, key = (psT, "psTg") if c % 2 == 0 \
                    else (psU, ("psU", 0))
                tt = T_(lambda e, h, c=c, pd=pd: e.transpose(
                    pd[:, 0:128], pcb[:, c * 128:(c + 1) * 128], ident[:]),
                    waits=[tv, t_ident] + ruse(key))
                tc = V_(lambda e, h, c=c, i=i, pd=pd: e.tensor_copy(
                    xnT[:, c * T + i * 128: c * T + i * 128 + 128],
                    pd[:, 0:128]), waits=[tt])
                rdone(key, tc)
                last = tc
                toks.append(tc)
            rdone("pcb", last)
        return toks

    qpar = [0]
    chain_done = {(0, 0): [], (0, 1): [], (1, 0): [], (1, 1): []}

    def emit_quant(z_ap, w, out_ap, kind, bias_col=None, relu=False,
                   skip_stage1=False, waits=(), lane=0):
        man, mine = (3, -6) if kind == "e4m3" else (10, -14)
        p = qpar[0]
        qpar[0] ^= 1
        c0 = lane * 256
        keys = [(p, 0), (p, 1)] if w > 256 else [(p, lane)]

        def fs(j):
            return scr[:, (4 * p + j) * 512 + c0:
                       (4 * p + j) * 512 + c0 + w]
        iA = scri[:, p * 512 + c0:p * 512 + c0 + w]
        SCB = scrb[:, p * 512 + c0:p * 512 + c0 + w]
        wl = list(waits)
        for k in keys:
            wl = wl + chain_done[k]
        if skip_stage1:
            A = z_ap
            z_done = None
            pre = wl
        else:
            bcol = bias_col if bias_col is not None else cstcol(0.0)
            t1 = S_(lambda e, h: e.activation(
                fs(0), z_ap, AF.Relu if relu else AF.Identity, bias=bcol),
                waits=wl)
            A = fs(0)
            z_done = t1
            pre = [t1]
        if relu:
            lin = A
        else:
            tab = S_(lambda e, h: e.activation(fs(1), A, AF.Abs), waits=pre)
            lin = fs(1)
            pre = [tab]
        tl = S_(lambda e, h: e.activation(fs(2), lin, AF.Ln,
                                          scale=float(2.0 ** 46),
                                          bias=cstcol(2.0)), waits=pre)
        ta = S_(lambda e, h: e.activation(iA, fs(2), AF.Identity,
                                          scale=INV_LN2), waits=[tl])
        tb = V_(lambda e, h: e.scalar_tensor_tensor(
            fs(1), iA, LN2, fs(2), ALU.mult, ALU.is_gt), waits=[ta])
        tn = V_(lambda e, h: e.scalar_tensor_tensor(
            fs(3), fs(1), float(46 - mine), iA, ALU.add, ALU.subtract),
            waits=[tb])
        tcl = V_(lambda e, h: e.tensor_scalar(
            fs(1), fs(3), float(-2 * mine), None, ALU.min), waits=[tn])
        tsc = S_(lambda e, h: e.activation(
            SCB, fs(1), AF.Exp, scale=-LN2,
            bias=cstcol((-mine - man) * LN2)), waits=[tcl])
        trs = S_(lambda e, h: e.activation(
            fs(3), fs(1), AF.Exp, scale=LN2,
            bias=cstcol((mine + man) * LN2)), waits=[tcl])
        tt_ = V_(lambda e, h: e.tensor_tensor(fs(2), A, fs(3), ALU.mult),
                 waits=[trs])
        z_done2 = tt_ if skip_stage1 else z_done
        ta2 = S_(lambda e, h: e.activation(iA, fs(2), AF.Identity,
                                           bias=cstcol(4096.0)),
                 waits=[tt_])
        tc2 = V_(lambda e, h: e.scalar_tensor_tensor(
            fs(1), iA, -4095.5, fs(2), ALU.add, ALU.is_le), waits=[ta2])
        tr = V_(lambda e, h: e.tensor_tensor(fs(3), iA, fs(1), ALU.add),
                waits=[tc2])
        tq = V_(lambda e, h: e.scalar_tensor_tensor(
            out_ap, fs(3), -4096.0, SCB, ALU.add, ALU.mult),
            waits=[tr, tsc])
        for k in keys:
            chain_done[k] = [tq]
        return tq, z_done2

    W2R = {}

    def w2q_gen(l):
        toks = []
        for hc in range(32):
            done = []
            for half in range(2):
                pn = qpar[0]
                zin = scr[:, (4 * pn) * 512:(4 * pn) * 512 + 512]
                wl = chain_done[(pn, 0)] + chain_done[(pn, 1)]
                tw = DMAw(lambda e, h, hc=hc, half=half, zin=zin, l=l:
                          e.dma_start(zin, wff[l]["w2"][hc][:,
                                      half * 512:half * 512 + 512]),
                          waits=wl, sem="DZ0" if pn == 0 else "DZ1")
                tq, _ = emit_quant(
                    zin, 512,
                    w2qs[:, half * 512:half * 512 + 512],
                    "e4m3", skip_stage1=True,
                    waits=[tw] + ruse("w2qs"))
                done.append(tq)
            ts_ = DMAw(lambda e, h, hc=hc, l=l: e.dma_start(
                w2q_sc[l][hc], w2qs[:, 0:1024]), waits=done,
                sem="DS0" if l == 0 else "DS1")
            rdone("w2qs", ts_)
            toks.append(ts_)
            W2R[l] = ts_
            yield

    def emit_lambda():
        t1 = V_(lambda e, h: e.tensor_tensor(
            sml[0:1, 0:128], lamt[0:1, 0:128], lamt[0:1, 128:256],
            ALU.mult), waits=[t_lam])
        t2a = V_(lambda e, h: e.reduce_sum(sml2[0:1, 0:1], sml[0:1, 0:64],
                                           AX.X), waits=[t1])
        t2b = V_(lambda e, h: e.reduce_sum(sml2[0:1, 1:2],
                                           sml[0:1, 64:128], AX.X),
                 waits=[t1])
        t3 = S_(lambda e, h: e.activation(sml2[0:1, 2:4], sml2[0:1, 0:2],
                                          AF.Exp), waits=[t2a, t2b])
        t6 = V_(lambda e, h: e.scalar_tensor_tensor(
            sml2[0:1, 4:5], sml2[0:1, 3:4], -LAMBDA_INIT,
            sml2[0:1, 2:3], ALU.add, ALU.subtract), waits=[t3])
        th = V_(lambda e, h: e.tensor_copy(smlb[0:1, 0:1],
                                           sml2[0:1, 4:5]), waits=[t6])
        tl_ = V_(lambda e, h: e.scalar_tensor_tensor(
            smlb[0:1, 2:3], smlb[0:1, 0:1], -1.0, sml2[0:1, 4:5],
            ALU.mult, ALU.add), waits=[th])
        t7 = emit_bcast2([smlb[0:1, 0:2], smlb[0:1, 2:4]],
                         [sml2[:, 5:6], None], [tl_])
        return t7

    NLAM = sml2[:, 5:6]

    def l0_prep_side(side, wbuf, tw, rawbuf, outbuf, xn_dep):
        out_toks = []
        for tc in range(4):
            sl = PSAB[tc % 2][:, 0:512]
            pk = ("pab", tc % 2)
            xw = ([xn_dep[8 * (4 * tc + 3) + 7]]
                  if len(xn_dep) == 128 else xn_dep)
            mt = None
            for ic in range(8):
                mt = (T_ if ic == 7 else T0_)(
                    lambda e, h, ic=ic, tc=tc, sl=sl, wbuf=wbuf:
                        e.matmul(sl, wbuf[:, ic * 128:(ic + 1) * 128],
                                 xnT[:, ic * T + tc * 512:
                                     ic * T + tc * 512 + 512],
                                 start=(ic == 0), stop=(ic == 7)),
                        waits=(([tw] + xw + ruse(pk))
                               if ic == 0 else []))
            if PREP_STAGE <= 0:
                rdone(pk, mt)
                out_toks.append(mt)
                continue
            t_raw = V_(lambda e, h, tc=tc, sl=sl, rawbuf=rawbuf:
                       e.tensor_copy(rawbuf[:, tc * 512:(tc + 1) * 512],
                                     sl), waits=[mt])
            sq = outbuf[:, tc * 512:(tc + 1) * 512]
            t_sq = S_(lambda e, h, sl=sl, sq=sq: e.activation(
                sq, sl, AF.Square), waits=[mt, t_raw])
            rdone(pk, [t_raw, t_sq])
            if PREP_STAGE <= 1:
                out_toks.append(t_sq)
                rdone("pcb", t_raw)
                continue
            t_r0 = T_(lambda e, h, sq=sq: e.matmul(
                psS[:, 0:512], onesq[0:64, 0:128], sq[0:64, :],
                start=True, stop=True),
                waits=[t_sq, t_blkc] + ruse("psS"))
            t_r1 = T_(lambda e, h, sq=sq: e.matmul(
                psR[:, 0:512], onesq[64:128, 0:128], sq[64:128, :],
                start=True, stop=True),
                waits=[t_sq] + ruse("psR"))
            bias_r = cstcol(-math.log(8.0)) if side == 0 else cstcol(0.0)
            tu0 = S_(lambda e, h: e.activation(
                rmsr[0:64, 0:512], psS[0:64, 0:512], AF.Ln,
                scale=1.0 / 64.0, bias=cstcol(RMS_EPS)[0:64, :]),
                waits=[t_r0] + ruse("rmsr"))
            rdone("psS", tu0)
            tu1 = S_(lambda e, h: e.activation(
                rmsr[64:128, 0:512], psR[64:128, 0:512], AF.Ln,
                scale=1.0 / 64.0, bias=cstcol(RMS_EPS)[64:128, :]),
                waits=[t_r1])
            rdone("psR", tu1)
            t_bc = S_(lambda e, h, bias_r=bias_r: e.activation(
                rmsb[:, 0:512], rmsr[:, 0:512], AF.Exp, scale=-0.5,
                bias=bias_r), waits=[tu0, tu1] + ruse("rmsb"))
            rdone("rmsr", t_bc)
            if PREP_STAGE <= 2:
                out_toks.append(t_bc)
                continue
            t_sw = T_(lambda e, h, tc=tc, rawbuf=rawbuf: e.matmul(
                psR[:, 0:512], perm0[:],
                rawbuf[:, tc * 512:(tc + 1) * 512], start=True, stop=True),
                waits=[t_raw, t_perm0] + ruse("psR"))
            tm1 = V_(lambda e, h, tc=tc, rawbuf=rawbuf: e.tensor_tensor(
                pcb[:, 0:512], rawbuf[:, tc * 512:(tc + 1) * 512],
                ropeC[:, tc * 512:(tc + 1) * 512], ALU.mult),
                waits=[t_raw, t_ropeC] + ruse("pcb"))
            ts2 = V_(lambda e, h, tc=tc: e.tensor_tensor(
                pcb[:, 512:1024], psR[:, 0:512],
                ropeS[:, tc * 512:(tc + 1) * 512], ALU.mult),
                waits=[t_sw, t_ropeS])
            rdone("psR", ts2)
            tsum = V_(lambda e, h: e.tensor_tensor(
                pcb[:, 0:512], pcb[:, 0:512], pcb[:, 512:1024], ALU.add),
                waits=[tm1, ts2])
            tq = V_(lambda e, h, tc=tc, outbuf=outbuf: e.tensor_tensor(
                outbuf[:, tc * 512:(tc + 1) * 512], pcb[:, 0:512],
                rmsb[:, 0:512], ALU.mult), waits=[tsum, t_bc])
            rdone("pcb", tq)
            rdone("rmsb", tq)
            radd("ropes", tq)
            out_toks.append(tq)
        return out_toks

    def l0_head(hd, nlam_tok, xn_dep, out_toks, pump=None):
        tw_q = DMAg(lambda e, h, hd=hd: e.dma_start(wbA[:], wq0[hd]),
                    waits=ruse("wbA"), sem="DA")
        tw_k = DMAg(lambda e, h, hd=hd: e.dma_start(wbB[:], wk0[hd]),
                    waits=ruse("wbB"), sem="DB")
        tw_v = DMAg(lambda e, h, hd=hd: e.dma_start(wbC[:], wv0[hd]),
                    waits=ruse("wbC"), sem="DC")
        tqs = l0_prep_side(0, wbA, tw_q, p0, qh, xn_dep)
        rdone("wbA", tqs[-1])
        tks = l0_prep_side(1, wbB, tw_k, p1, kh, xn_dep)
        rdone("wbB", tks[-1])
        if ATTN_STAGE <= 1:
            out_toks.extend(tqs + tks)
            return out_toks
        vt = []
        for qt in range(16):
            sl = PSCD[qt % 2][:, 0:128]
            pk = ("pcd", qt % 2)
            xw = ([xn_dep[8 * qt + 7]] if len(xn_dep) == 128 else xn_dep)
            mt = None
            for ic in range(8):
                mt = (T_ if ic == 7 else T0_)(
                    lambda e, h, ic=ic, qt=qt, sl=sl: e.matmul(
                    sl, xnT[:, ic * T + qt * 128:ic * T + qt * 128 + 128],
                    wbC[:, ic * 128:(ic + 1) * 128],
                    start=(ic == 0), stop=(ic == 7)),
                    waits=(([tw_v] + xw + ruse(pk))
                           if ic == 0 else []))
            tv = V_(lambda e, h, qt=qt, sl=sl: e.tensor_copy(
                vtok[:, qt * VS:qt * VS + 64], sl[:, 0:64]), waits=[mt])
            tv = V_(lambda e, h, qt=qt, sl=sl: e.tensor_copy(
                vtok[:, qt * VS + 65:qt * VS + 129], sl[:, 64:128]),
                waits=[tv])
            rdone(pk, tv)
            vt.append(tv)
        rdone("wbC", vt[-1])
        if ATTN_STAGE <= 2:
            out_toks.extend(vt)
            return out_toks

        def stageA(qt):
            par = qt % 2
            exp_by = []
            for side, pbuf in ((0, p0[:, par * T:par * T + T]),
                               (1, p1[:, par * T:par * T + T])):
                side_toks = []
                ng = (qt + 4) // 4
                for g in range(ng):
                    jn = min(4, qt + 1 - 4 * g)
                    sl = PSAB[g % 2][:, 0:512]
                    pk = ("pab", g % 2)
                    mlast = None
                    for j in range(jn):
                        kb = 4 * g + j
                        mlast = (T_ if j == jn - 1 else T0_)(
                            lambda e, h, j=j, kb=kb, side=side, qt=qt,
                            sl=sl: e.matmul(
                            sl[:, j * 128:(j + 1) * 128],
                            kh[64 * side:64 * side + 64,
                               kb * 128:(kb + 1) * 128],
                            qh[64 * side:64 * side + 64,
                               qt * 128:(qt + 1) * 128],
                            start=True, stop=True),
                            waits=(([tqs[-1], tks[-1]] + ruse(pk))
                                   if j == 0 else []))
                    wl = [mlast, t_mask]
                    if g == ng - 1:
                        jd = qt - 4 * g
                        tmask = V_(lambda e, h, jd=jd, sl=sl:
                                   e.tensor_tensor(
                            sl[:, jd * 128:(jd + 1) * 128],
                            sl[:, jd * 128:(jd + 1) * 128],
                            maskb[:], ALU.add), waits=wl)
                        wl = [tmask]
                    texp = S_(lambda e, h, jn=jn, g=g, pbuf=pbuf,
                              sl=sl: e.activation(
                        pbuf[:, g * 512:g * 512 + jn * 128],
                        sl[:, 0:jn * 128], AF.Exp, bias=cstcol(-8.0)),
                        waits=wl + ruse(("pex", side, par)))
                    rdone(pk, texp)
                    side_toks.append(texp)
                exp_by.append(side_toks)
            return exp_by

        def stageB(qt, exp_by):
            par = qt % 2
            pvps = PSCD[par]
            pvk = ("pcd", par)
            pvt = None
            for side, base in ((0, 0), (1, 256)):
                pbuf = (p0 if side == 0 else p1)[:, par * T:par * T + T]
                for kb in range(qt + 1):
                    pvt = (T_ if kb == qt else T0_)(
                        lambda e, h, kb=kb, qt=qt, base=base, pbuf=pbuf:
                        e.matmul(
                        pvps[:, base:base + 129],
                        pbuf[:, kb * 128:(kb + 1) * 128],
                        vtok[:, kb * VS:kb * VS + 129],
                        start=(kb == 0), stop=(kb == qt)),
                        waits=[exp_by[side][kb // 4], vt[kb]] +
                        (ruse(pvk) if kb == 0 and side == 0 else []))
                rdone(("pex", side, par), pvt)
            c8 = 8 + 16 * par
            rz0 = V_(lambda e, h, c8=c8: e.reciprocal(
                sml2[:, c8:c8 + 1], pvps[:, 64:65]), waits=[pvt])
            rz1 = V_(lambda e, h, c8=c8: e.reciprocal(
                sml2[:, c8 + 1:c8 + 2], pvps[:, 320:321]), waits=[pvt])
            t1a = V_(lambda e, h, c8=c8: e.tensor_scalar(
                pcb[:, 128:192], pvps[:, 256:320], sml2[:, c8 + 1:c8 + 2],
                NLAM, ALU.mult, ALU.mult),
                waits=[rz1, nlam_tok] + ruse("pcb"))
            t1b = V_(lambda e, h, c8=c8: e.tensor_scalar(
                pcb[:, 192:256], pvps[:, 321:385], sml2[:, c8 + 1:c8 + 2],
                NLAM, ALU.mult, ALU.mult), waits=[rz1])
            t2a = V_(lambda e, h, c8=c8: e.scalar_tensor_tensor(
                pcb[:, 0:64], pvps[:, 0:64], sml2[:, c8:c8 + 1],
                pcb[:, 128:192], ALU.mult, ALU.add),
                waits=[rz0, t1a])
            t2b = V_(lambda e, h, c8=c8: e.scalar_tensor_tensor(
                pcb[:, 64:128], pvps[:, 65:129], sml2[:, c8:c8 + 1],
                pcb[:, 192:256], ALU.mult, ALU.add), waits=[t2a, t1b])
            rdone(pvk, t2b)
            tst = V_(lambda e, h: e.bn_stats(
                stats[:, 0:6], pcb[:, 0:128]),
                waits=[t2b] + ruse("aggr0", "stats"))
            tag = V_(lambda e, h: e.bn_aggr(aggr[:, 0:2], stats[:, 0:6]),
                     waits=[tst])
            rdone("stats", tag)
            tu = S_(lambda e, h: e.activation(sml2[:, 11:12], aggr[:, 1:2],
                                              AF.Ln, bias=cstcol(LN_EPS)),
                    waits=[tag] + ruse("lnsml"))
            tr_ = S_(lambda e, h: e.activation(
                sml2[:, 12:13], sml2[:, 11:12], AF.Exp, scale=-0.5,
                bias=cstcol(math.log(1.0 - LAMBDA_INIT))), waits=[tu])
            tln = V_(lambda e, h: e.tensor_scalar(
                pcb[:, 256:384], pcb[:, 0:128], aggr[:, 0:1],
                sml2[:, 12:13], ALU.subtract, ALU.mult),
                waits=[tr_])
            rdone("aggr0", tln)
            rdone("lnsml", tln)
            ttr = T_(lambda e, h: e.transpose(psU[:, 0:128],
                                              pcb[:, 256:384],
                                              ident[:]),
                     waits=[tln] + ruse(("psU", 0)))
            tev = V_(lambda e, h, hd=hd, qt=qt: e.tensor_copy(
                aT[:, hd * T + qt * 128:hd * T + qt * 128 + 128],
                psU[:, 0:128]), waits=[ttr])
            rdone(("psU", 0), tev)
            rdone("pcb", [tln, ttr, tev])
            out_toks.append(tev)

        prev = None
        for qt in range(16):
            eb = stageA(qt)
            if prev is not None:
                stageB(qt - 1, prev)
                if pump is not None:
                    pump(1)
            prev = eb
        stageB(15, prev)
        if pump is not None:
            pump(1)
        return out_toks

    def emit_out_proj(wdram, src, src_toks, bias_colst, n_oc, xadd_toks):
        for oc in range(n_oc):
            par = oc % 2
            wb = wbA if par == 0 else wbB
            tw = DMAg(lambda e, h, oc=oc, wb=wb: e.dma_start(
                wb[:], wdram[oc]),
                waits=ruse("wbA" if par == 0 else "wbB"),
                sem="DA" if par == 0 else "DB")
            mlast = None
            for tc in range(4):
                qd = tc % 2
                sl = PSAB[qd][:, 0:512]
                pk = ("pab", qd)
                stc = ([src_toks[tc]] if len(src_toks) == 4
                       else src_toks)
                mt = None
                for ic in range(8):
                    mt = (T_ if ic == 7 else T0_)(
                        lambda e, h, ic=ic, tc=tc, sl=sl, wb=wb:
                            e.matmul(sl,
                                     wb[:, ic * 128:(ic + 1) * 128],
                                     src[:, ic * T + tc * 512:
                                         ic * T + tc * 512 + 512],
                                     start=(ic == 0), stop=(ic == 7)),
                            waits=(([tw] + stc + ruse(pk))
                                   if ic == 0 else []))
                mlast = mt
                if bias_colst is not None:
                    tev = V_(lambda e, h, sl=sl, oc=oc: e.tensor_scalar(
                        pcb[:, 0:512], sl, bias_colst[:, oc:oc + 1], None,
                        ALU.add), waits=[mt] + ruse("pcb"))
                else:
                    tev = V_(lambda e, h, sl=sl: e.tensor_copy(
                        pcb[:, 0:512], sl), waits=[mt] + ruse("pcb"))
                rdone(pk, tev)
                lastadd = None
                for j in range(4):
                    ttr = T_(lambda e, h, j=j: e.transpose(
                        psU[:, 128:256], pcb[:, j * 128:(j + 1) * 128],
                        ident[:]), waits=[tev] + ruse(("psU", 1)))
                    tqi = tc * 4 + j
                    tadd = V_(lambda e, h, tqi=tqi, oc=oc:
                              e.scalar_tensor_tensor(
                        x_res[:, tqi * 1024 + oc * 128:
                              tqi * 1024 + oc * 128 + 128],
                        psU[:, 128:256], 0.0,
                        x_res[:, tqi * 1024 + oc * 128:
                              tqi * 1024 + oc * 128 + 128],
                        ALU.add, ALU.add), waits=[ttr])
                    rdone(("psU", 1), tadd)
                    lastadd = tadd
                    xadd_toks.append(tadd)
                rdone("pcb", lastadd)
            rdone("wbA" if par == 0 else "wbB", mlast)
        return xadd_toks

    def emit_ffn(l, xn_dep, xadd_toks, w2ready):
        tb2q, _ = emit_quant(b2cols[l][:, 0:8], 8, b2qcols[l][:, 0:8],
                             "fp16", skip_stage1=True, waits=[t_b2[l]])
        for blk in range(2):
            hq_toks = {}
            for hc in range(32):
                par = hc % 2
                wb = wbA if par == 0 else wbB
                tw = DMAg(lambda e, h, hc=hc, wb=wb, l=l: e.dma_start(
                    wb[:], wff[l]["w1"][hc]),
                    waits=ruse("wbA" if par == 0 else "wbB"),
                    sem="DA" if par == 0 else "DB")
                hts = []
                mm_l = None
                for half in range(2):
                    tok0 = blk * 1024 + half * 512
                    qd = (2 * hc + half) % 2
                    sl = PSAB[qd][:, 0:512]
                    pk = ("pab", qd)
                    mt = None
                    for ic in range(8):
                        mt = (T_ if ic == 7 else T0_)(
                            lambda e, h, ic=ic, sl=sl, wb=wb,
                                tok0=tok0: e.matmul(
                            sl, wb[:, ic * 128:(ic + 1) * 128],
                            xnT[:, ic * T + tok0:ic * T + tok0 + 512],
                            start=(ic == 0), stop=(ic == 7)),
                            waits=(([tw] + xn_dep + ruse(pk))
                                   if ic == 0 else []))
                    mm_l = mt
                    zds = []
                    for lane in range(2):
                        out_sl = hqv[:, hc * 1024 + half * 512 +
                                     lane * 256:
                                     hc * 1024 + half * 512 +
                                     lane * 256 + 256]
                        tqd, zd = emit_quant(
                            sl[:, lane * 256:lane * 256 + 256], 256,
                            out_sl, "e4m3",
                            bias_col=b1cols[l][:, hc:hc + 1],
                            relu=True, lane=lane,
                            waits=[mt, t_b1[l]] + ruse(("hqw", hc)))
                        zds.append(zd)
                        hts.append(tqd)
                    rdone(pk, zds)
                rdone("wbA" if par == 0 else "wbB", mm_l)
                hq_toks[hc] = hts
                rdone(("hqw", hc), hts)
            for tc2 in range(2):
                tok_in_blk = tc2 * 512
                for og in range(2):
                    mlast = [None] * 4
                    for hc in range(32):
                        par = hc % 2
                        tws = DMAw(lambda e, h, hc=hc, par=par, l=l:
                                   e.dma_start(
                            w2qs[:, par * 1024:par * 1024 + 1024],
                            w2q_sc[l][hc]),
                            waits=[w2ready] + ruse(("w2qsb", par),
                                                    "w2qs"),
                            sem="W2A" if par == 0 else "W2B")
                        for oc4 in range(4):
                            oc = og * 4 + oc4
                            mlast[oc4] = (T_ if (hc == 31 or oc4 == 3) else T0_)(
                                lambda e, h, oc4=oc4, oc=oc, par=par,
                                hc=hc, tok_in_blk=tok_in_blk: e.matmul(
                                    PS4[oc4][:, 0:512],
                                    w2qs[:, par * 1024 + oc * 128:
                                         par * 1024 + oc * 128 + 128],
                                    hqv[:, hc * 1024 + tok_in_blk:
                                        hc * 1024 + tok_in_blk + 512],
                                    start=(hc == 0), stop=(hc == 31)),
                                waits=([tws] + hq_toks[hc] +
                                       (ruse(("pab", 0), ("pab", 1),
                                             ("pcd", 0), ("pcd", 1))
                                        if hc == 0 else [])))
                        rdone(("w2qsb", par), mlast[3])
                        radd("w2qs", mlast[3])
                    for oc4 in range(4):
                        oc = og * 4 + oc4
                        sl = PS4[oc4][:, 0:512]
                        pk = [("pab", 0), ("pab", 1), ("pcd", 0),
                              ("pcd", 1)][oc4]
                        tqd, zd = emit_quant(
                            sl, 512, pcb[:, 0:512], "fp16",
                            bias_col=b2qcols[l][:, oc:oc + 1], relu=False,
                            waits=[mlast[oc4], tb2q] + ruse("pcb"))
                        rdone(pk, zd)
                        lastadd = None
                        for j in range(4):
                            ttr = T_(lambda e, h, j=j: e.transpose(
                                psU[:, 128:256],
                                pcb[:, j * 128:(j + 1) * 128], ident[:]),
                                waits=[tqd] + ruse(("psU", 1)))
                            tqi = blk * 8 + tc2 * 4 + j
                            tadd = V_(lambda e, h, tqi=tqi, oc=oc:
                                      e.scalar_tensor_tensor(
                                x_res[:, tqi * 1024 + oc * 128:
                                      tqi * 1024 + oc * 128 + 128],
                                psU[:, 128:256], 0.0,
                                x_res[:, tqi * 1024 + oc * 128:
                                      tqi * 1024 + oc * 128 + 128],
                                ALU.add, ALU.add), waits=[ttr])
                            rdone(("psU", 1), tadd)
                            lastadd = tadd
                            xadd_toks.append(tadd)
                        rdone("pcb", lastadd)
        return xadd_toks

    def pair_qk(ph, wdr, bcol, src, src_toks, out):
        tw = DMAg(lambda e, h, ph=ph, wdr=wdr: e.dma_start(
            wbA[:, 0:256], wdr[ph]), waits=ruse("wbA"), sem="DA")
        lt = None
        for tc in range(4):
            qd = tc % 2
            sl = PSAB[qd][:, 0:512]
            pk = ("pab", qd)
            mt = None
            for cc in range(2):
                mt = (T_ if cc == 1 else T0_)(
                    lambda e, h, cc=cc, tc=tc, sl=sl, src=src: e.matmul(
                    sl, wbA[:, cc * 128:(cc + 1) * 128],
                    src[:, cc * T + tc * 512:cc * T + tc * 512 + 512],
                    start=(cc == 0), stop=(cc == 1)),
                    waits=(([tw] + src_toks + ruse(pk))
                           if cc == 0 else []))
            te = S_(lambda e, h, sl=sl, tc=tc, out=out, bcol=bcol:
                    e.activation(out[:, tc * 512:(tc + 1) * 512], sl,
                                 AF.Gelu, bias=bcol), waits=[mt])
            rdone(pk, te)
            lt = te
        rdone("wbA", lt)
        return lt

    def rope_pair(wdr, bcol, out, xn_dep, trc, trs):
        tw = DMAg(lambda e, h, wdr=wdr: e.dma_start(wbA[:], wdr[0]),
                  waits=ruse("wbA"), sem="DA")
        lt = None
        for tc in range(4):
            qd = tc % 2
            sl = PSAB[qd][:, 0:512]
            pk = ("pab", qd)
            xw = ([xn_dep[8 * (4 * tc + 3) + 7]]
                  if len(xn_dep) == 128 else xn_dep)
            mt = None
            for ic in range(8):
                mt = (T_ if ic == 7 else T0_)(
                    lambda e, h, ic=ic, tc=tc, sl=sl: e.matmul(
                    sl, wbA[:, ic * 128:(ic + 1) * 128],
                    xnT[:, ic * T + tc * 512:ic * T + tc * 512 + 512],
                    start=(ic == 0), stop=(ic == 7)),
                    waits=(([tw] + xw + ruse(pk))
                           if ic == 0 else []))
            traw = S_(lambda e, h, sl=sl, tc=tc, bcol=bcol: e.activation(
                p1[:, tc * 512:(tc + 1) * 512], sl, AF.Identity,
                bias=bcol), waits=[mt] + ruse("p1r"))
            rdone(pk, traw)
            tsw = T_(lambda e, h, tc=tc: e.matmul(
                psS[:, 0:512], perm1[:],
                p1[:, tc * 512:(tc + 1) * 512], start=True, stop=True),
                waits=[traw, t_perm1] + ruse("psS"))
            tm1 = V_(lambda e, h, tc=tc: e.tensor_tensor(
                pcb[:, 0:512], p1[:, tc * 512:(tc + 1) * 512],
                ropeC[:, tc * 512:(tc + 1) * 512], ALU.mult),
                waits=[traw, trc] + ruse("pcb"))
            ts2 = V_(lambda e, h, tc=tc: e.tensor_tensor(
                pcb[:, 512:1024], psS[:, 0:512],
                ropeS[:, tc * 512:(tc + 1) * 512], ALU.mult),
                waits=[tsw, trs])
            rdone("psS", ts2)
            tsum = V_(lambda e, h: e.tensor_tensor(
                pcb[:, 0:512], pcb[:, 0:512], pcb[:, 512:1024], ALU.add),
                waits=[tm1, ts2])
            tg = S_(lambda e, h, tc=tc, out=out: e.activation(
                out[:, tc * 512:(tc + 1) * 512], pcb[:, 0:512], AF.Gelu),
                waits=[tsum])
            rdone("pcb", tg)
            radd("p1r", tg)
            lt = tg
        rdone("wbA", lt)
        return lt

    def emit_bounds(ph):
        first = V_(lambda e, h: e.memset(sml[0:1, 16:32], 0.0),
                   waits=ruse("bnd"))
        tmx_toks = []
        for side, buf in ((0, qh), (1, kh)):
            for tc in range(4):
                sq = p1[:, tc * 512:(tc + 1) * 512]
                tsq = S_(lambda e, h, buf=buf, tc=tc, sq=sq:
                         e.activation(
                    sq, buf[:, tc * 512:(tc + 1) * 512], AF.Square),
                    waits=ruse("bsq", "p1r") + [first])
                for h2 in range(2):
                    ps_ = psS if h2 == 0 else psR
                    pk_ = "psS" if h2 == 0 else "psR"
                    tred = T_(lambda e, h, sq=sq, h2=h2, ps_=ps_:
                              e.matmul(
                        ps_[:, 0:512],
                        onesq[64 * h2:64 * h2 + 64, 0:128],
                        sq[64 * h2:64 * h2 + 64, :],
                        start=True, stop=True),
                        waits=[tsq, t_blkc] + ruse(pk_))
                    cix = 16 + side * 8 + h2 * 4 + tc
                    tmx = V_(lambda e, h, cix=cix, ps_=ps_: e.reduce_max(
                        sml[0:1, cix:cix + 1],
                        ps_[0:1, 0:512], AX.X), waits=[tred])
                    rdone(pk_, tmx)
                    tmx_toks.append(tmx)
                rdone("bsq", tmx_toks[-2:])
        tms = []
        for side in range(2):
            for h2 in range(2):
                c0 = 16 + side * 8 + h2 * 4
                tms.append(V_(lambda e, h, c0=c0, side=side, h2=h2:
                              e.reduce_max(
                    sml2[0:1, 16 + side * 2 + h2:17 + side * 2 + h2],
                    sml[0:1, c0:c0 + 4], AX.X), waits=tmx_toks))
        tp = V_(lambda e, h: e.tensor_tensor(
            sml2[0:1, 20:22], sml2[0:1, 16:18], sml2[0:1, 18:20],
            ALU.mult), waits=tms)
        tp = S_(lambda e, h: e.activation(sml2[0:1, 22:24],
                                          sml2[0:1, 20:22], AF.Ln,
                                          bias=cstcol(2.0 ** -60)[0:1, :]),
                waits=[tp])
        tp = S_(lambda e, h: e.activation(
            sml2[0:1, 24:26], sml2[0:1, 22:24], AF.Exp, scale=0.5,
            bias=cstcol(-math.log(8.0))[0:1, :]), waits=[tp])
        tp = V_(lambda e, h: e.tensor_scalar(
            smlb[0:1, 4:6], sml2[0:1, 24:26], 40.0, -1.0, ALU.min,
            ALU.mult), waits=[tp])
        t2 = emit_bcast2([smlb[0:1, 4:6]],
                         [bcols[:, 2 * ph:2 * ph + 1],
                          bcols[:, 2 * ph + 1:2 * ph + 2]], [tp])
        rdone("bnd", [t2])
        return t2

    def l1_attention(xn_dep, aT_toks):
        DMAg(lambda e, h: e.dma_start(ropeC[:], ropec1[0]),
             waits=ruse("ropes"), sem="Dr")
        trs = DMAg(lambda e, h: e.dma_start(ropeS[:], ropec1[1]),
                   sem="Dr")
        trc = trs
        cq_toks = []
        ckv_toks = []
        for (wdr, bc_sl, out, toks) in ((wq1, bias1[:, 0:2], cqb, cq_toks),
                                        (wkv1, bias2[:, 0:2], ckvb,
                                         ckv_toks)):
            for oc in range(2):
                tw = DMAg(lambda e, h, oc=oc, wdr=wdr: e.dma_start(
                    wbA[:], wdr[oc]), waits=ruse("wbA"), sem="DA")
                lt = None
                for tc in range(4):
                    qd = tc % 2
                    sl = PSAB[qd][:, 0:512]
                    pk = ("pab", qd)
                    xw = ([xn_dep[8 * (4 * tc + 3) + 7]]
                          if len(xn_dep) == 128 else xn_dep)
                    mt = None
                    for ic in range(8):
                        mt = (T_ if ic == 7 else T0_)(
                            lambda e, h, ic=ic, tc=tc, sl=sl: e.matmul(
                            sl, wbA[:, ic * 128:(ic + 1) * 128],
                            xnT[:, ic * T + tc * 512:
                                ic * T + tc * 512 + 512],
                            start=(ic == 0), stop=(ic == 7)),
                            waits=(([tw] + xw + t_bias +
                                    ruse(pk))
                                   if ic == 0 else []))
                    te = S_(lambda e, h, sl=sl, oc=oc, tc=tc, out=out,
                            bc_sl=bc_sl: e.activation(
                        out[:, oc * T + tc * 512:oc * T + tc * 512 + 512],
                        sl, AF.Gelu, bias=bc_sl[:, oc:oc + 1]),
                        waits=[mt])
                    rdone(pk, te)
                    toks.append(te)
                    lt = te
                rdone("wbA", lt)
        for ph in range(8):
            if ph < 7:
                tq_ = pair_qk(ph, wqc1, bias1[:, 2 + ph:3 + ph], cqb,
                              cq_toks, qh)
                tk_ = pair_qk(ph, wkc1, bias2[:, 2 + ph:3 + ph], ckvb,
                              ckv_toks, kh)
            else:
                tq_ = rope_pair(wqr1, bias1[:, 9:10], qh, xn_dep, trc, trs)
                tk_ = rope_pair(wkr1, bias2[:, 9:10], kh, xn_dep, trc, trs)
            twv = DMAg(lambda e, h, ph=ph: e.dma_start(wbC[:, 0:256],
                                                       wv1[ph]),
                       waits=ruse("wbC"), sem="DC")
            vt = []
            for qt in range(16):
                sl = PSCD[qt % 2][:, 0:128]
                pk = ("pcd", qt % 2)
                mt = None
                for cc in range(2):
                    mt = (T_ if cc == 1 else T0_)(
                        lambda e, h, cc=cc, qt=qt, sl=sl: e.matmul(
                        sl, ckvb[:, cc * T + qt * 128:
                                 cc * T + qt * 128 + 128],
                        wbC[:, cc * 128:(cc + 1) * 128],
                        start=(cc == 0), stop=(cc == 1)),
                        waits=(([twv] + ckv_toks + ruse(pk))
                               if cc == 0 else []))
                tv = V_(lambda e, h, qt=qt, sl=sl: e.tensor_copy(
                    vtok[:, qt * VS:qt * VS + 64], sl[:, 0:64]),
                    waits=[mt])
                tv = V_(lambda e, h, qt=qt, sl=sl: e.tensor_copy(
                    vtok[:, qt * VS + 65:qt * VS + 129], sl[:, 64:128]),
                    waits=[tv])
                rdone(pk, tv)
                vt.append(tv)
            rdone("wbC", vt[-1])
            bnd_tok = emit_bounds(ph)

            def stageA1(qt, h2):
                par = (2 * qt + h2) % 2
                po = p0[:, par * T:par * T + T]
                hh = 2 * ph + h2
                side_toks = []
                ng = (qt + 4) // 4
                for g in range(ng):
                    jn = min(4, qt + 1 - 4 * g)
                    sl = PSAB[g % 2][:, 0:512]
                    pk = ("pab", g % 2)
                    mlast = None
                    for j in range(jn):
                        kb = 4 * g + j
                        mlast = (T_ if j == jn - 1 else T0_)(
                            lambda e, h, j=j, kb=kb, h2=h2, qt=qt,
                            sl=sl: e.matmul(
                            sl[:, j * 128:(j + 1) * 128],
                            kh[64 * h2:64 * h2 + 64,
                               kb * 128:(kb + 1) * 128],
                            qh[64 * h2:64 * h2 + 64,
                               qt * 128:(qt + 1) * 128],
                            start=True, stop=True),
                            waits=(([tq_, tk_] + ruse(pk))
                                   if j == 0 else []))
                    wl = [mlast, t_mask]
                    if g == ng - 1:
                        jd = qt - 4 * g
                        tmask = V_(lambda e, h, jd=jd, sl=sl:
                                   e.tensor_tensor(
                            sl[:, jd * 128:(jd + 1) * 128],
                            sl[:, jd * 128:(jd + 1) * 128],
                            maskb[:], ALU.add), waits=wl)
                        wl = [tmask]
                    texp = S_(lambda e, h, jn=jn, g=g, hh=hh, sl=sl,
                              po=po: e.activation(
                        po[:, g * 512:g * 512 + jn * 128],
                        sl[:, 0:jn * 128], AF.Exp, scale=0.125,
                        bias=bcols[:, hh:hh + 1]),
                        waits=wl + [bnd_tok] + ruse(("pex", par)))
                    rdone(pk, texp)
                    side_toks.append(texp)
                return side_toks

            def stageB1(qt, h2, chunks):
                par = (2 * qt + h2) % 2
                po = p0[:, par * T:par * T + T]
                pvps = PSCD[qt % 2]
                pvk = ("pcd", qt % 2)
                base = 256 * h2
                pvt = None
                for kb in range(qt + 1):
                    pvt = (T_ if kb == qt else T0_)(
                        lambda e, h, kb=kb, qt=qt, h2=h2, base=base,
                        po=po: e.matmul(
                        pvps[:, base:base + 65],
                        po[:, kb * 128:(kb + 1) * 128],
                        vtok[:, kb * VS + 64 * h2:kb * VS + 64 * h2 + 65],
                        start=(kb == 0), stop=(kb == qt)),
                        waits=[chunks[kb // 4], vt[kb]] +
                        (ruse(pvk) if kb == 0 and h2 == 0 else []))
                rdone(("pex", par), pvt)
                return pvt

            def tail1(qt, pv0, pv1):
                pvps = PSCD[qt % 2]
                pvk = ("pcd", qt % 2)
                c8 = 8 + 2 * (qt % 2)
                rz0 = V_(lambda e, h, c8=c8, pvps=pvps: e.reciprocal(
                    sml2[:, c8:c8 + 1], pvps[:, 64:65]),
                    waits=[pv0, pv1])
                rz1 = V_(lambda e, h, c8=c8, pvps=pvps: e.reciprocal(
                    sml2[:, c8 + 1:c8 + 2], pvps[:, 256:257]),
                    waits=[pv1])
                tno0 = V_(lambda e, h, c8=c8, pvps=pvps: e.tensor_scalar(
                    pcb[:, 0:64], pvps[:, 0:64],
                    sml2[:, c8:c8 + 1], None, ALU.mult),
                    waits=[rz0] + ruse("pcb"))
                tno1 = V_(lambda e, h, c8=c8, pvps=pvps: e.tensor_scalar(
                    pcb[:, 64:128], pvps[:, 257:321],
                    sml2[:, c8 + 1:c8 + 2], None, ALU.mult),
                    waits=[rz1])
                rdone(pvk, tno1)
                ttr = T_(lambda e, h: e.transpose(
                    psU[:, 0:128], pcb[:, 0:128], ident[:]),
                    waits=[tno0, tno1] + ruse(("psU", 0)))
                rdone("pcb", ttr)
                tev = V_(lambda e, h, ph=ph, qt=qt: e.tensor_scalar(
                    aT[:, ph * T + qt * 128:ph * T + qt * 128 + 128],
                    psU[:, 0:128], bvcol[:, ph:ph + 1], None,
                    ALU.add), waits=[ttr])
                rdone(("psU", 0), tev)
                aT_toks.append(tev)

            units = [(qt, h2) for qt in range(16) for h2 in range(2)]
            pend = {}
            prevu = None
            for u in units:
                chunks = stageA1(*u)
                pend[u] = [chunks, None]
                if prevu is not None:
                    pend[prevu][1] = stageB1(prevu[0], prevu[1],
                                             pend[prevu][0])
                    if prevu[1] == 1:
                        q0 = prevu[0]
                        tail1(q0, pend[(q0, 0)][1], pend[(q0, 1)][1])
                        del pend[(q0, 0)], pend[(q0, 1)]
                prevu = u
            pend[prevu][1] = stageB1(prevu[0], prevu[1], pend[prevu][0])
            tail1(15, pend[(15, 0)][1], pend[(15, 1)][1])
        return aT_toks

    lvl = SCHED_LEVEL
    final_dep = []

    def gen_w2():
        for g in (w2q_gen(0), w2q_gen(1)):
            for _ in g:
                yield

    def make_pump(gen):
        state = [gen is None]

        def pump(n=1):
            if state[0]:
                return
            try:
                for _ in range(n):
                    next(gen)
            except StopIteration:
                state[0] = True

        def drain():
            if state[0]:
                return
            for _ in gen:
                pass
            state[0] = True
        return pump, drain

    gen0 = gen_w2() if lvl >= 2 else None
    pump0, drain0 = make_pump(gen0)
    pump0(6)
    ln0 = emit_layernorm(t_x + [t_ident])
    xn_dep0 = ln0 if len(ln0) == 128 else [ln0[-1]]
    final_dep = [ln0[-1]]
    if lvl >= 1:
        nlam_tok = emit_lambda() if LAMBDA_ON else None
        l0_toks = []
        for hd in range(N_HEADS_L0):
            l0_head(hd, nlam_tok, xn_dep0, l0_toks, pump=pump0)
        drain0()
        final_dep = l0_toks[-1:] or final_dep
        if PROJ_ON:
            xadd0 = []
            ptoks = ([l0_toks[-16:][4 * tc + 3] for tc in range(4)]
                     if len(l0_toks) >= 16 else l0_toks[-1:])
            emit_out_proj(wproj0, aT, ptoks, None, 8, xadd0)
            final_dep = [xadd0[-1]]
    if lvl >= 2:
        emit_ffn(0, [xn_dep0[-1]], xadd0, W2R[0])
        final_dep = [xadd0[-1]]
    if lvl >= 3:
        ln1 = emit_layernorm([xadd0[-1]])
        xn_dep1 = ln1 if len(ln1) == 128 else [ln1[-1]]
        aT_toks1 = []
        l1_attention(xn_dep1, aT_toks1)
        xadd1 = []
        p1toks = ([aT_toks1[-16:][4 * tc + 3] for tc in range(4)]
                  if len(aT_toks1) >= 16 else [aT_toks1[-1]])
        emit_out_proj(wo1, aT, p1toks, bocol, 8, xadd1)
        final_dep = [xadd1[-1]]
    if lvl >= 4:
        emit_ffn(1, [xn_dep1[-1]], xadd1, W2R[1])
        final_dep = [xadd1[-1]]

    for i in range(16):
        DMAo(lambda e, h, i=i: e.dma_start(
            y3[i], x_res[:, i * 1024:(i + 1) * 1024]), waits=final_dep)
    P.op("sync", lambda e, h: None,
         waits=[(s, P.sem_counts[s]) for s in
                ("Do", "Dc", "Dg", "DA", "DB", "DC", "W2A", "W2B",
                 "DZ0", "DZ1", "DS0", "DS1", "Dr")
                if P.sem_counts.get(s, 0) > 0])

    with nc.Block() as block:
        @block.tensor
        def _(eng):
            for f in P.th["tensor"]:
                f(eng, H)

        @block.vector
        def _(eng):
            for f in P.th["vector"]:
                f(eng, H)

        @block.scalar
        def _(eng):
            for f in P.th["scalar"]:
                f(eng, H)

        @block.gpsimd
        def _(eng):
            for f in P.th["gpsimd"]:
                f(eng, H)

        @block.sync
        def _(eng):
            for f in P.th["sync"]:
                f(eng, H)
    es.close()
    return nc


def prep_inputs(inputs):
    g = {k: np.asarray(v) for k, v in inputs.items()}
    d = {}
    d["wq0"] = pack_lhsT(np.ascontiguousarray(g["L0_attn_Wq"].T))
    d["wk0"] = pack_lhsT(np.ascontiguousarray(g["L0_attn_Wk"].T))
    d["wv0"] = pack_lhsT(np.ascontiguousarray(g["L0_attn_Wv"].T))
    d["wproj0"] = pack_lhsT(np.ascontiguousarray(g["L0_attn_Wproj"].T))
    for l in range(2):
        d[f"w1_{l}"] = pack_lhsT(np.ascontiguousarray(g[f"L{l}_ffn_w1"].T))
        d[f"w2_{l}"] = np.ascontiguousarray(
            g[f"L{l}_ffn_w2"].T.astype(np.float32)).reshape(32, 128, 1024)
        d[f"b1c_{l}"] = pack_bias_cols(g[f"L{l}_ffn_b1"])
        d[f"b2c_{l}"] = pack_bias_cols(g[f"L{l}_ffn_b2"])
    d["wq1"] = pack_lhsT(g["L1_attn_wq"])
    d["bq1"] = pack_bias_cols(g["L1_attn_bq"])
    d["wqc1"] = pack_lhsT(g["L1_attn_wqc"])
    d["bqc1"] = pack_bias_cols(g["L1_attn_bqc"])
    d["wqr1"] = pack_lhsT(g["L1_attn_wqr"])
    d["bqr1"] = pack_bias_cols(g["L1_attn_bqr"])
    d["wkv1"] = pack_lhsT(g["L1_attn_wkv"])
    d["bkv1"] = pack_bias_cols(g["L1_attn_bkv"])
    d["wkr1"] = pack_lhsT(g["L1_attn_wkr"])
    d["bkr1"] = pack_bias_cols(g["L1_attn_bkr"])
    d["wkc1"] = pack_lhsT(g["L1_attn_wkc"])
    d["bkc1"] = pack_bias_cols(g["L1_attn_bkc"])
    d["wv1"] = pack_lhsT(g["L1_attn_wv"])
    d["bv1"] = pack_bias_cols(g["L1_attn_bv"])
    d["wo1"] = pack_lhsT(g["L1_attn_wo"])
    d["bo1"] = pack_bias_cols(g["L1_attn_bo"])
    d["lamv"] = np.concatenate(
        [g["L0_attn_lq1"], g["L0_attn_lq2"], g["L0_attn_lk1"],
         g["L0_attn_lk2"]]).astype(np.float32).reshape(1, 256)
    d["identc"] = np.eye(128, dtype=np.float32)
    d["permc0"] = perm_mat(32)
    d["permc1"] = perm_mat(64)
    d["maskc"] = causal_maskbias()
    d["ropec0"] = np.stack(rope_tables_l0())
    d["ropec1"] = np.stack(rope_tables_l1())
    return d


def kernel(**inputs):
    nc = build_program()
    shared = prep_inputs(inputs)
    x = np.asarray(inputs["L0_attn_x"])
    in_maps = []
    for b in range(NB):
        m = dict(shared)
        m["x"] = np.ascontiguousarray(x[b])
        in_maps.append(m)
    res = run_bass_kernel_spmd(nc, in_maps, core_ids=list(range(NB)))
    return np.stack([np.asarray(r["y"])
                     for r in res.results]).astype(np.float32)


# version 18
# speedup vs baseline: 1.0892x; 1.0892x over previous
import math
from contextlib import ExitStack

import numpy as np

import concourse.bass as bass
import concourse.mybir as mybir
from concourse.bass_utils import run_bass_kernel_spmd

F32 = mybir.dt.float32
F32R = mybir.dt.float32r
BF16 = mybir.dt.bfloat16
F8 = mybir.dt.float8e4
I32 = mybir.dt.int32
AF = mybir.ActivationFunctionType
ALU = mybir.AluOpType
AX = mybir.AxisListType

T = 2048
NB = 8
LN2 = math.log(2.0)
INV_LN2 = 1.0 / LN2
LAMBDA_INIT = 0.8 - 0.6 * math.exp(-0.3 * 4)
RMS_EPS = float(np.finfo(np.float32).eps)
LN_EPS = 1e-5
SCHED_LEVEL = 4
N_HEADS_L0 = 8
ATTN_STAGE = 4
PROJ_ON = 1
LAMBDA_ON = 1
PREP_STAGE = 3



def pack_lhsT(W):
    I, O = W.shape
    return np.ascontiguousarray(
        W.reshape(I // 128, 128, O // 128, 128)
        .transpose(2, 1, 0, 3)
        .reshape(O // 128, 128, I).astype(np.float32))


def pack_bias_cols(b):
    n = b.shape[0] // 128
    return np.ascontiguousarray(np.asarray(b, np.float32).reshape(n, 128).T)


def rope_tables_l0():
    HD = 64
    inv_freq = 1.0 / 10000.0 ** (np.arange(0, HD, 2, dtype=np.float32) / HD)
    fr = np.outer(np.arange(T, dtype=np.float32), inv_freq)
    cos, sin = np.cos(fr).T, np.sin(fr).T
    Cf = np.empty((128, T), np.float32)
    Sf = np.empty((128, T), np.float32)
    for blk in range(4):
        Cf[blk * 32:(blk + 1) * 32] = cos
        Sf[blk * 32:(blk + 1) * 32] = sin if blk % 2 == 0 else -sin
    return Cf, Sf


def rope_tables_l1():
    DR = 128
    inv_freq = 1.0 / 10000.0 ** (np.arange(0, DR, 2, dtype=np.float32) / DR)
    fr = np.outer(np.arange(T, dtype=np.float32), inv_freq)
    cos, sin = np.cos(fr).T, np.sin(fr).T
    Cf = np.empty((128, T), np.float32)
    Sf = np.empty((128, T), np.float32)
    Cf[0:64] = cos
    Cf[64:128] = cos
    Sf[0:64] = -sin
    Sf[64:128] = sin
    return Cf, Sf


def perm_mat(n):
    P = np.zeros((128, 128), np.float32)
    for a in range(128):
        P[(a // n ^ 1) * n + (a % n), a] = 1.0
    return P


def causal_maskbias():
    m = np.zeros((128, 128), np.float32)
    m[np.tril_indices(128, -1)] = -60.0
    return m


class Prog:
    ENGINES = ("tensor", "vector", "scalar", "gpsimd", "sync")

    def __init__(self):
        self.th = {e: [] for e in self.ENGINES}
        self.sem_counts = {}
        self.hwm = {e: {} for e in self.ENGINES}
        self.default_waits = []

    def op(self, eng, fn, waits=(), inc=None, inc_step=1):
        tok = None
        if inc is not None:
            v = self.sem_counts.get(inc, 0) + inc_step
            self.sem_counts[inc] = v
            tok = (inc, v)
        rw = []
        for w in list(self.default_waits) + list(waits):
            if w is None:
                continue
            s, v = w
            if self.hwm[eng].get(s, 0) < v:
                self.hwm[eng][s] = v
                rw.append((s, v))
        step = inc_step

        def thunk(engine, H):
            for (s, v) in rw:
                engine.wait_ge(H[s], v)
            ins = fn(engine, H)
            if inc is not None:
                if ins is None:
                    engine.sem_inc(H[inc], step)
                else:
                    ins.then_inc(H[inc], step)
        self.th[eng].append(thunk)
        return tok


def build_program():
    nc = bass.Bass("TRN2", target_bir_lowering=False)
    P = Prog()

    def din(name, shape, dtype=F32):
        return nc.dram_tensor(name, list(shape), dtype,
                              kind="ExternalInput").ap()

    x_in = din("x", [T, 1024])
    y_out = nc.dram_tensor("y", [T, 1024], F32, kind="ExternalOutput").ap()

    wq0 = din("wq0", [8, 128, 1024])
    wk0 = din("wk0", [8, 128, 1024])
    wv0 = din("wv0", [8, 128, 1024])
    wproj0 = din("wproj0", [8, 128, 1024])
    wff = []
    for l in range(2):
        wff.append(dict(
            w1=din(f"w1_{l}", [32, 128, 1024]),
            w2=din(f"w2_{l}", [32, 128, 1024]),
            b1=din(f"b1c_{l}", [128, 32]),
            b2=din(f"b2c_{l}", [128, 8])))
    wq1 = din("wq1", [2, 128, 1024])
    bq1 = din("bq1", [128, 2])
    wqc1 = din("wqc1", [7, 128, 256])
    bqc1 = din("bqc1", [128, 7])
    wqr1 = din("wqr1", [1, 128, 1024])
    bqr1 = din("bqr1", [128, 1])
    wkv1 = din("wkv1", [2, 128, 1024])
    bkv1 = din("bkv1", [128, 2])
    wkr1 = din("wkr1", [1, 128, 1024])
    bkr1 = din("bkr1", [128, 1])
    wkc1 = din("wkc1", [7, 128, 256])
    bkc1 = din("bkc1", [128, 7])
    wv1 = din("wv1", [8, 128, 256])
    bv1 = din("bv1", [128, 8])
    wo1 = din("wo1", [8, 128, 1024])
    bo1 = din("bo1", [128, 8])
    lam_in = din("lamv", [1, 256])
    identc = din("identc", [128, 128])
    permc0 = din("permc0", [128, 128])
    permc1 = din("permc1", [128, 128])
    maskc = din("maskc", [128, 128])
    ropec0 = din("ropec0", [2, 128, T])
    ropec1 = din("ropec1", [2, 128, T])
    w2q_sc = [nc.dram_tensor(f"w2qsc_{l}", [32, 128, 1024], F8,
                             kind="Internal").ap() for l in range(2)]

    es = ExitStack()

    def sbt(name, shape, dtype):
        return es.enter_context(nc.sbuf_tensor(name, shape, dtype))

    x_res = sbt("x_res", [128, 16 * 1024], F32)
    xnT = sbt("xnT", [128, 8 * T], BF16)
    arA = sbt("arA", [128, 8 * T], BF16)
    aT = arA[:, 0:8 * T]
    hqv = arA.bitcast(F8)
    arB = sbt("arB", [128, 6 * T], BF16)
    cqb = arB[:, 0:2 * T]
    ckvb = arB[:, 2 * T:4 * T]
    qh = sbt("qh", [128, T], BF16)
    kh = sbt("kh", [128, T], BF16)
    VS = 136
    vtok = sbt("vtok", [128, 16 * VS], BF16)
    p0 = sbt("p0", [128, 2 * T], BF16)
    p1 = sbt("p1", [128, 2 * T], BF16)
    pcb = sbt("pcb", [128, 1024], BF16)
    NSCR = 8
    scr = arB.bitcast(F32)
    scri = arB.bitcast(I32)[:, 4096:5120]
    scrb = arB[:, 10240:11264]
    rmsr = sbt("rmsr", [128, 512], F32)
    rmsb = sbt("rmsb", [128, 512], F32)
    wbA = sbt("wbA", [128, 1024], BF16)
    wbB = sbt("wbB", [128, 1024], BF16)
    wbC = sbt("wbC", [128, 1024], BF16)
    w2qs = sbt("w2qs", [128, 2 * 1024], F8)
    ident = sbt("ident", [128, 128], BF16)
    perm0 = sbt("perm0", [128, 128], BF16)
    perm1 = sbt("perm1", [128, 128], BF16)
    maskb = sbt("maskb", [128, 128], BF16)
    ropeC = sbt("ropeC", [128, T], BF16)
    ropeS = sbt("ropeS", [128, T], BF16)
    onesq = sbt("onesq", [128, 128], BF16)
    b1cols = [sbt(f"b1col{l}", [128, 32], F32) for l in range(2)]
    b2cols = [sbt(f"b2col{l}", [128, 8], F32) for l in range(2)]
    b2qcols = [sbt(f"b2qcol{l}", [128, 8], F32) for l in range(2)]
    bias1 = sbt("bias1", [128, 12], F32)
    bias2 = sbt("bias2", [128, 12], F32)
    bvcol = sbt("bvcol", [128, 8], F32)
    bocol = sbt("bocol", [128, 8], F32)
    lamt = sbt("lamt", [1, 256], F32)
    smlb = sbt("smlb", [1, 8], BF16)
    sml = sbt("sml", [128, 128], F32)
    sml2 = sbt("sml2", [128, 32], F32)
    cst = sbt("cst", [128, 16], F32)
    stats = sbt("stats", [128, 12], F32)
    aggr = sbt("aggr", [128, 2 * 16], F32)
    rstd = sbt("rstd", [128, 16], F32)
    bcols = sbt("bcols", [128, 16], F32)

    psA = es.enter_context(nc.psum_tensor("psA", [128, 512], F32))
    psB = es.enter_context(nc.psum_tensor("psB", [128, 512], F32))
    psC = es.enter_context(nc.psum_tensor("psC", [128, 512], F32))
    psD = es.enter_context(nc.psum_tensor("psD", [128, 512], F32))
    psT = es.enter_context(nc.psum_tensor("psT", [128, 1024], BF16))
    psU = es.enter_context(nc.psum_tensor("psU", [128, 1024], BF16))
    psS = es.enter_context(nc.psum_tensor("psS", [128, 512], F32))
    psR = es.enter_context(nc.psum_tensor("psR", [128, 512], F32))
    PSAB = [psA, psB]
    PSCD = [psC, psD]
    PS4 = [psA, psB, psC, psD]

    sem_names = ["T", "V", "S", "G", "Dc", "Dg", "DA", "DB", "DC",
                 "W2A", "W2B", "DZ0", "DZ1", "DS0", "DS1",
                 "Dr", "Do"]
    H = {n: es.enter_context(nc.semaphore(n)) for n in sem_names}

    def T_(fn, waits=()):
        return P.op("tensor", fn, waits, inc="T")

    def T0_(fn, waits=()):
        return P.op("tensor", fn, waits, inc=None)

    def V_(fn, waits=()):
        return P.op("vector", fn, waits, inc="V")

    def S_(fn, waits=()):
        return P.op("scalar", fn, waits, inc="S")

    def G_(fn, waits=()):
        return P.op("gpsimd", fn, waits, inc="G")

    def DMA(eng, sem, fn, waits=()):
        return P.op(eng, fn, waits, inc=sem, inc_step=16)

    def DMAw(fn, waits=(), sem="Dc"):
        return P.op("sync", fn, waits, inc=sem, inc_step=16)

    def DMAg(fn, waits=(), sem="Dg"):
        return P.op("gpsimd", fn, waits, inc=sem, inc_step=16)

    def DMAo(fn, waits=()):
        return P.op("sync", fn, waits, inc="Do", inc_step=16)

    last_use = {}

    def ruse(*keys):
        out = []
        for k in keys:
            out.extend(last_use.get(k, []))
        return out

    def rdone(key, toks):
        if not isinstance(toks, list):
            toks = [toks]
        last_use[key] = [t for t in toks if t is not None]

    def radd(key, tok):
        if tok is not None:
            last_use.setdefault(key, []).append(tok)

    def emit_bcast2(srcs, dsts, waits):
        tm = None
        for i, s2 in enumerate(srcs):
            tm = T_(lambda e, h, s2=s2, i=i: e.matmul(
                psS[:, 0:2], onesq[0:1, 0:128], s2,
                start=(i == 0), stop=(i == len(srcs) - 1)),
                waits=list(waits) + [t_blkr] + (ruse("psS")
                                               if i == 0 else []))
        out = []
        for j, d in enumerate(dsts):
            if d is None:
                continue
            out.append(V_(lambda e, h, j=j, d=d: e.tensor_copy(
                d, psS[:, j:j + 1]), waits=[tm]))
        rdone("psS", out)
        return out[-1]

    CSTV = {}
    CSTV_FROZEN = [False]

    def cstcol(val):
        val = float(val)
        if val not in CSTV:
            assert not CSTV_FROZEN[0], f"const col {val} not preregistered"
            i = len(CSTV)
            assert i < 16, "too many const cols"
            CSTV[val] = i
            V_(lambda e, h, i=i, val=val: e.memset(cst[:, i:i + 1], val))
        return cst[:, CSTV[val]:CSTV[val] + 1]

    t_ident = DMAg(lambda e, h: e.dma_start(ident[:], identc))
    t_perm0 = DMAg(lambda e, h: e.dma_start(perm0[:], permc0))
    t_perm1 = DMAg(lambda e, h: e.dma_start(perm1[:], permc1))
    t_ropeC = DMAg(lambda e, h: e.dma_start(ropeC[:], ropec0[0]))
    t_ropeS = DMAg(lambda e, h: e.dma_start(ropeS[:], ropec0[1]))
    t_mask = DMAg(lambda e, h: e.dma_start(maskb[:], maskc))
    t_blkr = V_(lambda e, h: e.memset(onesq[:], 1.0))
    t_blkc = t_blkr
    V_(lambda e, h: e.memset(smlb[:], 0.0))
    t_lam = DMAw(lambda e, h: e.dma_start(lamt[:], lam_in))
    t_b1 = [DMAw(lambda e, h, l=l: e.dma_start(b1cols[l][:], wff[l]["b1"]))
            for l in range(2)]
    t_b2 = [DMAw(lambda e, h, l=l: e.dma_start(b2cols[l][:], wff[l]["b2"]))
            for l in range(2)]
    x3 = x_in.rearrange("(n p) c -> n p c", p=128)
    y3 = y_out.rearrange("(n p) c -> n p c", p=128)
    t_x = [DMAo(lambda e, h, i=i: e.dma_start(
        x_res[:, i * 1024:(i + 1) * 1024], x3[i])) for i in range(16)]
    t_bias_l = [
        DMAw(lambda e, h: e.dma_start(bias1[:, 0:2], bq1)),
        DMAw(lambda e, h: e.dma_start(bias1[:, 2:9], bqc1)),
        DMAw(lambda e, h: e.dma_start(bias1[:, 9:10], bqr1)),
        DMAw(lambda e, h: e.dma_start(bias2[:, 0:2], bkv1)),
        DMAw(lambda e, h: e.dma_start(bias2[:, 2:9], bkc1)),
        DMAw(lambda e, h: e.dma_start(bias2[:, 9:10], bkr1)),
        DMAw(lambda e, h: e.dma_start(bvcol[:], bv1)),
        DMAw(lambda e, h: e.dma_start(bocol[:], bo1)),
    ]
    for kb in range(16):
        V_(lambda e, h, kb=kb: e.memset(vtok[:, kb * VS + 64:kb * VS + 65],
                                        1.0))
    for _v in (0.0, 2.0, 4096.0, LN_EPS, RMS_EPS, 2.0 ** -60, -8.0,
               -math.log(8.0), math.log(1.0 - LAMBDA_INIT),
               3 * LN2, -3 * LN2, 4 * LN2, -4 * LN2):
        cstcol(_v)
    CSTV_FROZEN[0] = True
    tok_c = P.op("vector", lambda e, h: None,
                 waits=[("Dc", P.sem_counts["Dc"]),
                        ("Dg", P.sem_counts.get("Dg", 0))], inc="V")
    P.default_waits = [tok_c]
    t_ident = t_perm0 = t_perm1 = t_ropeC = t_ropeS = tok_c
    t_mask = t_blkc = t_blkr = t_lam = t_ones = tok_c
    t_b1 = [tok_c, tok_c]
    t_b2 = [tok_c, tok_c]
    t_x = [t_x[-1]]
    t_bias = [tok_c]

    def emit_layernorm(dep):
        tA = []
        for i in range(16):
            xt = x_res[:, i * 1024:(i + 1) * 1024]
            tsa = V_(lambda e, h, xt=xt: e.bn_stats(stats[:, 0:6],
                                                    xt[:, 0:512]),
                     waits=list(dep) + ruse("aggr", "stats"))
            tsb = V_(lambda e, h, xt=xt: e.bn_stats(stats[:, 6:12],
                                                    xt[:, 512:1024]),
                     waits=[tsa])
            ta_ = V_(lambda e, h, i=i: e.bn_aggr(
                aggr[:, 2 * i:2 * i + 2], stats[:, 0:12]), waits=[tsb])
            rdone("stats", ta_)
            tA.append(ta_)
        var_view = aggr[:, 1:32:2]
        tl = S_(lambda e, h: e.activation(rstd[:], var_view, AF.Ln,
                                          bias=cstcol(LN_EPS)),
                waits=[tA[-1]])
        tl = S_(lambda e, h: e.activation(rstd[:], rstd[:], AF.Exp,
                                          scale=-0.5), waits=[tl])
        rdone("aggr", tl)
        toks = []
        for i in range(16):
            xt = x_res[:, i * 1024:(i + 1) * 1024]
            tv = V_(lambda e, h, xt=xt, i=i: e.tensor_scalar(
                pcb[:, 0:1024], xt, aggr[:, 2 * i:2 * i + 1],
                rstd[:, i:i + 1], ALU.subtract, ALU.mult),
                waits=[tl] + ruse("pcb"))
            last = None
            for c in range(8):
                pd, key = (psT, "psTg") if c % 2 == 0 \
                    else (psU, ("psU", 0))
                tt = T_(lambda e, h, c=c, pd=pd: e.transpose(
                    pd[:, 0:128], pcb[:, c * 128:(c + 1) * 128], ident[:]),
                    waits=[tv, t_ident] + ruse(key))
                tc = V_(lambda e, h, c=c, i=i, pd=pd: e.tensor_copy(
                    xnT[:, c * T + i * 128: c * T + i * 128 + 128],
                    pd[:, 0:128]), waits=[tt])
                rdone(key, tc)
                last = tc
                toks.append(tc)
            rdone("pcb", last)
        return toks

    qpar = [0]
    chain_done = {0: None, 1: None}

    def emit_quant(z_ap, w, out_ap, kind, bias_col=None, relu=False,
                   skip_stage1=False, waits=()):
        man, mine = (3, -6) if kind == "e4m3" else (10, -14)
        p = qpar[0]
        qpar[0] ^= 1

        def fs(j):
            return scr[:, (4 * p + j) * 512:(4 * p + j) * 512 + w]
        iA = scri[:, p * 512:p * 512 + w]
        SCB = scrb[:, p * 512:p * 512 + w]
        wl = list(waits)
        if chain_done[p] is not None:
            wl = wl + [chain_done[p]]
        if skip_stage1:
            A = z_ap
            z_done = None
            pre = wl
        else:
            bcol = bias_col if bias_col is not None else cstcol(0.0)
            t1 = S_(lambda e, h: e.activation(
                fs(0), z_ap, AF.Relu if relu else AF.Identity, bias=bcol),
                waits=wl)
            A = fs(0)
            z_done = t1
            pre = [t1]
        if relu:
            lin = A
        else:
            tab = S_(lambda e, h: e.activation(fs(1), A, AF.Abs), waits=pre)
            lin = fs(1)
            pre = [tab]
        tl = S_(lambda e, h: e.activation(fs(2), lin, AF.Ln,
                                          scale=float(2.0 ** 46),
                                          bias=cstcol(2.0)), waits=pre)
        ta = S_(lambda e, h: e.activation(iA, fs(2), AF.Identity,
                                          scale=INV_LN2), waits=[tl])
        tb = V_(lambda e, h: e.scalar_tensor_tensor(
            fs(1), iA, LN2, fs(2), ALU.mult, ALU.is_gt), waits=[ta])
        tn = V_(lambda e, h: e.scalar_tensor_tensor(
            fs(3), fs(1), float(46 - mine), iA, ALU.add, ALU.subtract),
            waits=[tb])
        tcl = V_(lambda e, h: e.tensor_scalar(
            fs(1), fs(3), float(-2 * mine), None, ALU.min), waits=[tn])
        tsc = S_(lambda e, h: e.activation(
            SCB, fs(1), AF.Exp, scale=-LN2,
            bias=cstcol((-mine - man) * LN2)), waits=[tcl])
        trs = S_(lambda e, h: e.activation(
            fs(3), fs(1), AF.Exp, scale=LN2,
            bias=cstcol((mine + man) * LN2)), waits=[tcl])
        tt_ = V_(lambda e, h: e.tensor_tensor(fs(2), A, fs(3), ALU.mult),
                 waits=[trs])
        z_done2 = tt_ if skip_stage1 else z_done
        ta2 = S_(lambda e, h: e.activation(iA, fs(2), AF.Identity,
                                           bias=cstcol(4096.0)),
                 waits=[tt_])
        tc2 = V_(lambda e, h: e.scalar_tensor_tensor(
            fs(1), iA, -4095.5, fs(2), ALU.add, ALU.is_le), waits=[ta2])
        tr = V_(lambda e, h: e.tensor_tensor(fs(3), iA, fs(1), ALU.add),
                waits=[tc2])
        tq = V_(lambda e, h: e.scalar_tensor_tensor(
            out_ap, fs(3), -4096.0, SCB, ALU.add, ALU.mult),
            waits=[tr, tsc])
        chain_done[p] = tq
        return tq, z_done2

    W2R = {}

    def w2q_gen(l):
        toks = []
        for hc in range(32):
            done = []
            for half in range(2):
                pn = qpar[0]
                zin = scr[:, (4 * pn) * 512:(4 * pn) * 512 + 512]
                wl = [chain_done[pn]] if chain_done[pn] is not None else []
                tw = DMAw(lambda e, h, hc=hc, half=half, zin=zin, l=l:
                          e.dma_start(zin, wff[l]["w2"][hc][:,
                                      half * 512:half * 512 + 512]),
                          waits=wl, sem="DZ0" if pn == 0 else "DZ1")
                tq, _ = emit_quant(
                    zin, 512,
                    w2qs[:, half * 512:half * 512 + 512],
                    "e4m3", skip_stage1=True,
                    waits=[tw] + ruse("w2qs"))
                done.append(tq)
            ts_ = DMAw(lambda e, h, hc=hc, l=l: e.dma_start(
                w2q_sc[l][hc], w2qs[:, 0:1024]), waits=done,
                sem="DS0" if l == 0 else "DS1")
            rdone("w2qs", ts_)
            toks.append(ts_)
            W2R[l] = ts_
            yield

    def emit_lambda():
        t1 = V_(lambda e, h: e.tensor_tensor(
            sml[0:1, 0:128], lamt[0:1, 0:128], lamt[0:1, 128:256],
            ALU.mult), waits=[t_lam])
        t2a = V_(lambda e, h: e.reduce_sum(sml2[0:1, 0:1], sml[0:1, 0:64],
                                           AX.X), waits=[t1])
        t2b = V_(lambda e, h: e.reduce_sum(sml2[0:1, 1:2],
                                           sml[0:1, 64:128], AX.X),
                 waits=[t1])
        t3 = S_(lambda e, h: e.activation(sml2[0:1, 2:4], sml2[0:1, 0:2],
                                          AF.Exp), waits=[t2a, t2b])
        t6 = V_(lambda e, h: e.scalar_tensor_tensor(
            sml2[0:1, 4:5], sml2[0:1, 3:4], -LAMBDA_INIT,
            sml2[0:1, 2:3], ALU.add, ALU.subtract), waits=[t3])
        th = V_(lambda e, h: e.tensor_copy(smlb[0:1, 0:1],
                                           sml2[0:1, 4:5]), waits=[t6])
        tl_ = V_(lambda e, h: e.scalar_tensor_tensor(
            smlb[0:1, 2:3], smlb[0:1, 0:1], -1.0, sml2[0:1, 4:5],
            ALU.mult, ALU.add), waits=[th])
        t7 = emit_bcast2([smlb[0:1, 0:2], smlb[0:1, 2:4]],
                         [sml2[:, 5:6], None], [tl_])
        return t7

    NLAM = sml2[:, 5:6]

    def l0_prep_side(side, wbuf, tw, rawbuf, outbuf, xn_dep):
        out_toks = []
        for tc in range(4):
            sl = PSAB[tc % 2][:, 0:512]
            pk = ("pab", tc % 2)
            xw = ([xn_dep[8 * (4 * tc + 3) + 7]]
                  if len(xn_dep) == 128 else xn_dep)
            mt = None
            for ic in range(8):
                mt = (T_ if ic == 7 else T0_)(
                    lambda e, h, ic=ic, tc=tc, sl=sl, wbuf=wbuf:
                        e.matmul(sl, wbuf[:, ic * 128:(ic + 1) * 128],
                                 xnT[:, ic * T + tc * 512:
                                     ic * T + tc * 512 + 512],
                                 start=(ic == 0), stop=(ic == 7)),
                        waits=(([tw] + xw + ruse(pk))
                               if ic == 0 else []))
            if PREP_STAGE <= 0:
                rdone(pk, mt)
                out_toks.append(mt)
                continue
            t_raw = V_(lambda e, h, tc=tc, sl=sl, rawbuf=rawbuf:
                       e.tensor_copy(rawbuf[:, tc * 512:(tc + 1) * 512],
                                     sl), waits=[mt])
            sq = outbuf[:, tc * 512:(tc + 1) * 512]
            t_sq = S_(lambda e, h, sl=sl, sq=sq: e.activation(
                sq, sl, AF.Square), waits=[mt, t_raw])
            rdone(pk, [t_raw, t_sq])
            if PREP_STAGE <= 1:
                out_toks.append(t_sq)
                rdone("pcb", t_raw)
                continue
            t_r0 = T_(lambda e, h, sq=sq: e.matmul(
                psS[:, 0:512], onesq[0:64, 0:128], sq[0:64, :],
                start=True, stop=True),
                waits=[t_sq, t_blkc] + ruse("psS"))
            t_r1 = T_(lambda e, h, sq=sq: e.matmul(
                psR[:, 0:512], onesq[64:128, 0:128], sq[64:128, :],
                start=True, stop=True),
                waits=[t_sq] + ruse("psR"))
            bias_r = cstcol(-math.log(8.0)) if side == 0 else cstcol(0.0)
            tu0 = S_(lambda e, h: e.activation(
                rmsr[0:64, 0:512], psS[0:64, 0:512], AF.Ln,
                scale=1.0 / 64.0, bias=cstcol(RMS_EPS)[0:64, :]),
                waits=[t_r0] + ruse("rmsr"))
            rdone("psS", tu0)
            tu1 = S_(lambda e, h: e.activation(
                rmsr[64:128, 0:512], psR[64:128, 0:512], AF.Ln,
                scale=1.0 / 64.0, bias=cstcol(RMS_EPS)[64:128, :]),
                waits=[t_r1])
            rdone("psR", tu1)
            t_bc = S_(lambda e, h, bias_r=bias_r: e.activation(
                rmsb[:, 0:512], rmsr[:, 0:512], AF.Exp, scale=-0.5,
                bias=bias_r), waits=[tu0, tu1] + ruse("rmsb"))
            rdone("rmsr", t_bc)
            if PREP_STAGE <= 2:
                out_toks.append(t_bc)
                continue
            t_sw = T_(lambda e, h, tc=tc, rawbuf=rawbuf: e.matmul(
                psR[:, 0:512], perm0[:],
                rawbuf[:, tc * 512:(tc + 1) * 512], start=True, stop=True),
                waits=[t_raw, t_perm0] + ruse("psR"))
            tm1 = V_(lambda e, h, tc=tc, rawbuf=rawbuf: e.tensor_tensor(
                pcb[:, 0:512], rawbuf[:, tc * 512:(tc + 1) * 512],
                ropeC[:, tc * 512:(tc + 1) * 512], ALU.mult),
                waits=[t_raw, t_ropeC] + ruse("pcb"))
            ts2 = V_(lambda e, h, tc=tc: e.tensor_tensor(
                pcb[:, 512:1024], psR[:, 0:512],
                ropeS[:, tc * 512:(tc + 1) * 512], ALU.mult),
                waits=[t_sw, t_ropeS])
            rdone("psR", ts2)
            tsum = V_(lambda e, h: e.tensor_tensor(
                pcb[:, 0:512], pcb[:, 0:512], pcb[:, 512:1024], ALU.add),
                waits=[tm1, ts2])
            tq = V_(lambda e, h, tc=tc, outbuf=outbuf: e.tensor_tensor(
                outbuf[:, tc * 512:(tc + 1) * 512], pcb[:, 0:512],
                rmsb[:, 0:512], ALU.mult), waits=[tsum, t_bc])
            rdone("pcb", tq)
            rdone("rmsb", tq)
            radd("ropes", tq)
            out_toks.append(tq)
        return out_toks

    def l0_head(hd, nlam_tok, xn_dep, out_toks, pump=None):
        tw_q = DMAg(lambda e, h, hd=hd: e.dma_start(wbA[:], wq0[hd]),
                    waits=ruse("wbA"), sem="DA")
        tw_k = DMAg(lambda e, h, hd=hd: e.dma_start(wbB[:], wk0[hd]),
                    waits=ruse("wbB"), sem="DB")
        tw_v = DMAg(lambda e, h, hd=hd: e.dma_start(wbC[:], wv0[hd]),
                    waits=ruse("wbC"), sem="DC")
        tqs = l0_prep_side(0, wbA, tw_q, p0, qh, xn_dep)
        rdone("wbA", tqs[-1])
        tks = l0_prep_side(1, wbB, tw_k, p1, kh, xn_dep)
        rdone("wbB", tks[-1])
        if ATTN_STAGE <= 1:
            out_toks.extend(tqs + tks)
            return out_toks
        vt = []
        for qt in range(16):
            sl = PSCD[qt % 2][:, 0:128]
            pk = ("pcd", qt % 2)
            xw = ([xn_dep[8 * qt + 7]] if len(xn_dep) == 128 else xn_dep)
            mt = None
            for ic in range(8):
                mt = (T_ if ic == 7 else T0_)(
                    lambda e, h, ic=ic, qt=qt, sl=sl: e.matmul(
                    sl, xnT[:, ic * T + qt * 128:ic * T + qt * 128 + 128],
                    wbC[:, ic * 128:(ic + 1) * 128],
                    start=(ic == 0), stop=(ic == 7)),
                    waits=(([tw_v] + xw + ruse(pk))
                           if ic == 0 else []))
            tv = V_(lambda e, h, qt=qt, sl=sl: e.tensor_copy(
                vtok[:, qt * VS:qt * VS + 64], sl[:, 0:64]), waits=[mt])
            tv = V_(lambda e, h, qt=qt, sl=sl: e.tensor_copy(
                vtok[:, qt * VS + 65:qt * VS + 129], sl[:, 64:128]),
                waits=[tv])
            rdone(pk, tv)
            vt.append(tv)
        rdone("wbC", vt[-1])
        if ATTN_STAGE <= 2:
            out_toks.extend(vt)
            return out_toks

        def stageA(qt):
            par = qt % 2
            exp_by = []
            for side, pbuf in ((0, p0[:, par * T:par * T + T]),
                               (1, p1[:, par * T:par * T + T])):
                side_toks = []
                ng = (qt + 4) // 4
                for g in range(ng):
                    jn = min(4, qt + 1 - 4 * g)
                    sl = PSAB[g % 2][:, 0:512]
                    pk = ("pab", g % 2)
                    mlast = None
                    for j in range(jn):
                        kb = 4 * g + j
                        mlast = (T_ if j == jn - 1 else T0_)(
                            lambda e, h, j=j, kb=kb, side=side, qt=qt,
                            sl=sl: e.matmul(
                            sl[:, j * 128:(j + 1) * 128],
                            kh[64 * side:64 * side + 64,
                               kb * 128:(kb + 1) * 128],
                            qh[64 * side:64 * side + 64,
                               qt * 128:(qt + 1) * 128],
                            start=True, stop=True),
                            waits=(([tqs[-1], tks[-1]] + ruse(pk))
                                   if j == 0 else []))
                    wl = [mlast, t_mask]
                    if g == ng - 1:
                        jd = qt - 4 * g
                        tmask = V_(lambda e, h, jd=jd, sl=sl:
                                   e.tensor_tensor(
                            sl[:, jd * 128:(jd + 1) * 128],
                            sl[:, jd * 128:(jd + 1) * 128],
                            maskb[:], ALU.add), waits=wl)
                        wl = [tmask]
                    texp = S_(lambda e, h, jn=jn, g=g, pbuf=pbuf,
                              sl=sl: e.activation(
                        pbuf[:, g * 512:g * 512 + jn * 128],
                        sl[:, 0:jn * 128], AF.Exp, bias=cstcol(-8.0)),
                        waits=wl + ruse(("pex", side, par)))
                    rdone(pk, texp)
                    side_toks.append(texp)
                exp_by.append(side_toks)
            return exp_by

        def stageB(qt, exp_by):
            par = qt % 2
            pvps = PSCD[par]
            pvk = ("pcd", par)
            pvt = None
            for side, base in ((0, 0), (1, 256)):
                pbuf = (p0 if side == 0 else p1)[:, par * T:par * T + T]
                for kb in range(qt + 1):
                    pvt = (T_ if kb == qt else T0_)(
                        lambda e, h, kb=kb, qt=qt, base=base, pbuf=pbuf:
                        e.matmul(
                        pvps[:, base:base + 129],
                        pbuf[:, kb * 128:(kb + 1) * 128],
                        vtok[:, kb * VS:kb * VS + 129],
                        start=(kb == 0), stop=(kb == qt)),
                        waits=[exp_by[side][kb // 4], vt[kb]] +
                        (ruse(pvk) if kb == 0 and side == 0 else []))
                rdone(("pex", side, par), pvt)
            c8 = 8 + 16 * par
            rz0 = V_(lambda e, h, c8=c8: e.reciprocal(
                sml2[:, c8:c8 + 1], pvps[:, 64:65]), waits=[pvt])
            rz1 = V_(lambda e, h, c8=c8: e.reciprocal(
                sml2[:, c8 + 1:c8 + 2], pvps[:, 320:321]), waits=[pvt])
            t1a = V_(lambda e, h, c8=c8: e.tensor_scalar(
                pcb[:, 128:192], pvps[:, 256:320], sml2[:, c8 + 1:c8 + 2],
                NLAM, ALU.mult, ALU.mult),
                waits=[rz1, nlam_tok] + ruse("pcb"))
            t1b = V_(lambda e, h, c8=c8: e.tensor_scalar(
                pcb[:, 192:256], pvps[:, 321:385], sml2[:, c8 + 1:c8 + 2],
                NLAM, ALU.mult, ALU.mult), waits=[rz1])
            t2a = V_(lambda e, h, c8=c8: e.scalar_tensor_tensor(
                pcb[:, 0:64], pvps[:, 0:64], sml2[:, c8:c8 + 1],
                pcb[:, 128:192], ALU.mult, ALU.add),
                waits=[rz0, t1a])
            t2b = V_(lambda e, h, c8=c8: e.scalar_tensor_tensor(
                pcb[:, 64:128], pvps[:, 65:129], sml2[:, c8:c8 + 1],
                pcb[:, 192:256], ALU.mult, ALU.add), waits=[t2a, t1b])
            rdone(pvk, t2b)
            tst = V_(lambda e, h: e.bn_stats(
                stats[:, 0:6], pcb[:, 0:128]),
                waits=[t2b] + ruse("aggr0", "stats"))
            tag = V_(lambda e, h: e.bn_aggr(aggr[:, 0:2], stats[:, 0:6]),
                     waits=[tst])
            rdone("stats", tag)
            tu = S_(lambda e, h: e.activation(sml2[:, 11:12], aggr[:, 1:2],
                                              AF.Ln, bias=cstcol(LN_EPS)),
                    waits=[tag] + ruse("lnsml"))
            tr_ = S_(lambda e, h: e.activation(
                sml2[:, 12:13], sml2[:, 11:12], AF.Exp, scale=-0.5,
                bias=cstcol(math.log(1.0 - LAMBDA_INIT))), waits=[tu])
            tln = V_(lambda e, h: e.tensor_scalar(
                pcb[:, 256:384], pcb[:, 0:128], aggr[:, 0:1],
                sml2[:, 12:13], ALU.subtract, ALU.mult),
                waits=[tr_])
            rdone("aggr0", tln)
            rdone("lnsml", tln)
            ttr = T_(lambda e, h: e.transpose(psU[:, 0:128],
                                              pcb[:, 256:384],
                                              ident[:]),
                     waits=[tln] + ruse(("psU", 0)))
            tev = V_(lambda e, h, hd=hd, qt=qt: e.tensor_copy(
                aT[:, hd * T + qt * 128:hd * T + qt * 128 + 128],
                psU[:, 0:128]), waits=[ttr])
            rdone(("psU", 0), tev)
            rdone("pcb", [tln, ttr, tev])
            out_toks.append(tev)

        prev = None
        for qt in range(16):
            eb = stageA(qt)
            if prev is not None:
                stageB(qt - 1, prev)
                if pump is not None:
                    pump(1)
            prev = eb
        stageB(15, prev)
        if pump is not None:
            pump(1)
        return out_toks

    def emit_out_proj(wdram, src, src_toks, bias_colst, n_oc, xadd_toks):
        for oc in range(n_oc):
            par = oc % 2
            wb = wbA if par == 0 else wbB
            tw = DMAg(lambda e, h, oc=oc, wb=wb: e.dma_start(
                wb[:], wdram[oc]),
                waits=ruse("wbA" if par == 0 else "wbB"),
                sem="DA" if par == 0 else "DB")
            mlast = None
            for tc in range(4):
                qd = tc % 2
                sl = PSAB[qd][:, 0:512]
                pk = ("pab", qd)
                stc = ([src_toks[tc]] if len(src_toks) == 4
                       else src_toks)
                mt = None
                for ic in range(8):
                    mt = (T_ if ic == 7 else T0_)(
                        lambda e, h, ic=ic, tc=tc, sl=sl, wb=wb:
                            e.matmul(sl,
                                     wb[:, ic * 128:(ic + 1) * 128],
                                     src[:, ic * T + tc * 512:
                                         ic * T + tc * 512 + 512],
                                     start=(ic == 0), stop=(ic == 7)),
                            waits=(([tw] + stc + ruse(pk))
                                   if ic == 0 else []))
                mlast = mt
                if bias_colst is not None:
                    tev = V_(lambda e, h, sl=sl, oc=oc: e.tensor_scalar(
                        pcb[:, 0:512], sl, bias_colst[:, oc:oc + 1], None,
                        ALU.add), waits=[mt] + ruse("pcb"))
                else:
                    tev = V_(lambda e, h, sl=sl: e.tensor_copy(
                        pcb[:, 0:512], sl), waits=[mt] + ruse("pcb"))
                rdone(pk, tev)
                lastadd = None
                for j in range(4):
                    ttr = T_(lambda e, h, j=j: e.transpose(
                        psU[:, 128:256], pcb[:, j * 128:(j + 1) * 128],
                        ident[:]), waits=[tev] + ruse(("psU", 1)))
                    tqi = tc * 4 + j
                    tadd = V_(lambda e, h, tqi=tqi, oc=oc:
                              e.scalar_tensor_tensor(
                        x_res[:, tqi * 1024 + oc * 128:
                              tqi * 1024 + oc * 128 + 128],
                        psU[:, 128:256], 0.0,
                        x_res[:, tqi * 1024 + oc * 128:
                              tqi * 1024 + oc * 128 + 128],
                        ALU.add, ALU.add), waits=[ttr])
                    rdone(("psU", 1), tadd)
                    lastadd = tadd
                    xadd_toks.append(tadd)
                rdone("pcb", lastadd)
            rdone("wbA" if par == 0 else "wbB", mlast)
        return xadd_toks

    TLAST = {}

    def emit_ffn(l, xn_dep, xadd_toks, w2ready):
        tb2q, _ = emit_quant(b2cols[l][:, 0:8], 8, b2qcols[l][:, 0:8],
                             "fp16", skip_stage1=True, waits=[t_b2[l]])
        for blk in range(2):
            hq_toks = {}
            for hc in range(32):
                par = hc % 2
                wb = wbA if par == 0 else wbB
                tw = DMAg(lambda e, h, hc=hc, wb=wb, l=l: e.dma_start(
                    wb[:], wff[l]["w1"][hc]),
                    waits=ruse("wbA" if par == 0 else "wbB"),
                    sem="DA" if par == 0 else "DB")
                hts = []
                mm_l = None
                for half in range(2):
                    tok0 = blk * 1024 + half * 512
                    qd = (2 * hc + half) % 2
                    sl = PSAB[qd][:, 0:512]
                    pk = ("pab", qd)
                    mt = None
                    for ic in range(8):
                        mt = (T_ if ic == 7 else T0_)(
                            lambda e, h, ic=ic, sl=sl, wb=wb,
                                tok0=tok0: e.matmul(
                            sl, wb[:, ic * 128:(ic + 1) * 128],
                            xnT[:, ic * T + tok0:ic * T + tok0 + 512],
                            start=(ic == 0), stop=(ic == 7)),
                            waits=(([tw] + xn_dep + ruse(pk))
                                   if ic == 0 else []))
                    mm_l = mt
                    out_sl = hqv[:, hc * 1024 + half * 512:
                                 hc * 1024 + half * 512 + 512]
                    tqd, zd = emit_quant(sl, 512, out_sl, "e4m3",
                                         bias_col=b1cols[l][:, hc:hc + 1],
                                         relu=True,
                                         waits=[mt, t_b1[l]] +
                                         ruse(("hqw", hc)))
                    rdone(pk, zd)
                    hts.append(tqd)
                rdone("wbA" if par == 0 else "wbB", mm_l)
                hq_toks[hc] = hts
                rdone(("hqw", hc), hts)
            for tc2 in range(2):
                tok_in_blk = tc2 * 512
                for og in range(2):
                    mlast = [None] * 4
                    for hc in range(32):
                        par = hc % 2
                        tws = DMAw(lambda e, h, hc=hc, par=par, l=l:
                                   e.dma_start(
                            w2qs[:, par * 1024:par * 1024 + 1024],
                            w2q_sc[l][hc]),
                            waits=[w2ready] + ruse(("w2qsb", par),
                                                    "w2qs"),
                            sem="W2A" if par == 0 else "W2B")
                        for oc4 in range(4):
                            oc = og * 4 + oc4
                            mlast[oc4] = (T_ if (hc == 31 or oc4 == 3) else T0_)(
                                lambda e, h, oc4=oc4, oc=oc, par=par,
                                hc=hc, tok_in_blk=tok_in_blk: e.matmul(
                                    PS4[oc4][:, 0:512],
                                    w2qs[:, par * 1024 + oc * 128:
                                         par * 1024 + oc * 128 + 128],
                                    hqv[:, hc * 1024 + tok_in_blk:
                                        hc * 1024 + tok_in_blk + 512],
                                    start=(hc == 0), stop=(hc == 31)),
                                waits=([tws] + hq_toks[hc] +
                                       (ruse(("pab", 0), ("pab", 1),
                                             ("pcd", 0), ("pcd", 1))
                                        if hc == 0 else [])))
                        rdone(("w2qsb", par), mlast[3])
                        radd("w2qs", mlast[3])
                    for oc4 in range(4):
                        oc = og * 4 + oc4
                        sl = PS4[oc4][:, 0:512]
                        pk = [("pab", 0), ("pab", 1), ("pcd", 0),
                              ("pcd", 1)][oc4]
                        tqd, zd = emit_quant(
                            sl, 512, pcb[:, 0:512], "fp16",
                            bias_col=b2qcols[l][:, oc:oc + 1], relu=False,
                            waits=[mlast[oc4], tb2q] + ruse("pcb"))
                        rdone(pk, zd)
                        lastadd = None
                        for j in range(4):
                            ttr = T_(lambda e, h, j=j: e.transpose(
                                psU[:, 128:256],
                                pcb[:, j * 128:(j + 1) * 128], ident[:]),
                                waits=[tqd] + ruse(("psU", 1)))
                            tqi = blk * 8 + tc2 * 4 + j
                            tadd = V_(lambda e, h, tqi=tqi, oc=oc:
                                      e.scalar_tensor_tensor(
                                x_res[:, tqi * 1024 + oc * 128:
                                      tqi * 1024 + oc * 128 + 128],
                                psU[:, 128:256], 0.0,
                                x_res[:, tqi * 1024 + oc * 128:
                                      tqi * 1024 + oc * 128 + 128],
                                ALU.add, ALU.add), waits=[ttr])
                            rdone(("psU", 1), tadd)
                            TLAST[tqi] = tadd
                            lastadd = tadd
                            xadd_toks.append(tadd)
                        rdone("pcb", lastadd)
        return xadd_toks

    def pair_qk(ph, wdr, bcol, src, src_toks, out):
        tw = DMAg(lambda e, h, ph=ph, wdr=wdr: e.dma_start(
            wbA[:, 0:256], wdr[ph]), waits=ruse("wbA"), sem="DA")
        lt = None
        for tc in range(4):
            qd = tc % 2
            sl = PSAB[qd][:, 0:512]
            pk = ("pab", qd)
            mt = None
            for cc in range(2):
                mt = (T_ if cc == 1 else T0_)(
                    lambda e, h, cc=cc, tc=tc, sl=sl, src=src: e.matmul(
                    sl, wbA[:, cc * 128:(cc + 1) * 128],
                    src[:, cc * T + tc * 512:cc * T + tc * 512 + 512],
                    start=(cc == 0), stop=(cc == 1)),
                    waits=(([tw] + src_toks + ruse(pk))
                           if cc == 0 else []))
            te = S_(lambda e, h, sl=sl, tc=tc, out=out, bcol=bcol:
                    e.activation(out[:, tc * 512:(tc + 1) * 512], sl,
                                 AF.Gelu, bias=bcol), waits=[mt])
            rdone(pk, te)
            lt = te
        rdone("wbA", lt)
        return lt

    def rope_pair(wdr, bcol, out, xn_dep, trc, trs):
        tw = DMAg(lambda e, h, wdr=wdr: e.dma_start(wbA[:], wdr[0]),
                  waits=ruse("wbA"), sem="DA")
        lt = None
        for tc in range(4):
            qd = tc % 2
            sl = PSAB[qd][:, 0:512]
            pk = ("pab", qd)
            xw = ([xn_dep[8 * (4 * tc + 3) + 7]]
                  if len(xn_dep) == 128 else xn_dep)
            mt = None
            for ic in range(8):
                mt = (T_ if ic == 7 else T0_)(
                    lambda e, h, ic=ic, tc=tc, sl=sl: e.matmul(
                    sl, wbA[:, ic * 128:(ic + 1) * 128],
                    xnT[:, ic * T + tc * 512:ic * T + tc * 512 + 512],
                    start=(ic == 0), stop=(ic == 7)),
                    waits=(([tw] + xw + ruse(pk))
                           if ic == 0 else []))
            traw = S_(lambda e, h, sl=sl, tc=tc, bcol=bcol: e.activation(
                p1[:, tc * 512:(tc + 1) * 512], sl, AF.Identity,
                bias=bcol), waits=[mt] + ruse("p1r"))
            rdone(pk, traw)
            tsw = T_(lambda e, h, tc=tc: e.matmul(
                psS[:, 0:512], perm1[:],
                p1[:, tc * 512:(tc + 1) * 512], start=True, stop=True),
                waits=[traw, t_perm1] + ruse("psS"))
            tm1 = V_(lambda e, h, tc=tc: e.tensor_tensor(
                pcb[:, 0:512], p1[:, tc * 512:(tc + 1) * 512],
                ropeC[:, tc * 512:(tc + 1) * 512], ALU.mult),
                waits=[traw, trc] + ruse("pcb"))
            ts2 = V_(lambda e, h, tc=tc: e.tensor_tensor(
                pcb[:, 512:1024], psS[:, 0:512],
                ropeS[:, tc * 512:(tc + 1) * 512], ALU.mult),
                waits=[tsw, trs])
            rdone("psS", ts2)
            tsum = V_(lambda e, h: e.tensor_tensor(
                pcb[:, 0:512], pcb[:, 0:512], pcb[:, 512:1024], ALU.add),
                waits=[tm1, ts2])
            tg = S_(lambda e, h, tc=tc, out=out: e.activation(
                out[:, tc * 512:(tc + 1) * 512], pcb[:, 0:512], AF.Gelu),
                waits=[tsum])
            rdone("pcb", tg)
            radd("p1r", tg)
            lt = tg
        rdone("wbA", lt)
        return lt

    def emit_bounds(ph):
        first = V_(lambda e, h: e.memset(sml[0:1, 16:32], 0.0),
                   waits=ruse("bnd"))
        tmx_toks = []
        for side, buf in ((0, qh), (1, kh)):
            for tc in range(4):
                sq = p1[:, tc * 512:(tc + 1) * 512]
                tsq = S_(lambda e, h, buf=buf, tc=tc, sq=sq:
                         e.activation(
                    sq, buf[:, tc * 512:(tc + 1) * 512], AF.Square),
                    waits=ruse("bsq", "p1r") + [first])
                for h2 in range(2):
                    ps_ = psS if h2 == 0 else psR
                    pk_ = "psS" if h2 == 0 else "psR"
                    tred = T_(lambda e, h, sq=sq, h2=h2, ps_=ps_:
                              e.matmul(
                        ps_[:, 0:512],
                        onesq[64 * h2:64 * h2 + 64, 0:128],
                        sq[64 * h2:64 * h2 + 64, :],
                        start=True, stop=True),
                        waits=[tsq, t_blkc] + ruse(pk_))
                    cix = 16 + side * 8 + h2 * 4 + tc
                    tmx = V_(lambda e, h, cix=cix, ps_=ps_: e.reduce_max(
                        sml[0:1, cix:cix + 1],
                        ps_[0:1, 0:512], AX.X), waits=[tred])
                    rdone(pk_, tmx)
                    tmx_toks.append(tmx)
                rdone("bsq", tmx_toks[-2:])
        tms = []
        for side in range(2):
            for h2 in range(2):
                c0 = 16 + side * 8 + h2 * 4
                tms.append(V_(lambda e, h, c0=c0, side=side, h2=h2:
                              e.reduce_max(
                    sml2[0:1, 16 + side * 2 + h2:17 + side * 2 + h2],
                    sml[0:1, c0:c0 + 4], AX.X), waits=tmx_toks))
        tp = V_(lambda e, h: e.tensor_tensor(
            sml2[0:1, 20:22], sml2[0:1, 16:18], sml2[0:1, 18:20],
            ALU.mult), waits=tms)
        tp = S_(lambda e, h: e.activation(sml2[0:1, 22:24],
                                          sml2[0:1, 20:22], AF.Ln,
                                          bias=cstcol(2.0 ** -60)[0:1, :]),
                waits=[tp])
        tp = S_(lambda e, h: e.activation(
            sml2[0:1, 24:26], sml2[0:1, 22:24], AF.Exp, scale=0.5,
            bias=cstcol(-math.log(8.0))[0:1, :]), waits=[tp])
        tp = V_(lambda e, h: e.tensor_scalar(
            smlb[0:1, 4:6], sml2[0:1, 24:26], 40.0, -1.0, ALU.min,
            ALU.mult), waits=[tp])
        t2 = emit_bcast2([smlb[0:1, 4:6]],
                         [bcols[:, 2 * ph:2 * ph + 1],
                          bcols[:, 2 * ph + 1:2 * ph + 2]], [tp])
        rdone("bnd", [t2])
        return t2

    def l1_attention(xn_dep, aT_toks):
        DMAg(lambda e, h: e.dma_start(ropeC[:], ropec1[0]),
             waits=ruse("ropes"), sem="Dr")
        trs = DMAg(lambda e, h: e.dma_start(ropeS[:], ropec1[1]),
                   sem="Dr")
        trc = trs
        cq_toks = []
        ckv_toks = []
        for (wdr, bc_sl, out, toks) in ((wq1, bias1[:, 0:2], cqb, cq_toks),
                                        (wkv1, bias2[:, 0:2], ckvb,
                                         ckv_toks)):
            for oc in range(2):
                tw = DMAg(lambda e, h, oc=oc, wdr=wdr: e.dma_start(
                    wbA[:], wdr[oc]), waits=ruse("wbA"), sem="DA")
                lt = None
                for tc in range(4):
                    qd = tc % 2
                    sl = PSAB[qd][:, 0:512]
                    pk = ("pab", qd)
                    xw = ([xn_dep[8 * (4 * tc + 3) + 7]]
                          if len(xn_dep) == 128 else xn_dep)
                    mt = None
                    for ic in range(8):
                        mt = (T_ if ic == 7 else T0_)(
                            lambda e, h, ic=ic, tc=tc, sl=sl: e.matmul(
                            sl, wbA[:, ic * 128:(ic + 1) * 128],
                            xnT[:, ic * T + tc * 512:
                                ic * T + tc * 512 + 512],
                            start=(ic == 0), stop=(ic == 7)),
                            waits=(([tw] + xw + t_bias +
                                    ruse(pk))
                                   if ic == 0 else []))
                    te = S_(lambda e, h, sl=sl, oc=oc, tc=tc, out=out,
                            bc_sl=bc_sl: e.activation(
                        out[:, oc * T + tc * 512:oc * T + tc * 512 + 512],
                        sl, AF.Gelu, bias=bc_sl[:, oc:oc + 1]),
                        waits=[mt])
                    rdone(pk, te)
                    toks.append(te)
                    lt = te
                rdone("wbA", lt)
        for ph in range(8):
            if ph < 7:
                tq_ = pair_qk(ph, wqc1, bias1[:, 2 + ph:3 + ph], cqb,
                              cq_toks, qh)
                tk_ = pair_qk(ph, wkc1, bias2[:, 2 + ph:3 + ph], ckvb,
                              ckv_toks, kh)
            else:
                tq_ = rope_pair(wqr1, bias1[:, 9:10], qh, xn_dep, trc, trs)
                tk_ = rope_pair(wkr1, bias2[:, 9:10], kh, xn_dep, trc, trs)
            twv = DMAg(lambda e, h, ph=ph: e.dma_start(wbC[:, 0:256],
                                                       wv1[ph]),
                       waits=ruse("wbC"), sem="DC")
            vt = []
            for qt in range(16):
                sl = PSCD[qt % 2][:, 0:128]
                pk = ("pcd", qt % 2)
                mt = None
                for cc in range(2):
                    mt = (T_ if cc == 1 else T0_)(
                        lambda e, h, cc=cc, qt=qt, sl=sl: e.matmul(
                        sl, ckvb[:, cc * T + qt * 128:
                                 cc * T + qt * 128 + 128],
                        wbC[:, cc * 128:(cc + 1) * 128],
                        start=(cc == 0), stop=(cc == 1)),
                        waits=(([twv] + ckv_toks + ruse(pk))
                               if cc == 0 else []))
                tv = V_(lambda e, h, qt=qt, sl=sl: e.tensor_copy(
                    vtok[:, qt * VS:qt * VS + 64], sl[:, 0:64]),
                    waits=[mt])
                tv = V_(lambda e, h, qt=qt, sl=sl: e.tensor_copy(
                    vtok[:, qt * VS + 65:qt * VS + 129], sl[:, 64:128]),
                    waits=[tv])
                rdone(pk, tv)
                vt.append(tv)
            rdone("wbC", vt[-1])
            bnd_tok = emit_bounds(ph)

            def stageA1(qt, h2):
                par = (2 * qt + h2) % 2
                po = p0[:, par * T:par * T + T]
                hh = 2 * ph + h2
                side_toks = []
                ng = (qt + 4) // 4
                for g in range(ng):
                    jn = min(4, qt + 1 - 4 * g)
                    sl = PSAB[g % 2][:, 0:512]
                    pk = ("pab", g % 2)
                    mlast = None
                    for j in range(jn):
                        kb = 4 * g + j
                        mlast = (T_ if j == jn - 1 else T0_)(
                            lambda e, h, j=j, kb=kb, h2=h2, qt=qt,
                            sl=sl: e.matmul(
                            sl[:, j * 128:(j + 1) * 128],
                            kh[64 * h2:64 * h2 + 64,
                               kb * 128:(kb + 1) * 128],
                            qh[64 * h2:64 * h2 + 64,
                               qt * 128:(qt + 1) * 128],
                            start=True, stop=True),
                            waits=(([tq_, tk_] + ruse(pk))
                                   if j == 0 else []))
                    wl = [mlast, t_mask]
                    if g == ng - 1:
                        jd = qt - 4 * g
                        tmask = V_(lambda e, h, jd=jd, sl=sl:
                                   e.tensor_tensor(
                            sl[:, jd * 128:(jd + 1) * 128],
                            sl[:, jd * 128:(jd + 1) * 128],
                            maskb[:], ALU.add), waits=wl)
                        wl = [tmask]
                    texp = S_(lambda e, h, jn=jn, g=g, hh=hh, sl=sl,
                              po=po: e.activation(
                        po[:, g * 512:g * 512 + jn * 128],
                        sl[:, 0:jn * 128], AF.Exp, scale=0.125,
                        bias=bcols[:, hh:hh + 1]),
                        waits=wl + [bnd_tok] + ruse(("pex", par)))
                    rdone(pk, texp)
                    side_toks.append(texp)
                return side_toks

            def stageB1(qt, h2, chunks):
                par = (2 * qt + h2) % 2
                po = p0[:, par * T:par * T + T]
                pvps = PSCD[qt % 2]
                pvk = ("pcd", qt % 2)
                base = 256 * h2
                pvt = None
                for kb in range(qt + 1):
                    pvt = (T_ if kb == qt else T0_)(
                        lambda e, h, kb=kb, qt=qt, h2=h2, base=base,
                        po=po: e.matmul(
                        pvps[:, base:base + 65],
                        po[:, kb * 128:(kb + 1) * 128],
                        vtok[:, kb * VS + 64 * h2:kb * VS + 64 * h2 + 65],
                        start=(kb == 0), stop=(kb == qt)),
                        waits=[chunks[kb // 4], vt[kb]] +
                        (ruse(pvk) if kb == 0 and h2 == 0 else []))
                rdone(("pex", par), pvt)
                return pvt

            def tail1(qt, pv0, pv1):
                pvps = PSCD[qt % 2]
                pvk = ("pcd", qt % 2)
                c8 = 8 + 2 * (qt % 2)
                rz0 = V_(lambda e, h, c8=c8, pvps=pvps: e.reciprocal(
                    sml2[:, c8:c8 + 1], pvps[:, 64:65]),
                    waits=[pv0, pv1])
                rz1 = V_(lambda e, h, c8=c8, pvps=pvps: e.reciprocal(
                    sml2[:, c8 + 1:c8 + 2], pvps[:, 256:257]),
                    waits=[pv1])
                tno0 = V_(lambda e, h, c8=c8, pvps=pvps: e.tensor_scalar(
                    pcb[:, 0:64], pvps[:, 0:64],
                    sml2[:, c8:c8 + 1], None, ALU.mult),
                    waits=[rz0] + ruse("pcb"))
                tno1 = V_(lambda e, h, c8=c8, pvps=pvps: e.tensor_scalar(
                    pcb[:, 64:128], pvps[:, 257:321],
                    sml2[:, c8 + 1:c8 + 2], None, ALU.mult),
                    waits=[rz1])
                rdone(pvk, tno1)
                ttr = T_(lambda e, h: e.transpose(
                    psU[:, 0:128], pcb[:, 0:128], ident[:]),
                    waits=[tno0, tno1] + ruse(("psU", 0)))
                rdone("pcb", ttr)
                tev = V_(lambda e, h, ph=ph, qt=qt: e.tensor_scalar(
                    aT[:, ph * T + qt * 128:ph * T + qt * 128 + 128],
                    psU[:, 0:128], bvcol[:, ph:ph + 1], None,
                    ALU.add), waits=[ttr])
                rdone(("psU", 0), tev)
                aT_toks.append(tev)

            units = [(qt, h2) for qt in range(16) for h2 in range(2)]
            pend = {}
            prevu = None
            for u in units:
                chunks = stageA1(*u)
                pend[u] = [chunks, None]
                if prevu is not None:
                    pend[prevu][1] = stageB1(prevu[0], prevu[1],
                                             pend[prevu][0])
                    if prevu[1] == 1:
                        q0 = prevu[0]
                        tail1(q0, pend[(q0, 0)][1], pend[(q0, 1)][1])
                        del pend[(q0, 0)], pend[(q0, 1)]
                prevu = u
            pend[prevu][1] = stageB1(prevu[0], prevu[1], pend[prevu][0])
            tail1(15, pend[(15, 0)][1], pend[(15, 1)][1])
        return aT_toks

    lvl = SCHED_LEVEL
    final_dep = []

    def gen_w2():
        for g in (w2q_gen(0), w2q_gen(1)):
            for _ in g:
                yield

    def make_pump(gen):
        state = [gen is None]

        def pump(n=1):
            if state[0]:
                return
            try:
                for _ in range(n):
                    next(gen)
            except StopIteration:
                state[0] = True

        def drain():
            if state[0]:
                return
            for _ in gen:
                pass
            state[0] = True
        return pump, drain

    gen0 = gen_w2() if lvl >= 2 else None
    pump0, drain0 = make_pump(gen0)
    pump0(6)
    ln0 = emit_layernorm(t_x + [t_ident])
    xn_dep0 = ln0 if len(ln0) == 128 else [ln0[-1]]
    final_dep = [ln0[-1]]
    if lvl >= 1:
        nlam_tok = emit_lambda() if LAMBDA_ON else None
        l0_toks = []
        for hd in range(N_HEADS_L0):
            l0_head(hd, nlam_tok, xn_dep0, l0_toks, pump=pump0)
        drain0()
        final_dep = l0_toks[-1:] or final_dep
        if PROJ_ON:
            xadd0 = []
            ptoks = ([l0_toks[-16:][4 * tc + 3] for tc in range(4)]
                     if len(l0_toks) >= 16 else l0_toks[-1:])
            emit_out_proj(wproj0, aT, ptoks, None, 8, xadd0)
            final_dep = [xadd0[-1]]
    if lvl >= 2:
        emit_ffn(0, [xn_dep0[-1]], xadd0, W2R[0])
        final_dep = [xadd0[-1]]
    if lvl >= 3:
        ln1 = emit_layernorm([xadd0[-1]])
        xn_dep1 = ln1 if len(ln1) == 128 else [ln1[-1]]
        aT_toks1 = []
        l1_attention(xn_dep1, aT_toks1)
        xadd1 = []
        p1toks = ([aT_toks1[-16:][4 * tc + 3] for tc in range(4)]
                  if len(aT_toks1) >= 16 else [aT_toks1[-1]])
        emit_out_proj(wo1, aT, p1toks, bocol, 8, xadd1)
        final_dep = [xadd1[-1]]
    if lvl >= 4:
        emit_ffn(1, [xn_dep1[-1]], xadd1, W2R[1])
        final_dep = [xadd1[-1]]

    for i in range(16):
        fd = [TLAST[i]] if lvl >= 4 and i in TLAST else final_dep
        DMAo(lambda e, h, i=i: e.dma_start(
            y3[i], x_res[:, i * 1024:(i + 1) * 1024]), waits=fd)
    P.op("sync", lambda e, h: None,
         waits=[(s, P.sem_counts[s]) for s in
                ("Do", "Dc", "Dg", "DA", "DB", "DC", "W2A", "W2B",
                 "DZ0", "DZ1", "DS0", "DS1", "Dr")
                if P.sem_counts.get(s, 0) > 0])

    with nc.Block() as block:
        @block.tensor
        def _(eng):
            for f in P.th["tensor"]:
                f(eng, H)

        @block.vector
        def _(eng):
            for f in P.th["vector"]:
                f(eng, H)

        @block.scalar
        def _(eng):
            for f in P.th["scalar"]:
                f(eng, H)

        @block.gpsimd
        def _(eng):
            for f in P.th["gpsimd"]:
                f(eng, H)

        @block.sync
        def _(eng):
            for f in P.th["sync"]:
                f(eng, H)
    es.close()
    return nc


def prep_inputs(inputs):
    g = {k: np.asarray(v) for k, v in inputs.items()}
    d = {}
    d["wq0"] = pack_lhsT(np.ascontiguousarray(g["L0_attn_Wq"].T))
    d["wk0"] = pack_lhsT(np.ascontiguousarray(g["L0_attn_Wk"].T))
    d["wv0"] = pack_lhsT(np.ascontiguousarray(g["L0_attn_Wv"].T))
    d["wproj0"] = pack_lhsT(np.ascontiguousarray(g["L0_attn_Wproj"].T))
    for l in range(2):
        d[f"w1_{l}"] = pack_lhsT(np.ascontiguousarray(g[f"L{l}_ffn_w1"].T))
        d[f"w2_{l}"] = np.ascontiguousarray(
            g[f"L{l}_ffn_w2"].T.astype(np.float32)).reshape(32, 128, 1024)
        d[f"b1c_{l}"] = pack_bias_cols(g[f"L{l}_ffn_b1"])
        d[f"b2c_{l}"] = pack_bias_cols(g[f"L{l}_ffn_b2"])
    d["wq1"] = pack_lhsT(g["L1_attn_wq"])
    d["bq1"] = pack_bias_cols(g["L1_attn_bq"])
    d["wqc1"] = pack_lhsT(g["L1_attn_wqc"])
    d["bqc1"] = pack_bias_cols(g["L1_attn_bqc"])
    d["wqr1"] = pack_lhsT(g["L1_attn_wqr"])
    d["bqr1"] = pack_bias_cols(g["L1_attn_bqr"])
    d["wkv1"] = pack_lhsT(g["L1_attn_wkv"])
    d["bkv1"] = pack_bias_cols(g["L1_attn_bkv"])
    d["wkr1"] = pack_lhsT(g["L1_attn_wkr"])
    d["bkr1"] = pack_bias_cols(g["L1_attn_bkr"])
    d["wkc1"] = pack_lhsT(g["L1_attn_wkc"])
    d["bkc1"] = pack_bias_cols(g["L1_attn_bkc"])
    d["wv1"] = pack_lhsT(g["L1_attn_wv"])
    d["bv1"] = pack_bias_cols(g["L1_attn_bv"])
    d["wo1"] = pack_lhsT(g["L1_attn_wo"])
    d["bo1"] = pack_bias_cols(g["L1_attn_bo"])
    d["lamv"] = np.concatenate(
        [g["L0_attn_lq1"], g["L0_attn_lq2"], g["L0_attn_lk1"],
         g["L0_attn_lk2"]]).astype(np.float32).reshape(1, 256)
    d["identc"] = np.eye(128, dtype=np.float32)
    d["permc0"] = perm_mat(32)
    d["permc1"] = perm_mat(64)
    d["maskc"] = causal_maskbias()
    d["ropec0"] = np.stack(rope_tables_l0())
    d["ropec1"] = np.stack(rope_tables_l1())
    return d


def kernel(**inputs):
    nc = build_program()
    shared = prep_inputs(inputs)
    x = np.asarray(inputs["L0_attn_x"])
    in_maps = []
    for b in range(NB):
        m = dict(shared)
        m["x"] = np.ascontiguousarray(x[b])
        in_maps.append(m)
    res = run_bass_kernel_spmd(nc, in_maps, core_ids=list(range(NB)))
    return np.stack([np.asarray(r["y"])
                     for r in res.results]).astype(np.float32)


# version 19
# speedup vs baseline: 1.0953x; 1.0056x over previous
import math
from contextlib import ExitStack

import numpy as np

import concourse.bass as bass
import concourse.mybir as mybir
from concourse.bass_utils import run_bass_kernel_spmd

F32 = mybir.dt.float32
F32R = mybir.dt.float32r
BF16 = mybir.dt.bfloat16
F8 = mybir.dt.float8e4
I32 = mybir.dt.int32
AF = mybir.ActivationFunctionType
ALU = mybir.AluOpType
AX = mybir.AxisListType

T = 2048
NB = 8
LN2 = math.log(2.0)
INV_LN2 = 1.0 / LN2
LAMBDA_INIT = 0.8 - 0.6 * math.exp(-0.3 * 4)
RMS_EPS = float(np.finfo(np.float32).eps)
LN_EPS = 1e-5
SCHED_LEVEL = 4
N_HEADS_L0 = 8
ATTN_STAGE = 4
PROJ_ON = 1
LAMBDA_ON = 1
PREP_STAGE = 3



def pack_lhsT(W):
    I, O = W.shape
    return np.ascontiguousarray(
        W.reshape(I // 128, 128, O // 128, 128)
        .transpose(2, 1, 0, 3)
        .reshape(O // 128, 128, I).astype(np.float32))


def pack_bias_cols(b):
    n = b.shape[0] // 128
    return np.ascontiguousarray(np.asarray(b, np.float32).reshape(n, 128).T)


def rope_tables_l0():
    HD = 64
    inv_freq = 1.0 / 10000.0 ** (np.arange(0, HD, 2, dtype=np.float32) / HD)
    fr = np.outer(np.arange(T, dtype=np.float32), inv_freq)
    cos, sin = np.cos(fr).T, np.sin(fr).T
    Cf = np.empty((128, T), np.float32)
    Sf = np.empty((128, T), np.float32)
    for blk in range(4):
        Cf[blk * 32:(blk + 1) * 32] = cos
        Sf[blk * 32:(blk + 1) * 32] = sin if blk % 2 == 0 else -sin
    return Cf, Sf


def rope_tables_l1():
    DR = 128
    inv_freq = 1.0 / 10000.0 ** (np.arange(0, DR, 2, dtype=np.float32) / DR)
    fr = np.outer(np.arange(T, dtype=np.float32), inv_freq)
    cos, sin = np.cos(fr).T, np.sin(fr).T
    Cf = np.empty((128, T), np.float32)
    Sf = np.empty((128, T), np.float32)
    Cf[0:64] = cos
    Cf[64:128] = cos
    Sf[0:64] = -sin
    Sf[64:128] = sin
    return Cf, Sf


def perm_mat(n):
    P = np.zeros((128, 128), np.float32)
    for a in range(128):
        P[(a // n ^ 1) * n + (a % n), a] = 1.0
    return P


def causal_maskbias():
    m = np.zeros((128, 128), np.float32)
    m[np.tril_indices(128, -1)] = -60.0
    return m


class Prog:
    ENGINES = ("tensor", "vector", "scalar", "gpsimd", "sync")

    def __init__(self):
        self.th = {e: [] for e in self.ENGINES}
        self.sem_counts = {}
        self.hwm = {e: {} for e in self.ENGINES}
        self.default_waits = []

    def op(self, eng, fn, waits=(), inc=None, inc_step=1):
        tok = None
        if inc is not None:
            v = self.sem_counts.get(inc, 0) + inc_step
            self.sem_counts[inc] = v
            tok = (inc, v)
        rw = []
        for w in list(self.default_waits) + list(waits):
            if w is None:
                continue
            s, v = w
            if self.hwm[eng].get(s, 0) < v:
                self.hwm[eng][s] = v
                rw.append((s, v))
        step = inc_step

        def thunk(engine, H):
            for (s, v) in rw:
                engine.wait_ge(H[s], v)
            ins = fn(engine, H)
            if inc is not None:
                if ins is None:
                    engine.sem_inc(H[inc], step)
                else:
                    ins.then_inc(H[inc], step)
        self.th[eng].append(thunk)
        return tok


def build_program():
    nc = bass.Bass("TRN2", target_bir_lowering=False)
    P = Prog()

    def din(name, shape, dtype=F32):
        return nc.dram_tensor(name, list(shape), dtype,
                              kind="ExternalInput").ap()

    x_in = din("x", [T, 1024])
    y_out = nc.dram_tensor("y", [T, 1024], F32, kind="ExternalOutput").ap()

    wq0 = din("wq0", [8, 128, 1024])
    wk0 = din("wk0", [8, 128, 1024])
    wv0 = din("wv0", [8, 128, 1024])
    wproj0 = din("wproj0", [8, 128, 1024])
    wff = []
    for l in range(2):
        wff.append(dict(
            w1=din(f"w1_{l}", [32, 128, 1024]),
            w2=din(f"w2_{l}", [32, 128, 1024]),
            b1=din(f"b1c_{l}", [128, 32]),
            b2=din(f"b2c_{l}", [128, 8])))
    wq1 = din("wq1", [2, 128, 1024])
    bq1 = din("bq1", [128, 2])
    wqc1 = din("wqc1", [7, 128, 256])
    bqc1 = din("bqc1", [128, 7])
    wqr1 = din("wqr1", [1, 128, 1024])
    bqr1 = din("bqr1", [128, 1])
    wkv1 = din("wkv1", [2, 128, 1024])
    bkv1 = din("bkv1", [128, 2])
    wkr1 = din("wkr1", [1, 128, 1024])
    bkr1 = din("bkr1", [128, 1])
    wkc1 = din("wkc1", [7, 128, 256])
    bkc1 = din("bkc1", [128, 7])
    wv1 = din("wv1", [8, 128, 256])
    bv1 = din("bv1", [128, 8])
    wo1 = din("wo1", [8, 128, 1024])
    bo1 = din("bo1", [128, 8])
    lam_in = din("lamv", [1, 256])
    identc = din("identc", [128, 128])
    permc0 = din("permc0", [128, 128])
    permc1 = din("permc1", [128, 128])
    maskc = din("maskc", [128, 128])
    ropec0 = din("ropec0", [2, 128, T])
    ropec1 = din("ropec1", [2, 128, T])
    w2q_sc = [nc.dram_tensor(f"w2qsc_{l}", [32, 128, 1024], F8,
                             kind="Internal").ap() for l in range(2)]

    es = ExitStack()

    def sbt(name, shape, dtype):
        return es.enter_context(nc.sbuf_tensor(name, shape, dtype))

    x_res = sbt("x_res", [128, 16 * 1024], F32)
    xnT = sbt("xnT", [128, 8 * T], BF16)
    arA = sbt("arA", [128, 8 * T], BF16)
    aT = arA[:, 0:8 * T]
    hqv = arA.bitcast(F8)
    arB = sbt("arB", [128, 6 * T], BF16)
    cqb = arB[:, 0:2 * T]
    ckvb = arB[:, 2 * T:4 * T]
    qh = sbt("qh", [128, T], BF16)
    kh = sbt("kh", [128, T], BF16)
    VS = 136
    vtok = sbt("vtok", [128, 16 * VS], BF16)
    p0 = sbt("p0", [128, 2 * T], BF16)
    p1 = sbt("p1", [128, 2 * T], BF16)
    pcb = sbt("pcb", [128, 1024], BF16)
    NSCR = 8
    scr = arB.bitcast(F32)
    scri = arB.bitcast(I32)[:, 4096:5120]
    scrb = arB[:, 10240:11264]
    rmsr = sbt("rmsr", [128, 512], F32)
    rmsb = sbt("rmsb", [128, 512], F32)
    wbA = sbt("wbA", [128, 1024], BF16)
    wbB = sbt("wbB", [128, 1024], BF16)
    wbC = sbt("wbC", [128, 1024], BF16)
    w2qs = sbt("w2qs", [128, 2 * 1024], F8)
    ident = sbt("ident", [128, 128], BF16)
    perm0 = sbt("perm0", [128, 128], BF16)
    perm1 = sbt("perm1", [128, 128], BF16)
    maskb = sbt("maskb", [128, 128], BF16)
    ropeC = sbt("ropeC", [128, T], BF16)
    ropeS = sbt("ropeS", [128, T], BF16)
    onesq = sbt("onesq", [128, 128], BF16)
    b1cols = [sbt(f"b1col{l}", [128, 32], F32) for l in range(2)]
    b2cols = [sbt(f"b2col{l}", [128, 8], F32) for l in range(2)]
    b2qcols = [sbt(f"b2qcol{l}", [128, 8], F32) for l in range(2)]
    bias1 = sbt("bias1", [128, 12], F32)
    bias2 = sbt("bias2", [128, 12], F32)
    bvcol = sbt("bvcol", [128, 8], F32)
    bocol = sbt("bocol", [128, 8], F32)
    lamt = sbt("lamt", [1, 256], F32)
    smlb = sbt("smlb", [1, 8], BF16)
    sml = sbt("sml", [128, 128], F32)
    sml2 = sbt("sml2", [128, 32], F32)
    cst = sbt("cst", [128, 16], F32)
    stats = sbt("stats", [128, 12], F32)
    aggr = sbt("aggr", [128, 2 * 16], F32)
    rstd = sbt("rstd", [128, 16], F32)
    bcols = sbt("bcols", [128, 16], F32)

    psA = es.enter_context(nc.psum_tensor("psA", [128, 512], F32))
    psB = es.enter_context(nc.psum_tensor("psB", [128, 512], F32))
    psC = es.enter_context(nc.psum_tensor("psC", [128, 512], F32))
    psD = es.enter_context(nc.psum_tensor("psD", [128, 512], F32))
    psT = es.enter_context(nc.psum_tensor("psT", [128, 1024], BF16))
    psU = es.enter_context(nc.psum_tensor("psU", [128, 1024], BF16))
    psS = es.enter_context(nc.psum_tensor("psS", [128, 512], F32))
    psR = es.enter_context(nc.psum_tensor("psR", [128, 512], F32))
    PSAB = [psA, psB]
    PSCD = [psC, psD]
    PS4 = [psA, psB, psC, psD]

    sem_names = ["T", "V", "S", "G", "Dc", "Dg", "DA", "DB", "DC",
                 "W2A", "W2B", "DZ0", "DZ1", "DS0", "DS1",
                 "Dr", "Do"]
    H = {n: es.enter_context(nc.semaphore(n)) for n in sem_names}

    def T_(fn, waits=()):
        return P.op("tensor", fn, waits, inc="T")

    def T0_(fn, waits=()):
        return P.op("tensor", fn, waits, inc=None)

    def V_(fn, waits=()):
        return P.op("vector", fn, waits, inc="V")

    def S_(fn, waits=()):
        return P.op("scalar", fn, waits, inc="S")

    def G_(fn, waits=()):
        return P.op("gpsimd", fn, waits, inc="G")

    def DMA(eng, sem, fn, waits=()):
        return P.op(eng, fn, waits, inc=sem, inc_step=16)

    def DMAw(fn, waits=(), sem="Dc"):
        return P.op("sync", fn, waits, inc=sem, inc_step=16)

    def DMAg(fn, waits=(), sem="Dg"):
        return P.op("gpsimd", fn, waits, inc=sem, inc_step=16)

    def DMAo(fn, waits=()):
        return P.op("sync", fn, waits, inc="Do", inc_step=16)

    last_use = {}

    def ruse(*keys):
        out = []
        for k in keys:
            out.extend(last_use.get(k, []))
        return out

    def rdone(key, toks):
        if not isinstance(toks, list):
            toks = [toks]
        last_use[key] = [t for t in toks if t is not None]

    def radd(key, tok):
        if tok is not None:
            last_use.setdefault(key, []).append(tok)

    def emit_bcast2(srcs, dsts, waits):
        tm = None
        for i, s2 in enumerate(srcs):
            tm = T_(lambda e, h, s2=s2, i=i: e.matmul(
                psS[:, 0:2], onesq[0:1, 0:128], s2,
                start=(i == 0), stop=(i == len(srcs) - 1)),
                waits=list(waits) + [t_blkr] + (ruse("psS")
                                               if i == 0 else []))
        out = []
        for j, d in enumerate(dsts):
            if d is None:
                continue
            out.append(V_(lambda e, h, j=j, d=d: e.tensor_copy(
                d, psS[:, j:j + 1]), waits=[tm]))
        rdone("psS", out)
        return out[-1]

    CSTV = {}
    CSTV_FROZEN = [False]

    def cstcol(val):
        val = float(val)
        if val not in CSTV:
            assert not CSTV_FROZEN[0], f"const col {val} not preregistered"
            i = len(CSTV)
            assert i < 16, "too many const cols"
            CSTV[val] = i
            V_(lambda e, h, i=i, val=val: e.memset(cst[:, i:i + 1], val))
        return cst[:, CSTV[val]:CSTV[val] + 1]

    t_ident = DMAg(lambda e, h: e.dma_start(ident[:], identc))
    t_perm0 = DMAg(lambda e, h: e.dma_start(perm0[:], permc0))
    t_perm1 = DMAg(lambda e, h: e.dma_start(perm1[:], permc1))
    t_ropeC = DMAg(lambda e, h: e.dma_start(ropeC[:], ropec0[0]))
    t_ropeS = DMAg(lambda e, h: e.dma_start(ropeS[:], ropec0[1]))
    t_mask = DMAg(lambda e, h: e.dma_start(maskb[:], maskc))
    t_blkr = V_(lambda e, h: e.memset(onesq[:], 1.0))
    t_blkc = t_blkr
    V_(lambda e, h: e.memset(smlb[:], 0.0))
    t_lam = DMAw(lambda e, h: e.dma_start(lamt[:], lam_in))
    t_b1 = [DMAw(lambda e, h, l=l: e.dma_start(b1cols[l][:], wff[l]["b1"]))
            for l in range(2)]
    t_b2 = [DMAw(lambda e, h, l=l: e.dma_start(b2cols[l][:], wff[l]["b2"]))
            for l in range(2)]
    x3 = x_in.rearrange("(n p) c -> n p c", p=128)
    y3 = y_out.rearrange("(n p) c -> n p c", p=128)
    t_x = [DMAw(lambda e, h, i=i: e.dma_start(
        x_res[:, i * 1024:(i + 1) * 1024], x3[i])) for i in range(16)]
    t_bias_l = [
        DMAw(lambda e, h: e.dma_start(bias1[:, 0:2], bq1)),
        DMAw(lambda e, h: e.dma_start(bias1[:, 2:9], bqc1)),
        DMAw(lambda e, h: e.dma_start(bias1[:, 9:10], bqr1)),
        DMAw(lambda e, h: e.dma_start(bias2[:, 0:2], bkv1)),
        DMAw(lambda e, h: e.dma_start(bias2[:, 2:9], bkc1)),
        DMAw(lambda e, h: e.dma_start(bias2[:, 9:10], bkr1)),
        DMAw(lambda e, h: e.dma_start(bvcol[:], bv1)),
        DMAw(lambda e, h: e.dma_start(bocol[:], bo1)),
    ]
    for kb in range(16):
        V_(lambda e, h, kb=kb: e.memset(vtok[:, kb * VS + 64:kb * VS + 65],
                                        1.0))
    for _v in (0.0, 2.0, 4096.0, LN_EPS, RMS_EPS, 2.0 ** -60, -8.0,
               -math.log(8.0), math.log(1.0 - LAMBDA_INIT),
               3 * LN2, -3 * LN2, 4 * LN2, -4 * LN2):
        cstcol(_v)
    CSTV_FROZEN[0] = True
    tok_c = P.op("vector", lambda e, h: None,
                 waits=[("Dc", P.sem_counts["Dc"]),
                        ("Dg", P.sem_counts.get("Dg", 0))], inc="V")
    P.default_waits = [tok_c]
    t_ident = t_perm0 = t_perm1 = t_ropeC = t_ropeS = tok_c
    t_mask = t_blkc = t_blkr = t_lam = t_ones = tok_c
    t_b1 = [tok_c, tok_c]
    t_b2 = [tok_c, tok_c]
    t_x = [tok_c]
    t_bias = [tok_c]

    def emit_layernorm(dep):
        tA = []
        for i in range(16):
            xt = x_res[:, i * 1024:(i + 1) * 1024]
            tsa = V_(lambda e, h, xt=xt: e.bn_stats(stats[:, 0:6],
                                                    xt[:, 0:512]),
                     waits=list(dep) + ruse("aggr", "stats"))
            tsb = V_(lambda e, h, xt=xt: e.bn_stats(stats[:, 6:12],
                                                    xt[:, 512:1024]),
                     waits=[tsa])
            ta_ = V_(lambda e, h, i=i: e.bn_aggr(
                aggr[:, 2 * i:2 * i + 2], stats[:, 0:12]), waits=[tsb])
            rdone("stats", ta_)
            tA.append(ta_)
        var_view = aggr[:, 1:32:2]
        tl = S_(lambda e, h: e.activation(rstd[:], var_view, AF.Ln,
                                          bias=cstcol(LN_EPS)),
                waits=[tA[-1]])
        tl = S_(lambda e, h: e.activation(rstd[:], rstd[:], AF.Exp,
                                          scale=-0.5), waits=[tl])
        rdone("aggr", tl)
        toks = []
        for i in range(16):
            xt = x_res[:, i * 1024:(i + 1) * 1024]
            tv = V_(lambda e, h, xt=xt, i=i: e.tensor_scalar(
                pcb[:, 0:1024], xt, aggr[:, 2 * i:2 * i + 1],
                rstd[:, i:i + 1], ALU.subtract, ALU.mult),
                waits=[tl] + ruse("pcb"))
            last = None
            for c in range(8):
                pd, key = (psT, "psTg") if c % 2 == 0 \
                    else (psU, ("psU", 0))
                tt = T_(lambda e, h, c=c, pd=pd: e.transpose(
                    pd[:, 0:128], pcb[:, c * 128:(c + 1) * 128], ident[:]),
                    waits=[tv, t_ident] + ruse(key))
                tc = V_(lambda e, h, c=c, i=i, pd=pd: e.tensor_copy(
                    xnT[:, c * T + i * 128: c * T + i * 128 + 128],
                    pd[:, 0:128]), waits=[tt])
                rdone(key, tc)
                last = tc
                toks.append(tc)
            rdone("pcb", last)
        return toks

    qpar = [0]
    chain_done = {0: None, 1: None}

    def emit_quant(z_ap, w, out_ap, kind, bias_col=None, relu=False,
                   skip_stage1=False, waits=()):
        man, mine = (3, -6) if kind == "e4m3" else (10, -14)
        p = qpar[0]
        qpar[0] ^= 1

        def fs(j):
            return scr[:, (4 * p + j) * 512:(4 * p + j) * 512 + w]
        iA = scri[:, p * 512:p * 512 + w]
        SCB = scrb[:, p * 512:p * 512 + w]
        wl = list(waits)
        if chain_done[p] is not None:
            wl = wl + [chain_done[p]]
        if skip_stage1:
            A = z_ap
            z_done = None
            pre = wl
        else:
            bcol = bias_col if bias_col is not None else cstcol(0.0)
            t1 = S_(lambda e, h: e.activation(
                fs(0), z_ap, AF.Relu if relu else AF.Identity, bias=bcol),
                waits=wl)
            A = fs(0)
            z_done = t1
            pre = [t1]
        if relu:
            lin = A
        else:
            tab = S_(lambda e, h: e.activation(fs(1), A, AF.Abs), waits=pre)
            lin = fs(1)
            pre = [tab]
        tl = S_(lambda e, h: e.activation(fs(2), lin, AF.Ln,
                                          scale=float(2.0 ** 46),
                                          bias=cstcol(2.0)), waits=pre)
        ta = S_(lambda e, h: e.activation(iA, fs(2), AF.Identity,
                                          scale=INV_LN2), waits=[tl])
        tb = V_(lambda e, h: e.scalar_tensor_tensor(
            fs(1), iA, LN2, fs(2), ALU.mult, ALU.is_gt), waits=[ta])
        tn = V_(lambda e, h: e.scalar_tensor_tensor(
            fs(3), fs(1), float(46 - mine), iA, ALU.add, ALU.subtract),
            waits=[tb])
        tcl = V_(lambda e, h: e.tensor_scalar(
            fs(1), fs(3), float(-2 * mine), None, ALU.min), waits=[tn])
        tsc = S_(lambda e, h: e.activation(
            SCB, fs(1), AF.Exp, scale=-LN2,
            bias=cstcol((-mine - man) * LN2)), waits=[tcl])
        trs = S_(lambda e, h: e.activation(
            fs(3), fs(1), AF.Exp, scale=LN2,
            bias=cstcol((mine + man) * LN2)), waits=[tcl])
        tt_ = V_(lambda e, h: e.tensor_tensor(fs(2), A, fs(3), ALU.mult),
                 waits=[trs])
        z_done2 = tt_ if skip_stage1 else z_done
        ta2 = S_(lambda e, h: e.activation(iA, fs(2), AF.Identity,
                                           bias=cstcol(4096.0)),
                 waits=[tt_])
        tc2 = V_(lambda e, h: e.scalar_tensor_tensor(
            fs(1), iA, -4095.5, fs(2), ALU.add, ALU.is_le), waits=[ta2])
        tr = V_(lambda e, h: e.tensor_tensor(fs(3), iA, fs(1), ALU.add),
                waits=[tc2])
        tq = V_(lambda e, h: e.scalar_tensor_tensor(
            out_ap, fs(3), -4096.0, SCB, ALU.add, ALU.mult),
            waits=[tr, tsc])
        chain_done[p] = tq
        return tq, z_done2

    W2R = {}

    def w2q_gen(l):
        toks = []
        for hc in range(32):
            done = []
            for half in range(2):
                pn = qpar[0]
                zin = scr[:, (4 * pn) * 512:(4 * pn) * 512 + 512]
                wl = [chain_done[pn]] if chain_done[pn] is not None else []
                tw = DMAw(lambda e, h, hc=hc, half=half, zin=zin, l=l:
                          e.dma_start(zin, wff[l]["w2"][hc][:,
                                      half * 512:half * 512 + 512]),
                          waits=wl, sem="DZ0" if pn == 0 else "DZ1")
                tq, _ = emit_quant(
                    zin, 512,
                    w2qs[:, half * 512:half * 512 + 512],
                    "e4m3", skip_stage1=True,
                    waits=[tw] + ruse("w2qs"))
                done.append(tq)
            ts_ = DMAw(lambda e, h, hc=hc, l=l: e.dma_start(
                w2q_sc[l][hc], w2qs[:, 0:1024]), waits=done,
                sem="DS0" if l == 0 else "DS1")
            rdone("w2qs", ts_)
            toks.append(ts_)
            W2R[l] = ts_
            yield

    def emit_lambda():
        t1 = V_(lambda e, h: e.tensor_tensor(
            sml[0:1, 0:128], lamt[0:1, 0:128], lamt[0:1, 128:256],
            ALU.mult), waits=[t_lam])
        t2a = V_(lambda e, h: e.reduce_sum(sml2[0:1, 0:1], sml[0:1, 0:64],
                                           AX.X), waits=[t1])
        t2b = V_(lambda e, h: e.reduce_sum(sml2[0:1, 1:2],
                                           sml[0:1, 64:128], AX.X),
                 waits=[t1])
        t3 = S_(lambda e, h: e.activation(sml2[0:1, 2:4], sml2[0:1, 0:2],
                                          AF.Exp), waits=[t2a, t2b])
        t6 = V_(lambda e, h: e.scalar_tensor_tensor(
            sml2[0:1, 4:5], sml2[0:1, 3:4], -LAMBDA_INIT,
            sml2[0:1, 2:3], ALU.add, ALU.subtract), waits=[t3])
        th = V_(lambda e, h: e.tensor_copy(smlb[0:1, 0:1],
                                           sml2[0:1, 4:5]), waits=[t6])
        tl_ = V_(lambda e, h: e.scalar_tensor_tensor(
            smlb[0:1, 2:3], smlb[0:1, 0:1], -1.0, sml2[0:1, 4:5],
            ALU.mult, ALU.add), waits=[th])
        t7 = emit_bcast2([smlb[0:1, 0:2], smlb[0:1, 2:4]],
                         [sml2[:, 5:6], None], [tl_])
        return t7

    NLAM = sml2[:, 5:6]

    def l0_prep_side(side, wbuf, tw, rawbuf, outbuf, xn_dep):
        out_toks = []
        for tc in range(4):
            sl = PSAB[tc % 2][:, 0:512]
            pk = ("pab", tc % 2)
            mt = None
            for ic in range(8):
                mt = (T_ if ic == 7 else T0_)(
                    lambda e, h, ic=ic, tc=tc, sl=sl, wbuf=wbuf:
                        e.matmul(sl, wbuf[:, ic * 128:(ic + 1) * 128],
                                 xnT[:, ic * T + tc * 512:
                                     ic * T + tc * 512 + 512],
                                 start=(ic == 0), stop=(ic == 7)),
                        waits=(([tw] + xn_dep + ruse(pk))
                               if ic == 0 else []))
            if PREP_STAGE <= 0:
                rdone(pk, mt)
                out_toks.append(mt)
                continue
            t_raw = V_(lambda e, h, tc=tc, sl=sl, rawbuf=rawbuf:
                       e.tensor_copy(rawbuf[:, tc * 512:(tc + 1) * 512],
                                     sl), waits=[mt])
            sq = outbuf[:, tc * 512:(tc + 1) * 512]
            t_sq = S_(lambda e, h, sl=sl, sq=sq: e.activation(
                sq, sl, AF.Square), waits=[mt, t_raw])
            rdone(pk, [t_raw, t_sq])
            if PREP_STAGE <= 1:
                out_toks.append(t_sq)
                rdone("pcb", t_raw)
                continue
            t_r0 = T_(lambda e, h, sq=sq: e.matmul(
                psS[:, 0:512], onesq[0:64, 0:128], sq[0:64, :],
                start=True, stop=True),
                waits=[t_sq, t_blkc] + ruse("psS"))
            t_r1 = T_(lambda e, h, sq=sq: e.matmul(
                psR[:, 0:512], onesq[64:128, 0:128], sq[64:128, :],
                start=True, stop=True),
                waits=[t_sq] + ruse("psR"))
            bias_r = cstcol(-math.log(8.0)) if side == 0 else cstcol(0.0)
            tu0 = S_(lambda e, h: e.activation(
                rmsr[0:64, 0:512], psS[0:64, 0:512], AF.Ln,
                scale=1.0 / 64.0, bias=cstcol(RMS_EPS)[0:64, :]),
                waits=[t_r0] + ruse("rmsr"))
            rdone("psS", tu0)
            tu1 = S_(lambda e, h: e.activation(
                rmsr[64:128, 0:512], psR[64:128, 0:512], AF.Ln,
                scale=1.0 / 64.0, bias=cstcol(RMS_EPS)[64:128, :]),
                waits=[t_r1])
            rdone("psR", tu1)
            t_bc = S_(lambda e, h, bias_r=bias_r: e.activation(
                rmsb[:, 0:512], rmsr[:, 0:512], AF.Exp, scale=-0.5,
                bias=bias_r), waits=[tu0, tu1] + ruse("rmsb"))
            rdone("rmsr", t_bc)
            if PREP_STAGE <= 2:
                out_toks.append(t_bc)
                continue
            t_sw = T_(lambda e, h, tc=tc, rawbuf=rawbuf: e.matmul(
                psR[:, 0:512], perm0[:],
                rawbuf[:, tc * 512:(tc + 1) * 512], start=True, stop=True),
                waits=[t_raw, t_perm0] + ruse("psR"))
            tm1 = V_(lambda e, h, tc=tc, rawbuf=rawbuf: e.tensor_tensor(
                pcb[:, 0:512], rawbuf[:, tc * 512:(tc + 1) * 512],
                ropeC[:, tc * 512:(tc + 1) * 512], ALU.mult),
                waits=[t_raw, t_ropeC] + ruse("pcb"))
            ts2 = V_(lambda e, h, tc=tc: e.tensor_tensor(
                pcb[:, 512:1024], psR[:, 0:512],
                ropeS[:, tc * 512:(tc + 1) * 512], ALU.mult),
                waits=[t_sw, t_ropeS])
            rdone("psR", ts2)
            tsum = V_(lambda e, h: e.tensor_tensor(
                pcb[:, 0:512], pcb[:, 0:512], pcb[:, 512:1024], ALU.add),
                waits=[tm1, ts2])
            tq = V_(lambda e, h, tc=tc, outbuf=outbuf: e.tensor_tensor(
                outbuf[:, tc * 512:(tc + 1) * 512], pcb[:, 0:512],
                rmsb[:, 0:512], ALU.mult), waits=[tsum, t_bc])
            rdone("pcb", tq)
            rdone("rmsb", tq)
            radd("ropes", tq)
            out_toks.append(tq)
        return out_toks

    def l0_head(hd, nlam_tok, xn_dep, out_toks, pump=None):
        tw_q = DMAg(lambda e, h, hd=hd: e.dma_start(wbA[:], wq0[hd]),
                    waits=ruse("wbA"), sem="DA")
        tw_k = DMAg(lambda e, h, hd=hd: e.dma_start(wbB[:], wk0[hd]),
                    waits=ruse("wbB"), sem="DB")
        tw_v = DMAg(lambda e, h, hd=hd: e.dma_start(wbC[:], wv0[hd]),
                    waits=ruse("wbC"), sem="DC")
        tqs = l0_prep_side(0, wbA, tw_q, p0, qh, xn_dep)
        rdone("wbA", tqs[-1])
        tks = l0_prep_side(1, wbB, tw_k, p1, kh, xn_dep)
        rdone("wbB", tks[-1])
        if ATTN_STAGE <= 1:
            out_toks.extend(tqs + tks)
            return out_toks
        vt = []
        for qt in range(16):
            sl = PSCD[qt % 2][:, 0:128]
            pk = ("pcd", qt % 2)
            mt = None
            for ic in range(8):
                mt = (T_ if ic == 7 else T0_)(
                    lambda e, h, ic=ic, qt=qt, sl=sl: e.matmul(
                    sl, xnT[:, ic * T + qt * 128:ic * T + qt * 128 + 128],
                    wbC[:, ic * 128:(ic + 1) * 128],
                    start=(ic == 0), stop=(ic == 7)),
                    waits=(([tw_v] + xn_dep + ruse(pk))
                           if ic == 0 else []))
            tv = V_(lambda e, h, qt=qt, sl=sl: e.tensor_copy(
                vtok[:, qt * VS:qt * VS + 64], sl[:, 0:64]), waits=[mt])
            tv = V_(lambda e, h, qt=qt, sl=sl: e.tensor_copy(
                vtok[:, qt * VS + 65:qt * VS + 129], sl[:, 64:128]),
                waits=[tv])
            rdone(pk, tv)
            vt.append(tv)
        rdone("wbC", vt[-1])
        if ATTN_STAGE <= 2:
            out_toks.extend(vt)
            return out_toks

        def stageA(qt):
            par = qt % 2
            exp_by = []
            for side, pbuf in ((0, p0[:, par * T:par * T + T]),
                               (1, p1[:, par * T:par * T + T])):
                side_toks = []
                ng = (qt + 4) // 4
                for g in range(ng):
                    jn = min(4, qt + 1 - 4 * g)
                    sl = PSAB[g % 2][:, 0:512]
                    pk = ("pab", g % 2)
                    mlast = None
                    for j in range(jn):
                        kb = 4 * g + j
                        mlast = (T_ if j == jn - 1 else T0_)(
                            lambda e, h, j=j, kb=kb, side=side, qt=qt,
                            sl=sl: e.matmul(
                            sl[:, j * 128:(j + 1) * 128],
                            kh[64 * side:64 * side + 64,
                               kb * 128:(kb + 1) * 128],
                            qh[64 * side:64 * side + 64,
                               qt * 128:(qt + 1) * 128],
                            start=True, stop=True),
                            waits=(([tqs[-1], tks[-1]] + ruse(pk))
                                   if j == 0 else []))
                    wl = [mlast, t_mask]
                    if g == ng - 1:
                        jd = qt - 4 * g
                        tmask = V_(lambda e, h, jd=jd, sl=sl:
                                   e.tensor_tensor(
                            sl[:, jd * 128:(jd + 1) * 128],
                            sl[:, jd * 128:(jd + 1) * 128],
                            maskb[:], ALU.add), waits=wl)
                        wl = [tmask]
                    texp = S_(lambda e, h, jn=jn, g=g, pbuf=pbuf,
                              sl=sl: e.activation(
                        pbuf[:, g * 512:g * 512 + jn * 128],
                        sl[:, 0:jn * 128], AF.Exp, bias=cstcol(-8.0)),
                        waits=wl + ruse(("pex", side, par)))
                    rdone(pk, texp)
                    side_toks.append(texp)
                exp_by.append(side_toks)
            return exp_by

        def stageB(qt, exp_by):
            par = qt % 2
            pvps = PSCD[par]
            pvk = ("pcd", par)
            pvt = None
            for side, base in ((0, 0), (1, 256)):
                pbuf = (p0 if side == 0 else p1)[:, par * T:par * T + T]
                for kb in range(qt + 1):
                    pvt = (T_ if kb == qt else T0_)(
                        lambda e, h, kb=kb, qt=qt, base=base, pbuf=pbuf:
                        e.matmul(
                        pvps[:, base:base + 129],
                        pbuf[:, kb * 128:(kb + 1) * 128],
                        vtok[:, kb * VS:kb * VS + 129],
                        start=(kb == 0), stop=(kb == qt)),
                        waits=[exp_by[side][kb // 4], vt[kb]] +
                        (ruse(pvk) if kb == 0 and side == 0 else []))
                rdone(("pex", side, par), pvt)
            c8 = 8 + 16 * par
            rz0 = V_(lambda e, h, c8=c8: e.reciprocal(
                sml2[:, c8:c8 + 1], pvps[:, 64:65]), waits=[pvt])
            rz1 = V_(lambda e, h, c8=c8: e.reciprocal(
                sml2[:, c8 + 1:c8 + 2], pvps[:, 320:321]), waits=[pvt])
            t1a = V_(lambda e, h, c8=c8: e.tensor_scalar(
                pcb[:, 128:192], pvps[:, 256:320], sml2[:, c8 + 1:c8 + 2],
                NLAM, ALU.mult, ALU.mult),
                waits=[rz1, nlam_tok] + ruse("pcb"))
            t1b = V_(lambda e, h, c8=c8: e.tensor_scalar(
                pcb[:, 192:256], pvps[:, 321:385], sml2[:, c8 + 1:c8 + 2],
                NLAM, ALU.mult, ALU.mult), waits=[rz1])
            t2a = V_(lambda e, h, c8=c8: e.scalar_tensor_tensor(
                pcb[:, 0:64], pvps[:, 0:64], sml2[:, c8:c8 + 1],
                pcb[:, 128:192], ALU.mult, ALU.add),
                waits=[rz0, t1a])
            t2b = V_(lambda e, h, c8=c8: e.scalar_tensor_tensor(
                pcb[:, 64:128], pvps[:, 65:129], sml2[:, c8:c8 + 1],
                pcb[:, 192:256], ALU.mult, ALU.add), waits=[t2a, t1b])
            rdone(pvk, t2b)
            tst = V_(lambda e, h: e.bn_stats(
                stats[:, 0:6], pcb[:, 0:128]),
                waits=[t2b] + ruse("aggr0", "stats"))
            tag = V_(lambda e, h: e.bn_aggr(aggr[:, 0:2], stats[:, 0:6]),
                     waits=[tst])
            rdone("stats", tag)
            tu = S_(lambda e, h: e.activation(sml2[:, 11:12], aggr[:, 1:2],
                                              AF.Ln, bias=cstcol(LN_EPS)),
                    waits=[tag] + ruse("lnsml"))
            tr_ = S_(lambda e, h: e.activation(
                sml2[:, 12:13], sml2[:, 11:12], AF.Exp, scale=-0.5,
                bias=cstcol(math.log(1.0 - LAMBDA_INIT))), waits=[tu])
            tln = V_(lambda e, h: e.tensor_scalar(
                pcb[:, 256:384], pcb[:, 0:128], aggr[:, 0:1],
                sml2[:, 12:13], ALU.subtract, ALU.mult),
                waits=[tr_])
            rdone("aggr0", tln)
            rdone("lnsml", tln)
            ttr = T_(lambda e, h: e.transpose(psU[:, 0:128],
                                              pcb[:, 256:384],
                                              ident[:]),
                     waits=[tln] + ruse(("psU", 0)))
            tev = V_(lambda e, h, hd=hd, qt=qt: e.tensor_copy(
                aT[:, hd * T + qt * 128:hd * T + qt * 128 + 128],
                psU[:, 0:128]), waits=[ttr])
            rdone(("psU", 0), tev)
            rdone("pcb", [tln, ttr, tev])
            out_toks.append(tev)

        prev = None
        for qt in range(16):
            eb = stageA(qt)
            if prev is not None:
                stageB(qt - 1, prev)
                if pump is not None:
                    pump(1)
            prev = eb
        stageB(15, prev)
        if pump is not None:
            pump(1)
        return out_toks

    def emit_out_proj(wdram, src, src_toks, bias_colst, n_oc, xadd_toks):
        for oc in range(n_oc):
            par = oc % 2
            wb = wbA if par == 0 else wbB
            tw = DMAg(lambda e, h, oc=oc, wb=wb: e.dma_start(
                wb[:], wdram[oc]),
                waits=ruse("wbA" if par == 0 else "wbB"),
                sem="DA" if par == 0 else "DB")
            mlast = None
            for tc in range(4):
                qd = tc % 2
                sl = PSAB[qd][:, 0:512]
                pk = ("pab", qd)
                mt = None
                for ic in range(8):
                    mt = (T_ if ic == 7 else T0_)(
                        lambda e, h, ic=ic, tc=tc, sl=sl, wb=wb:
                            e.matmul(sl,
                                     wb[:, ic * 128:(ic + 1) * 128],
                                     src[:, ic * T + tc * 512:
                                         ic * T + tc * 512 + 512],
                                     start=(ic == 0), stop=(ic == 7)),
                            waits=(([tw] + src_toks + ruse(pk))
                                   if ic == 0 else []))
                mlast = mt
                if bias_colst is not None:
                    tev = V_(lambda e, h, sl=sl, oc=oc: e.tensor_scalar(
                        pcb[:, 0:512], sl, bias_colst[:, oc:oc + 1], None,
                        ALU.add), waits=[mt] + ruse("pcb"))
                else:
                    tev = V_(lambda e, h, sl=sl: e.tensor_copy(
                        pcb[:, 0:512], sl), waits=[mt] + ruse("pcb"))
                rdone(pk, tev)
                lastadd = None
                for j in range(4):
                    ttr = T_(lambda e, h, j=j: e.transpose(
                        psU[:, 128:256], pcb[:, j * 128:(j + 1) * 128],
                        ident[:]), waits=[tev] + ruse(("psU", 1)))
                    tqi = tc * 4 + j
                    tadd = V_(lambda e, h, tqi=tqi, oc=oc:
                              e.scalar_tensor_tensor(
                        x_res[:, tqi * 1024 + oc * 128:
                              tqi * 1024 + oc * 128 + 128],
                        psU[:, 128:256], 0.0,
                        x_res[:, tqi * 1024 + oc * 128:
                              tqi * 1024 + oc * 128 + 128],
                        ALU.add, ALU.add), waits=[ttr])
                    rdone(("psU", 1), tadd)
                    lastadd = tadd
                    xadd_toks.append(tadd)
                rdone("pcb", lastadd)
            rdone("wbA" if par == 0 else "wbB", mlast)
        return xadd_toks

    def emit_ffn(l, xn_dep, xadd_toks, w2ready):
        tb2q, _ = emit_quant(b2cols[l][:, 0:8], 8, b2qcols[l][:, 0:8],
                             "fp16", skip_stage1=True, waits=[t_b2[l]])
        for blk in range(2):
            hq_toks = {}
            for hc in range(32):
                par = hc % 2
                wb = wbA if par == 0 else wbB
                tw = DMAg(lambda e, h, hc=hc, wb=wb, l=l: e.dma_start(
                    wb[:], wff[l]["w1"][hc]),
                    waits=ruse("wbA" if par == 0 else "wbB"),
                    sem="DA" if par == 0 else "DB")
                hts = []
                mm_l = None
                for half in range(2):
                    tok0 = blk * 1024 + half * 512
                    qd = (2 * hc + half) % 2
                    sl = PSAB[qd][:, 0:512]
                    pk = ("pab", qd)
                    mt = None
                    for ic in range(8):
                        mt = (T_ if ic == 7 else T0_)(
                            lambda e, h, ic=ic, sl=sl, wb=wb,
                                tok0=tok0: e.matmul(
                            sl, wb[:, ic * 128:(ic + 1) * 128],
                            xnT[:, ic * T + tok0:ic * T + tok0 + 512],
                            start=(ic == 0), stop=(ic == 7)),
                            waits=(([tw] + xn_dep + ruse(pk))
                                   if ic == 0 else []))
                    mm_l = mt
                    out_sl = hqv[:, hc * 1024 + half * 512:
                                 hc * 1024 + half * 512 + 512]
                    tqd, zd = emit_quant(sl, 512, out_sl, "e4m3",
                                         bias_col=b1cols[l][:, hc:hc + 1],
                                         relu=True,
                                         waits=[mt, t_b1[l]] +
                                         ruse(("hqw", hc)))
                    rdone(pk, zd)
                    hts.append(tqd)
                rdone("wbA" if par == 0 else "wbB", mm_l)
                hq_toks[hc] = hts
                rdone(("hqw", hc), hts)
            for tc2 in range(2):
                tok_in_blk = tc2 * 512
                for og in range(2):
                    mlast = [None] * 4
                    for hc in range(32):
                        par = hc % 2
                        tws = DMAw(lambda e, h, hc=hc, par=par, l=l:
                                   e.dma_start(
                            w2qs[:, par * 1024:par * 1024 + 1024],
                            w2q_sc[l][hc]),
                            waits=[w2ready] + ruse(("w2qsb", par),
                                                    "w2qs"),
                            sem="W2A" if par == 0 else "W2B")
                        for oc4 in range(4):
                            oc = og * 4 + oc4
                            mlast[oc4] = (T_ if (hc == 31 or oc4 == 3) else T0_)(
                                lambda e, h, oc4=oc4, oc=oc, par=par,
                                hc=hc, tok_in_blk=tok_in_blk: e.matmul(
                                    PS4[oc4][:, 0:512],
                                    w2qs[:, par * 1024 + oc * 128:
                                         par * 1024 + oc * 128 + 128],
                                    hqv[:, hc * 1024 + tok_in_blk:
                                        hc * 1024 + tok_in_blk + 512],
                                    start=(hc == 0), stop=(hc == 31)),
                                waits=([tws] + hq_toks[hc] +
                                       (ruse(("pab", 0), ("pab", 1),
                                             ("pcd", 0), ("pcd", 1))
                                        if hc == 0 else [])))
                        rdone(("w2qsb", par), mlast[3])
                        radd("w2qs", mlast[3])
                    for oc4 in range(4):
                        oc = og * 4 + oc4
                        sl = PS4[oc4][:, 0:512]
                        pk = [("pab", 0), ("pab", 1), ("pcd", 0),
                              ("pcd", 1)][oc4]
                        tqd, zd = emit_quant(
                            sl, 512, pcb[:, 0:512], "fp16",
                            bias_col=b2qcols[l][:, oc:oc + 1], relu=False,
                            waits=[mlast[oc4], tb2q] + ruse("pcb"))
                        rdone(pk, zd)
                        ttr = None
                        for j in range(4):
                            ttr = (T_ if j == 3 else T0_)(
                                lambda e, h, j=j: e.transpose(
                                psU[:, 128 + j * 128:256 + j * 128],
                                pcb[:, j * 128:(j + 1) * 128], ident[:]),
                                waits=(([tqd] + ruse(("psU", 1)))
                                       if j == 0 else []))
                        tq0 = blk * 8 + tc2 * 4
                        xrv = x_res[:, 0:16384].rearrange(
                            "p (a b) -> p a b", b=1024)[
                            :, tq0:tq0 + 4, oc * 128:oc * 128 + 128]
                        puv = psU[:, 128:640].rearrange(
                            "p (a b) -> p a b", b=128)
                        tadd = V_(lambda e, h, xrv=xrv, puv=puv:
                                  e.scalar_tensor_tensor(
                            xrv, puv, 0.0, xrv,
                            ALU.add, ALU.add), waits=[ttr])
                        rdone(("psU", 1), tadd)
                        xadd_toks.append(tadd)
                        rdone("pcb", tadd)
        return xadd_toks

    def pair_qk(ph, wdr, bcol, src, src_toks, out):
        tw = DMAg(lambda e, h, ph=ph, wdr=wdr: e.dma_start(
            wbA[:, 0:256], wdr[ph]), waits=ruse("wbA"), sem="DA")
        lt = None
        for tc in range(4):
            qd = tc % 2
            sl = PSAB[qd][:, 0:512]
            pk = ("pab", qd)
            mt = None
            for cc in range(2):
                mt = (T_ if cc == 1 else T0_)(
                    lambda e, h, cc=cc, tc=tc, sl=sl, src=src: e.matmul(
                    sl, wbA[:, cc * 128:(cc + 1) * 128],
                    src[:, cc * T + tc * 512:cc * T + tc * 512 + 512],
                    start=(cc == 0), stop=(cc == 1)),
                    waits=(([tw] + src_toks + ruse(pk))
                           if cc == 0 else []))
            te = S_(lambda e, h, sl=sl, tc=tc, out=out, bcol=bcol:
                    e.activation(out[:, tc * 512:(tc + 1) * 512], sl,
                                 AF.Gelu, bias=bcol), waits=[mt])
            rdone(pk, te)
            lt = te
        rdone("wbA", lt)
        return lt

    def rope_pair(wdr, bcol, out, xn_dep, trc, trs):
        tw = DMAg(lambda e, h, wdr=wdr: e.dma_start(wbA[:], wdr[0]),
                  waits=ruse("wbA"), sem="DA")
        lt = None
        for tc in range(4):
            qd = tc % 2
            sl = PSAB[qd][:, 0:512]
            pk = ("pab", qd)
            mt = None
            for ic in range(8):
                mt = (T_ if ic == 7 else T0_)(
                    lambda e, h, ic=ic, tc=tc, sl=sl: e.matmul(
                    sl, wbA[:, ic * 128:(ic + 1) * 128],
                    xnT[:, ic * T + tc * 512:ic * T + tc * 512 + 512],
                    start=(ic == 0), stop=(ic == 7)),
                    waits=(([tw] + xn_dep + ruse(pk))
                           if ic == 0 else []))
            traw = S_(lambda e, h, sl=sl, tc=tc, bcol=bcol: e.activation(
                p1[:, tc * 512:(tc + 1) * 512], sl, AF.Identity,
                bias=bcol), waits=[mt] + ruse("p1r"))
            rdone(pk, traw)
            tsw = T_(lambda e, h, tc=tc: e.matmul(
                psS[:, 0:512], perm1[:],
                p1[:, tc * 512:(tc + 1) * 512], start=True, stop=True),
                waits=[traw, t_perm1] + ruse("psS"))
            tm1 = V_(lambda e, h, tc=tc: e.tensor_tensor(
                pcb[:, 0:512], p1[:, tc * 512:(tc + 1) * 512],
                ropeC[:, tc * 512:(tc + 1) * 512], ALU.mult),
                waits=[traw, trc] + ruse("pcb"))
            ts2 = V_(lambda e, h, tc=tc: e.tensor_tensor(
                pcb[:, 512:1024], psS[:, 0:512],
                ropeS[:, tc * 512:(tc + 1) * 512], ALU.mult),
                waits=[tsw, trs])
            rdone("psS", ts2)
            tsum = V_(lambda e, h: e.tensor_tensor(
                pcb[:, 0:512], pcb[:, 0:512], pcb[:, 512:1024], ALU.add),
                waits=[tm1, ts2])
            tg = S_(lambda e, h, tc=tc, out=out: e.activation(
                out[:, tc * 512:(tc + 1) * 512], pcb[:, 0:512], AF.Gelu),
                waits=[tsum])
            rdone("pcb", tg)
            radd("p1r", tg)
            lt = tg
        rdone("wbA", lt)
        return lt

    def emit_bounds(ph):
        first = V_(lambda e, h: e.memset(sml[0:1, 16:32], 0.0),
                   waits=ruse("bnd"))
        tmx_toks = []
        for side, buf in ((0, qh), (1, kh)):
            for tc in range(4):
                sq = p1[:, tc * 512:(tc + 1) * 512]
                tsq = S_(lambda e, h, buf=buf, tc=tc, sq=sq:
                         e.activation(
                    sq, buf[:, tc * 512:(tc + 1) * 512], AF.Square),
                    waits=ruse("bsq", "p1r") + [first])
                for h2 in range(2):
                    ps_ = psS if h2 == 0 else psR
                    pk_ = "psS" if h2 == 0 else "psR"
                    tred = T_(lambda e, h, sq=sq, h2=h2, ps_=ps_:
                              e.matmul(
                        ps_[:, 0:512],
                        onesq[64 * h2:64 * h2 + 64, 0:128],
                        sq[64 * h2:64 * h2 + 64, :],
                        start=True, stop=True),
                        waits=[tsq, t_blkc] + ruse(pk_))
                    cix = 16 + side * 8 + h2 * 4 + tc
                    tmx = V_(lambda e, h, cix=cix, ps_=ps_: e.reduce_max(
                        sml[0:1, cix:cix + 1],
                        ps_[0:1, 0:512], AX.X), waits=[tred])
                    rdone(pk_, tmx)
                    tmx_toks.append(tmx)
                rdone("bsq", tmx_toks[-2:])
        tms = []
        for side in range(2):
            for h2 in range(2):
                c0 = 16 + side * 8 + h2 * 4
                tms.append(V_(lambda e, h, c0=c0, side=side, h2=h2:
                              e.reduce_max(
                    sml2[0:1, 16 + side * 2 + h2:17 + side * 2 + h2],
                    sml[0:1, c0:c0 + 4], AX.X), waits=tmx_toks))
        tp = V_(lambda e, h: e.tensor_tensor(
            sml2[0:1, 20:22], sml2[0:1, 16:18], sml2[0:1, 18:20],
            ALU.mult), waits=tms)
        tp = S_(lambda e, h: e.activation(sml2[0:1, 22:24],
                                          sml2[0:1, 20:22], AF.Ln,
                                          bias=cstcol(2.0 ** -60)[0:1, :]),
                waits=[tp])
        tp = S_(lambda e, h: e.activation(
            sml2[0:1, 24:26], sml2[0:1, 22:24], AF.Exp, scale=0.5,
            bias=cstcol(-math.log(8.0))[0:1, :]), waits=[tp])
        tp = V_(lambda e, h: e.tensor_scalar(
            smlb[0:1, 4:6], sml2[0:1, 24:26], 40.0, -1.0, ALU.min,
            ALU.mult), waits=[tp])
        t2 = emit_bcast2([smlb[0:1, 4:6]],
                         [bcols[:, 2 * ph:2 * ph + 1],
                          bcols[:, 2 * ph + 1:2 * ph + 2]], [tp])
        rdone("bnd", [t2])
        return t2

    def l1_attention(xn_dep, aT_toks):
        DMAg(lambda e, h: e.dma_start(ropeC[:], ropec1[0]),
             waits=ruse("ropes"), sem="Dr")
        trs = DMAg(lambda e, h: e.dma_start(ropeS[:], ropec1[1]),
                   sem="Dr")
        trc = trs
        cq_toks = []
        ckv_toks = []
        for (wdr, bc_sl, out, toks) in ((wq1, bias1[:, 0:2], cqb, cq_toks),
                                        (wkv1, bias2[:, 0:2], ckvb,
                                         ckv_toks)):
            for oc in range(2):
                tw = DMAg(lambda e, h, oc=oc, wdr=wdr: e.dma_start(
                    wbA[:], wdr[oc]), waits=ruse("wbA"), sem="DA")
                lt = None
                for tc in range(4):
                    qd = tc % 2
                    sl = PSAB[qd][:, 0:512]
                    pk = ("pab", qd)
                    mt = None
                    for ic in range(8):
                        mt = (T_ if ic == 7 else T0_)(
                            lambda e, h, ic=ic, tc=tc, sl=sl: e.matmul(
                            sl, wbA[:, ic * 128:(ic + 1) * 128],
                            xnT[:, ic * T + tc * 512:
                                ic * T + tc * 512 + 512],
                            start=(ic == 0), stop=(ic == 7)),
                            waits=(([tw] + xn_dep + t_bias +
                                    ruse(pk))
                                   if ic == 0 else []))
                    te = S_(lambda e, h, sl=sl, oc=oc, tc=tc, out=out,
                            bc_sl=bc_sl: e.activation(
                        out[:, oc * T + tc * 512:oc * T + tc * 512 + 512],
                        sl, AF.Gelu, bias=bc_sl[:, oc:oc + 1]),
                        waits=[mt])
                    rdone(pk, te)
                    toks.append(te)
                    lt = te
                rdone("wbA", lt)
        for ph in range(8):
            if ph < 7:
                tq_ = pair_qk(ph, wqc1, bias1[:, 2 + ph:3 + ph], cqb,
                              cq_toks, qh)
                tk_ = pair_qk(ph, wkc1, bias2[:, 2 + ph:3 + ph], ckvb,
                              ckv_toks, kh)
            else:
                tq_ = rope_pair(wqr1, bias1[:, 9:10], qh, xn_dep, trc, trs)
                tk_ = rope_pair(wkr1, bias2[:, 9:10], kh, xn_dep, trc, trs)
            twv = DMAg(lambda e, h, ph=ph: e.dma_start(wbC[:, 0:256],
                                                       wv1[ph]),
                       waits=ruse("wbC"), sem="DC")
            vt = []
            for qt in range(16):
                sl = PSCD[qt % 2][:, 0:128]
                pk = ("pcd", qt % 2)
                mt = None
                for cc in range(2):
                    mt = (T_ if cc == 1 else T0_)(
                        lambda e, h, cc=cc, qt=qt, sl=sl: e.matmul(
                        sl, ckvb[:, cc * T + qt * 128:
                                 cc * T + qt * 128 + 128],
                        wbC[:, cc * 128:(cc + 1) * 128],
                        start=(cc == 0), stop=(cc == 1)),
                        waits=(([twv] + ckv_toks + ruse(pk))
                               if cc == 0 else []))
                tv = V_(lambda e, h, qt=qt, sl=sl: e.tensor_copy(
                    vtok[:, qt * VS:qt * VS + 64], sl[:, 0:64]),
                    waits=[mt])
                tv = V_(lambda e, h, qt=qt, sl=sl: e.tensor_copy(
                    vtok[:, qt * VS + 65:qt * VS + 129], sl[:, 64:128]),
                    waits=[tv])
                rdone(pk, tv)
                vt.append(tv)
            rdone("wbC", vt[-1])
            bnd_tok = emit_bounds(ph)

            def stageA1(qt, h2):
                par = (2 * qt + h2) % 2
                po = p0[:, par * T:par * T + T]
                hh = 2 * ph + h2
                side_toks = []
                ng = (qt + 4) // 4
                for g in range(ng):
                    jn = min(4, qt + 1 - 4 * g)
                    sl = PSAB[g % 2][:, 0:512]
                    pk = ("pab", g % 2)
                    mlast = None
                    for j in range(jn):
                        kb = 4 * g + j
                        mlast = (T_ if j == jn - 1 else T0_)(
                            lambda e, h, j=j, kb=kb, h2=h2, qt=qt,
                            sl=sl: e.matmul(
                            sl[:, j * 128:(j + 1) * 128],
                            kh[64 * h2:64 * h2 + 64,
                               kb * 128:(kb + 1) * 128],
                            qh[64 * h2:64 * h2 + 64,
                               qt * 128:(qt + 1) * 128],
                            start=True, stop=True),
                            waits=(([tq_, tk_] + ruse(pk))
                                   if j == 0 else []))
                    wl = [mlast, t_mask]
                    if g == ng - 1:
                        jd = qt - 4 * g
                        tmask = V_(lambda e, h, jd=jd, sl=sl:
                                   e.tensor_tensor(
                            sl[:, jd * 128:(jd + 1) * 128],
                            sl[:, jd * 128:(jd + 1) * 128],
                            maskb[:], ALU.add), waits=wl)
                        wl = [tmask]
                    texp = S_(lambda e, h, jn=jn, g=g, hh=hh, sl=sl,
                              po=po: e.activation(
                        po[:, g * 512:g * 512 + jn * 128],
                        sl[:, 0:jn * 128], AF.Exp, scale=0.125,
                        bias=bcols[:, hh:hh + 1]),
                        waits=wl + [bnd_tok] + ruse(("pex", par)))
                    rdone(pk, texp)
                    side_toks.append(texp)
                return side_toks

            def stageB1(qt, h2, chunks):
                par = (2 * qt + h2) % 2
                po = p0[:, par * T:par * T + T]
                pvps = PSCD[qt % 2]
                pvk = ("pcd", qt % 2)
                base = 256 * h2
                pvt = None
                for kb in range(qt + 1):
                    pvt = (T_ if kb == qt else T0_)(
                        lambda e, h, kb=kb, qt=qt, h2=h2, base=base,
                        po=po: e.matmul(
                        pvps[:, base:base + 65],
                        po[:, kb * 128:(kb + 1) * 128],
                        vtok[:, kb * VS + 64 * h2:kb * VS + 64 * h2 + 65],
                        start=(kb == 0), stop=(kb == qt)),
                        waits=[chunks[kb // 4], vt[kb]] +
                        (ruse(pvk) if kb == 0 and h2 == 0 else []))
                rdone(("pex", par), pvt)
                return pvt

            def tail1(qt, pv0, pv1):
                pvps = PSCD[qt % 2]
                pvk = ("pcd", qt % 2)
                c8 = 8 + 2 * (qt % 2)
                rz0 = V_(lambda e, h, c8=c8, pvps=pvps: e.reciprocal(
                    sml2[:, c8:c8 + 1], pvps[:, 64:65]),
                    waits=[pv0, pv1])
                rz1 = V_(lambda e, h, c8=c8, pvps=pvps: e.reciprocal(
                    sml2[:, c8 + 1:c8 + 2], pvps[:, 256:257]),
                    waits=[pv1])
                tno0 = V_(lambda e, h, c8=c8, pvps=pvps: e.tensor_scalar(
                    pcb[:, 0:64], pvps[:, 0:64],
                    sml2[:, c8:c8 + 1], None, ALU.mult),
                    waits=[rz0] + ruse("pcb"))
                tno1 = V_(lambda e, h, c8=c8, pvps=pvps: e.tensor_scalar(
                    pcb[:, 64:128], pvps[:, 257:321],
                    sml2[:, c8 + 1:c8 + 2], None, ALU.mult),
                    waits=[rz1])
                rdone(pvk, tno1)
                ttr = T_(lambda e, h: e.transpose(
                    psU[:, 0:128], pcb[:, 0:128], ident[:]),
                    waits=[tno0, tno1] + ruse(("psU", 0)))
                rdone("pcb", ttr)
                tev = V_(lambda e, h, ph=ph, qt=qt: e.tensor_scalar(
                    aT[:, ph * T + qt * 128:ph * T + qt * 128 + 128],
                    psU[:, 0:128], bvcol[:, ph:ph + 1], None,
                    ALU.add), waits=[ttr])
                rdone(("psU", 0), tev)
                aT_toks.append(tev)

            units = [(qt, h2) for qt in range(16) for h2 in range(2)]
            pend = {}
            prevu = None
            for u in units:
                chunks = stageA1(*u)
                pend[u] = [chunks, None]
                if prevu is not None:
                    pend[prevu][1] = stageB1(prevu[0], prevu[1],
                                             pend[prevu][0])
                    if prevu[1] == 1:
                        q0 = prevu[0]
                        tail1(q0, pend[(q0, 0)][1], pend[(q0, 1)][1])
                        del pend[(q0, 0)], pend[(q0, 1)]
                prevu = u
            pend[prevu][1] = stageB1(prevu[0], prevu[1], pend[prevu][0])
            tail1(15, pend[(15, 0)][1], pend[(15, 1)][1])
        return aT_toks

    lvl = SCHED_LEVEL
    final_dep = []

    def gen_w2():
        for g in (w2q_gen(0), w2q_gen(1)):
            for _ in g:
                yield

    def make_pump(gen):
        state = [gen is None]

        def pump(n=1):
            if state[0]:
                return
            try:
                for _ in range(n):
                    next(gen)
            except StopIteration:
                state[0] = True

        def drain():
            if state[0]:
                return
            for _ in gen:
                pass
            state[0] = True
        return pump, drain

    gen0 = gen_w2() if lvl >= 2 else None
    pump0, drain0 = make_pump(gen0)
    pump0(6)
    ln0 = emit_layernorm(t_x + [t_ident])
    xn_dep0 = [ln0[-1]]
    final_dep = [ln0[-1]]
    if lvl >= 1:
        nlam_tok = emit_lambda() if LAMBDA_ON else None
        l0_toks = []
        for hd in range(N_HEADS_L0):
            l0_head(hd, nlam_tok, xn_dep0, l0_toks, pump=pump0)
        drain0()
        final_dep = l0_toks[-1:] or final_dep
        if PROJ_ON:
            xadd0 = []
            emit_out_proj(wproj0, aT, l0_toks[-1:], None, 8, xadd0)
            final_dep = [xadd0[-1]]
    if lvl >= 2:
        emit_ffn(0, xn_dep0, xadd0, W2R[0])
        final_dep = [xadd0[-1]]
    if lvl >= 3:
        ln1 = emit_layernorm([xadd0[-1]])
        xn_dep1 = [ln1[-1]]
        aT_toks1 = []
        l1_attention(xn_dep1, aT_toks1)
        xadd1 = []
        emit_out_proj(wo1, aT, [aT_toks1[-1]], bocol, 8, xadd1)
        final_dep = [xadd1[-1]]
    if lvl >= 4:
        emit_ffn(1, xn_dep1, xadd1, W2R[1])
        final_dep = [xadd1[-1]]

    for i in range(16):
        DMAo(lambda e, h, i=i: e.dma_start(
            y3[i], x_res[:, i * 1024:(i + 1) * 1024]), waits=final_dep)
    P.op("sync", lambda e, h: None,
         waits=[(s, P.sem_counts[s]) for s in
                ("Do", "Dc", "Dg", "DA", "DB", "DC", "W2A", "W2B",
                 "DZ0", "DZ1", "DS0", "DS1", "Dr")
                if P.sem_counts.get(s, 0) > 0])

    with nc.Block() as block:
        @block.tensor
        def _(eng):
            for f in P.th["tensor"]:
                f(eng, H)

        @block.vector
        def _(eng):
            for f in P.th["vector"]:
                f(eng, H)

        @block.scalar
        def _(eng):
            for f in P.th["scalar"]:
                f(eng, H)

        @block.gpsimd
        def _(eng):
            for f in P.th["gpsimd"]:
                f(eng, H)

        @block.sync
        def _(eng):
            for f in P.th["sync"]:
                f(eng, H)
    es.close()
    return nc


def prep_inputs(inputs):
    g = {k: np.asarray(v) for k, v in inputs.items()}
    d = {}
    d["wq0"] = pack_lhsT(np.ascontiguousarray(g["L0_attn_Wq"].T))
    d["wk0"] = pack_lhsT(np.ascontiguousarray(g["L0_attn_Wk"].T))
    d["wv0"] = pack_lhsT(np.ascontiguousarray(g["L0_attn_Wv"].T))
    d["wproj0"] = pack_lhsT(np.ascontiguousarray(g["L0_attn_Wproj"].T))
    for l in range(2):
        d[f"w1_{l}"] = pack_lhsT(np.ascontiguousarray(g[f"L{l}_ffn_w1"].T))
        d[f"w2_{l}"] = np.ascontiguousarray(
            g[f"L{l}_ffn_w2"].T.astype(np.float32)).reshape(32, 128, 1024)
        d[f"b1c_{l}"] = pack_bias_cols(g[f"L{l}_ffn_b1"])
        d[f"b2c_{l}"] = pack_bias_cols(g[f"L{l}_ffn_b2"])
    d["wq1"] = pack_lhsT(g["L1_attn_wq"])
    d["bq1"] = pack_bias_cols(g["L1_attn_bq"])
    d["wqc1"] = pack_lhsT(g["L1_attn_wqc"])
    d["bqc1"] = pack_bias_cols(g["L1_attn_bqc"])
    d["wqr1"] = pack_lhsT(g["L1_attn_wqr"])
    d["bqr1"] = pack_bias_cols(g["L1_attn_bqr"])
    d["wkv1"] = pack_lhsT(g["L1_attn_wkv"])
    d["bkv1"] = pack_bias_cols(g["L1_attn_bkv"])
    d["wkr1"] = pack_lhsT(g["L1_attn_wkr"])
    d["bkr1"] = pack_bias_cols(g["L1_attn_bkr"])
    d["wkc1"] = pack_lhsT(g["L1_attn_wkc"])
    d["bkc1"] = pack_bias_cols(g["L1_attn_bkc"])
    d["wv1"] = pack_lhsT(g["L1_attn_wv"])
    d["bv1"] = pack_bias_cols(g["L1_attn_bv"])
    d["wo1"] = pack_lhsT(g["L1_attn_wo"])
    d["bo1"] = pack_bias_cols(g["L1_attn_bo"])
    d["lamv"] = np.concatenate(
        [g["L0_attn_lq1"], g["L0_attn_lq2"], g["L0_attn_lk1"],
         g["L0_attn_lk2"]]).astype(np.float32).reshape(1, 256)
    d["identc"] = np.eye(128, dtype=np.float32)
    d["permc0"] = perm_mat(32)
    d["permc1"] = perm_mat(64)
    d["maskc"] = causal_maskbias()
    d["ropec0"] = np.stack(rope_tables_l0())
    d["ropec1"] = np.stack(rope_tables_l1())
    return d


def kernel(**inputs):
    nc = build_program()
    shared = prep_inputs(inputs)
    x = np.asarray(inputs["L0_attn_x"])
    in_maps = []
    for b in range(NB):
        m = dict(shared)
        m["x"] = np.ascontiguousarray(x[b])
        in_maps.append(m)
    res = run_bass_kernel_spmd(nc, in_maps, core_ids=list(range(NB)))
    return np.stack([np.asarray(r["y"])
                     for r in res.results]).astype(np.float32)


# version 20
# speedup vs baseline: 1.1052x; 1.0091x over previous
import math
from contextlib import ExitStack

import numpy as np

import concourse.bass as bass
import concourse.mybir as mybir
from concourse.bass_utils import run_bass_kernel_spmd

F32 = mybir.dt.float32
F32R = mybir.dt.float32r
BF16 = mybir.dt.bfloat16
F8 = mybir.dt.float8e4
I32 = mybir.dt.int32
AF = mybir.ActivationFunctionType
ALU = mybir.AluOpType
AX = mybir.AxisListType

T = 2048
NB = 8
LN2 = math.log(2.0)
INV_LN2 = 1.0 / LN2
LAMBDA_INIT = 0.8 - 0.6 * math.exp(-0.3 * 4)
RMS_EPS = float(np.finfo(np.float32).eps)
LN_EPS = 1e-5
SCHED_LEVEL = 4
N_HEADS_L0 = 8
ATTN_STAGE = 4
PROJ_ON = 1
LAMBDA_ON = 1
PREP_STAGE = 3



def pack_lhsT(W):
    I, O = W.shape
    return np.ascontiguousarray(
        W.reshape(I // 128, 128, O // 128, 128)
        .transpose(2, 1, 0, 3)
        .reshape(O // 128, 128, I).astype(np.float32))


def pack_bias_cols(b):
    n = b.shape[0] // 128
    return np.ascontiguousarray(np.asarray(b, np.float32).reshape(n, 128).T)


def rope_tables_l0():
    HD = 64
    inv_freq = 1.0 / 10000.0 ** (np.arange(0, HD, 2, dtype=np.float32) / HD)
    fr = np.outer(np.arange(T, dtype=np.float32), inv_freq)
    cos, sin = np.cos(fr).T, np.sin(fr).T
    Cf = np.empty((128, T), np.float32)
    Sf = np.empty((128, T), np.float32)
    for blk in range(4):
        Cf[blk * 32:(blk + 1) * 32] = cos
        Sf[blk * 32:(blk + 1) * 32] = sin if blk % 2 == 0 else -sin
    return Cf, Sf


def rope_tables_l1():
    DR = 128
    inv_freq = 1.0 / 10000.0 ** (np.arange(0, DR, 2, dtype=np.float32) / DR)
    fr = np.outer(np.arange(T, dtype=np.float32), inv_freq)
    cos, sin = np.cos(fr).T, np.sin(fr).T
    Cf = np.empty((128, T), np.float32)
    Sf = np.empty((128, T), np.float32)
    Cf[0:64] = cos
    Cf[64:128] = cos
    Sf[0:64] = -sin
    Sf[64:128] = sin
    return Cf, Sf


def perm_mat(n):
    P = np.zeros((128, 128), np.float32)
    for a in range(128):
        P[(a // n ^ 1) * n + (a % n), a] = 1.0
    return P


def causal_maskbias():
    m = np.zeros((128, 128), np.float32)
    m[np.tril_indices(128, -1)] = -60.0
    return m


class Prog:
    ENGINES = ("tensor", "vector", "scalar", "gpsimd", "sync")

    def __init__(self):
        self.th = {e: [] for e in self.ENGINES}
        self.sem_counts = {}
        self.hwm = {e: {} for e in self.ENGINES}
        self.default_waits = []

    def op(self, eng, fn, waits=(), inc=None, inc_step=1):
        tok = None
        if inc is not None:
            v = self.sem_counts.get(inc, 0) + inc_step
            self.sem_counts[inc] = v
            tok = (inc, v)
        rw = []
        for w in list(self.default_waits) + list(waits):
            if w is None:
                continue
            s, v = w
            if self.hwm[eng].get(s, 0) < v:
                self.hwm[eng][s] = v
                rw.append((s, v))
        step = inc_step

        def thunk(engine, H):
            for (s, v) in rw:
                engine.wait_ge(H[s], v)
            ins = fn(engine, H)
            if inc is not None:
                if ins is None:
                    engine.sem_inc(H[inc], step)
                else:
                    ins.then_inc(H[inc], step)
        self.th[eng].append(thunk)
        return tok


def build_program():
    nc = bass.Bass("TRN2", target_bir_lowering=False)
    P = Prog()

    def din(name, shape, dtype=F32):
        return nc.dram_tensor(name, list(shape), dtype,
                              kind="ExternalInput").ap()

    x_in = din("x", [T, 1024])
    y_out = nc.dram_tensor("y", [T, 1024], F32, kind="ExternalOutput").ap()

    wq0 = din("wq0", [8, 128, 1024])
    wk0 = din("wk0", [8, 128, 1024])
    wv0 = din("wv0", [8, 128, 1024])
    wproj0 = din("wproj0", [8, 128, 1024])
    wff = []
    for l in range(2):
        wff.append(dict(
            w1=din(f"w1_{l}", [32, 128, 1024]),
            w2=din(f"w2_{l}", [32, 128, 1024]),
            b1=din(f"b1c_{l}", [128, 32]),
            b2=din(f"b2c_{l}", [128, 8])))
    wq1 = din("wq1", [2, 128, 1024])
    bq1 = din("bq1", [128, 2])
    wqc1 = din("wqc1", [7, 128, 256])
    bqc1 = din("bqc1", [128, 7])
    wqr1 = din("wqr1", [1, 128, 1024])
    bqr1 = din("bqr1", [128, 1])
    wkv1 = din("wkv1", [2, 128, 1024])
    bkv1 = din("bkv1", [128, 2])
    wkr1 = din("wkr1", [1, 128, 1024])
    bkr1 = din("bkr1", [128, 1])
    wkc1 = din("wkc1", [7, 128, 256])
    bkc1 = din("bkc1", [128, 7])
    wv1 = din("wv1", [8, 128, 256])
    bv1 = din("bv1", [128, 8])
    wo1 = din("wo1", [8, 128, 1024])
    bo1 = din("bo1", [128, 8])
    lam_in = din("lamv", [1, 256])
    identc = din("identc", [128, 128])
    permc0 = din("permc0", [128, 128])
    permc1 = din("permc1", [128, 128])
    maskc = din("maskc", [128, 128])
    ropec0 = din("ropec0", [2, 128, T])
    ropec1 = din("ropec1", [2, 128, T])
    w2q_sc = [nc.dram_tensor(f"w2qsc_{l}", [32, 128, 1024], F8,
                             kind="Internal").ap() for l in range(2)]

    es = ExitStack()

    def sbt(name, shape, dtype):
        return es.enter_context(nc.sbuf_tensor(name, shape, dtype))

    x_res = sbt("x_res", [128, 16 * 1024], F32)
    xnT = sbt("xnT", [128, 8 * T], BF16)
    arA = sbt("arA", [128, 8 * T], BF16)
    aT = arA[:, 0:8 * T]
    hqv = arA.bitcast(F8)
    arB = sbt("arB", [128, 6 * T], BF16)
    cqb = arB[:, 0:2 * T]
    ckvb = arB[:, 2 * T:4 * T]
    qh = sbt("qh", [128, T], BF16)
    kh = sbt("kh", [128, T], BF16)
    VS = 136
    vtok = sbt("vtok", [128, 16 * VS], BF16)
    p0 = sbt("p0", [128, 2 * T], BF16)
    p1 = sbt("p1", [128, 2 * T], BF16)
    pcb = sbt("pcb", [128, 1024], BF16)
    NSCR = 8
    scr = arB.bitcast(F32)
    scri = arB.bitcast(I32)[:, 4096:5120]
    scrb = arB[:, 10240:11264]
    rmsr = sbt("rmsr", [128, 512], F32)
    rmsb = sbt("rmsb", [128, 512], F32)
    wbA = sbt("wbA", [128, 1024], BF16)
    wbB = sbt("wbB", [128, 1024], BF16)
    wbC = sbt("wbC", [128, 1024], BF16)
    w2qs = sbt("w2qs", [128, 2 * 1024], F8)
    ident = sbt("ident", [128, 128], BF16)
    perm0 = sbt("perm0", [128, 128], BF16)
    perm1 = sbt("perm1", [128, 128], BF16)
    maskb = sbt("maskb", [128, 128], BF16)
    ropeC = sbt("ropeC", [128, T], BF16)
    ropeS = sbt("ropeS", [128, T], BF16)
    onesq = sbt("onesq", [128, 128], BF16)
    b1cols = [sbt(f"b1col{l}", [128, 32], F32) for l in range(2)]
    b2cols = [sbt(f"b2col{l}", [128, 8], F32) for l in range(2)]
    b2qcols = [sbt(f"b2qcol{l}", [128, 8], F32) for l in range(2)]
    bias1 = sbt("bias1", [128, 12], F32)
    bias2 = sbt("bias2", [128, 12], F32)
    bvcol = sbt("bvcol", [128, 8], F32)
    bocol = sbt("bocol", [128, 8], F32)
    lamt = sbt("lamt", [1, 256], F32)
    smlb = sbt("smlb", [1, 8], BF16)
    sml = sbt("sml", [128, 128], F32)
    sml2 = sbt("sml2", [128, 32], F32)
    cst = sbt("cst", [128, 16], F32)
    stats = sbt("stats", [128, 12], F32)
    aggr = sbt("aggr", [128, 2 * 16], F32)
    rstd = sbt("rstd", [128, 16], F32)
    bcols = sbt("bcols", [128, 16], F32)

    psA = es.enter_context(nc.psum_tensor("psA", [128, 512], F32))
    psB = es.enter_context(nc.psum_tensor("psB", [128, 512], F32))
    psC = es.enter_context(nc.psum_tensor("psC", [128, 512], F32))
    psD = es.enter_context(nc.psum_tensor("psD", [128, 512], F32))
    psT = es.enter_context(nc.psum_tensor("psT", [128, 1024], BF16))
    psU = es.enter_context(nc.psum_tensor("psU", [128, 1024], BF16))
    psS = es.enter_context(nc.psum_tensor("psS", [128, 512], F32))
    psR = es.enter_context(nc.psum_tensor("psR", [128, 512], F32))
    PSAB = [psA, psB]
    PSCD = [psC, psD]
    PS4 = [psA, psB, psC, psD]

    sem_names = ["T", "V", "S", "G", "Dc", "Dg", "DA", "DB", "DC",
                 "W2A", "W2B", "DZ0", "DZ1", "DS0", "DS1",
                 "Dr", "Do"]
    H = {n: es.enter_context(nc.semaphore(n)) for n in sem_names}

    def T_(fn, waits=()):
        return P.op("tensor", fn, waits, inc="T")

    def T0_(fn, waits=()):
        return P.op("tensor", fn, waits, inc=None)

    def V_(fn, waits=()):
        return P.op("vector", fn, waits, inc="V")

    def S_(fn, waits=()):
        return P.op("scalar", fn, waits, inc="S")

    def G_(fn, waits=()):
        return P.op("gpsimd", fn, waits, inc="G")

    def DMA(eng, sem, fn, waits=()):
        return P.op(eng, fn, waits, inc=sem, inc_step=16)

    def DMAw(fn, waits=(), sem="Dc"):
        return P.op("sync", fn, waits, inc=sem, inc_step=16)

    def DMAg(fn, waits=(), sem="Dg"):
        return P.op("gpsimd", fn, waits, inc=sem, inc_step=16)

    def DMAo(fn, waits=()):
        return P.op("sync", fn, waits, inc="Do", inc_step=16)

    last_use = {}

    def ruse(*keys):
        out = []
        for k in keys:
            out.extend(last_use.get(k, []))
        return out

    def rdone(key, toks):
        if not isinstance(toks, list):
            toks = [toks]
        last_use[key] = [t for t in toks if t is not None]

    def radd(key, tok):
        if tok is not None:
            last_use.setdefault(key, []).append(tok)

    def emit_bcast2(srcs, dsts, waits):
        tm = None
        for i, s2 in enumerate(srcs):
            tm = T_(lambda e, h, s2=s2, i=i: e.matmul(
                psS[:, 0:2], onesq[0:1, 0:128], s2,
                start=(i == 0), stop=(i == len(srcs) - 1)),
                waits=list(waits) + [t_blkr] + (ruse("psS")
                                               if i == 0 else []))
        out = []
        for j, d in enumerate(dsts):
            if d is None:
                continue
            out.append(V_(lambda e, h, j=j, d=d: e.tensor_copy(
                d, psS[:, j:j + 1]), waits=[tm]))
        rdone("psS", out)
        return out[-1]

    CSTV = {}
    CSTV_FROZEN = [False]

    def cstcol(val):
        val = float(val)
        if val not in CSTV:
            assert not CSTV_FROZEN[0], f"const col {val} not preregistered"
            i = len(CSTV)
            assert i < 16, "too many const cols"
            CSTV[val] = i
            V_(lambda e, h, i=i, val=val: e.memset(cst[:, i:i + 1], val))
        return cst[:, CSTV[val]:CSTV[val] + 1]

    t_ident = DMAg(lambda e, h: e.dma_start(ident[:], identc))
    t_perm0 = DMAg(lambda e, h: e.dma_start(perm0[:], permc0))
    t_perm1 = DMAg(lambda e, h: e.dma_start(perm1[:], permc1))
    t_ropeC = DMAg(lambda e, h: e.dma_start(ropeC[:], ropec0[0]))
    t_ropeS = DMAg(lambda e, h: e.dma_start(ropeS[:], ropec0[1]))
    t_mask = DMAg(lambda e, h: e.dma_start(maskb[:], maskc))
    t_blkr = V_(lambda e, h: e.memset(onesq[:], 1.0))
    t_blkc = t_blkr
    V_(lambda e, h: e.memset(smlb[:], 0.0))
    t_lam = DMAw(lambda e, h: e.dma_start(lamt[:], lam_in))
    t_b1 = [DMAw(lambda e, h, l=l: e.dma_start(b1cols[l][:], wff[l]["b1"]))
            for l in range(2)]
    t_b2 = [DMAw(lambda e, h, l=l: e.dma_start(b2cols[l][:], wff[l]["b2"]))
            for l in range(2)]
    x3 = x_in.rearrange("(n p) c -> n p c", p=128)
    y3 = y_out.rearrange("(n p) c -> n p c", p=128)
    t_x = [DMAw(lambda e, h, i=i: e.dma_start(
        x_res[:, i * 1024:(i + 1) * 1024], x3[i])) for i in range(16)]
    t_bias_l = [
        DMAw(lambda e, h: e.dma_start(bias1[:, 0:2], bq1)),
        DMAw(lambda e, h: e.dma_start(bias1[:, 2:9], bqc1)),
        DMAw(lambda e, h: e.dma_start(bias1[:, 9:10], bqr1)),
        DMAw(lambda e, h: e.dma_start(bias2[:, 0:2], bkv1)),
        DMAw(lambda e, h: e.dma_start(bias2[:, 2:9], bkc1)),
        DMAw(lambda e, h: e.dma_start(bias2[:, 9:10], bkr1)),
        DMAw(lambda e, h: e.dma_start(bvcol[:], bv1)),
        DMAw(lambda e, h: e.dma_start(bocol[:], bo1)),
    ]
    for kb in range(16):
        V_(lambda e, h, kb=kb: e.memset(vtok[:, kb * VS + 64:kb * VS + 65],
                                        1.0))
    for _v in (0.0, 2.0, 4096.0, LN_EPS, RMS_EPS, 2.0 ** -60, -8.0,
               -math.log(8.0), math.log(1.0 - LAMBDA_INIT),
               3 * LN2, -3 * LN2, 4 * LN2, -4 * LN2):
        cstcol(_v)
    CSTV_FROZEN[0] = True
    tok_c = P.op("vector", lambda e, h: None,
                 waits=[("Dc", P.sem_counts["Dc"]),
                        ("Dg", P.sem_counts.get("Dg", 0))], inc="V")
    P.default_waits = [tok_c]
    t_ident = t_perm0 = t_perm1 = t_ropeC = t_ropeS = tok_c
    t_mask = t_blkc = t_blkr = t_lam = t_ones = tok_c
    t_b1 = [tok_c, tok_c]
    t_b2 = [tok_c, tok_c]
    t_x = [tok_c]
    t_bias = [tok_c]

    def emit_layernorm(dep):
        tA = []
        for i in range(16):
            xt = x_res[:, i * 1024:(i + 1) * 1024]
            tsa = V_(lambda e, h, xt=xt: e.bn_stats(stats[:, 0:6],
                                                    xt[:, 0:512]),
                     waits=list(dep) + ruse("aggr", "stats"))
            tsb = V_(lambda e, h, xt=xt: e.bn_stats(stats[:, 6:12],
                                                    xt[:, 512:1024]),
                     waits=[tsa])
            ta_ = V_(lambda e, h, i=i: e.bn_aggr(
                aggr[:, 2 * i:2 * i + 2], stats[:, 0:12]), waits=[tsb])
            rdone("stats", ta_)
            tA.append(ta_)
        var_view = aggr[:, 1:32:2]
        tl = S_(lambda e, h: e.activation(rstd[:], var_view, AF.Ln,
                                          bias=cstcol(LN_EPS)),
                waits=[tA[-1]])
        tl = S_(lambda e, h: e.activation(rstd[:], rstd[:], AF.Exp,
                                          scale=-0.5), waits=[tl])
        rdone("aggr", tl)
        toks = []
        for i in range(16):
            xt = x_res[:, i * 1024:(i + 1) * 1024]
            tv = V_(lambda e, h, xt=xt, i=i: e.tensor_scalar(
                pcb[:, 0:1024], xt, aggr[:, 2 * i:2 * i + 1],
                rstd[:, i:i + 1], ALU.subtract, ALU.mult),
                waits=[tl] + ruse("pcb"))
            last = None
            for c in range(8):
                pd, key = (psT, "psTg") if c % 2 == 0 \
                    else (psU, ("psU", 0))
                tt = T_(lambda e, h, c=c, pd=pd: e.transpose(
                    pd[:, 0:128], pcb[:, c * 128:(c + 1) * 128], ident[:]),
                    waits=[tv, t_ident] + ruse(key))
                tc = V_(lambda e, h, c=c, i=i, pd=pd: e.tensor_copy(
                    xnT[:, c * T + i * 128: c * T + i * 128 + 128],
                    pd[:, 0:128]), waits=[tt])
                rdone(key, tc)
                last = tc
                toks.append(tc)
            rdone("pcb", last)
        return toks

    qpar = [0]
    chain_done = {0: None, 1: None}

    def emit_quant(z_ap, w, out_ap, kind, bias_col=None, relu=False,
                   skip_stage1=False, waits=()):
        man, mine = (3, -6) if kind == "e4m3" else (10, -14)
        p = qpar[0]
        qpar[0] ^= 1

        def fs(j):
            return scr[:, (4 * p + j) * 512:(4 * p + j) * 512 + w]
        iA = scri[:, p * 512:p * 512 + w]
        SCB = scrb[:, p * 512:p * 512 + w]
        wl = list(waits)
        if chain_done[p] is not None:
            wl = wl + [chain_done[p]]
        if skip_stage1:
            A = z_ap
            z_done = None
            pre = wl
        else:
            bcol = bias_col if bias_col is not None else cstcol(0.0)
            t1 = S_(lambda e, h: e.activation(
                fs(0), z_ap, AF.Relu if relu else AF.Identity, bias=bcol),
                waits=wl)
            A = fs(0)
            z_done = t1
            pre = [t1]
        if relu:
            lin = A
        else:
            tab = S_(lambda e, h: e.activation(fs(1), A, AF.Abs), waits=pre)
            lin = fs(1)
            pre = [tab]
        tl = S_(lambda e, h: e.activation(fs(2), lin, AF.Ln,
                                          scale=float(2.0 ** 46),
                                          bias=cstcol(2.0)), waits=pre)
        ta = S_(lambda e, h: e.activation(iA, fs(2), AF.Identity,
                                          scale=INV_LN2), waits=[tl])
        tb = V_(lambda e, h: e.scalar_tensor_tensor(
            fs(1), iA, LN2, fs(2), ALU.mult, ALU.is_gt), waits=[ta])
        tn = V_(lambda e, h: e.scalar_tensor_tensor(
            fs(3), fs(1), float(46 - mine), iA, ALU.add, ALU.subtract),
            waits=[tb])
        tcl = V_(lambda e, h: e.tensor_scalar(
            fs(1), fs(3), float(-2 * mine), None, ALU.min), waits=[tn])
        tsc = S_(lambda e, h: e.activation(
            SCB, fs(1), AF.Exp, scale=-LN2,
            bias=cstcol((-mine - man) * LN2)), waits=[tcl])
        trs = S_(lambda e, h: e.activation(
            fs(3), fs(1), AF.Exp, scale=LN2,
            bias=cstcol((mine + man) * LN2)), waits=[tcl])
        tt_ = V_(lambda e, h: e.tensor_tensor(fs(2), A, fs(3), ALU.mult),
                 waits=[trs])
        z_done2 = tt_ if skip_stage1 else z_done
        ta2 = S_(lambda e, h: e.activation(iA, fs(2), AF.Identity,
                                           bias=cstcol(4096.0)),
                 waits=[tt_])
        tc2 = V_(lambda e, h: e.scalar_tensor_tensor(
            fs(1), iA, -4095.5, fs(2), ALU.add, ALU.is_le), waits=[ta2])
        tr = V_(lambda e, h: e.tensor_tensor(fs(3), iA, fs(1), ALU.add),
                waits=[tc2])
        tq = V_(lambda e, h: e.scalar_tensor_tensor(
            out_ap, fs(3), -4096.0, SCB, ALU.add, ALU.mult),
            waits=[tr, tsc])
        chain_done[p] = tq
        return tq, z_done2

    W2R = {}

    def w2q_gen(l):
        toks = []
        for hc in range(32):
            done = []
            for half in range(2):
                pn = qpar[0]
                zin = scr[:, (4 * pn) * 512:(4 * pn) * 512 + 512]
                wl = [chain_done[pn]] if chain_done[pn] is not None else []
                tw = DMAw(lambda e, h, hc=hc, half=half, zin=zin, l=l:
                          e.dma_start(zin, wff[l]["w2"][hc][:,
                                      half * 512:half * 512 + 512]),
                          waits=wl, sem="DZ0" if pn == 0 else "DZ1")
                tq, _ = emit_quant(
                    zin, 512,
                    w2qs[:, half * 512:half * 512 + 512],
                    "e4m3", skip_stage1=True,
                    waits=[tw] + ruse("w2qs"))
                done.append(tq)
            ts_ = DMAw(lambda e, h, hc=hc, l=l: e.dma_start(
                w2q_sc[l][hc], w2qs[:, 0:1024]), waits=done,
                sem="DS0" if l == 0 else "DS1")
            rdone("w2qs", ts_)
            toks.append(ts_)
            W2R[l] = ts_
            yield

    def emit_lambda():
        t1 = V_(lambda e, h: e.tensor_tensor(
            sml[0:1, 0:128], lamt[0:1, 0:128], lamt[0:1, 128:256],
            ALU.mult), waits=[t_lam])
        t2a = V_(lambda e, h: e.reduce_sum(sml2[0:1, 0:1], sml[0:1, 0:64],
                                           AX.X), waits=[t1])
        t2b = V_(lambda e, h: e.reduce_sum(sml2[0:1, 1:2],
                                           sml[0:1, 64:128], AX.X),
                 waits=[t1])
        t3 = S_(lambda e, h: e.activation(sml2[0:1, 2:4], sml2[0:1, 0:2],
                                          AF.Exp), waits=[t2a, t2b])
        t6 = V_(lambda e, h: e.scalar_tensor_tensor(
            sml2[0:1, 4:5], sml2[0:1, 3:4], -LAMBDA_INIT,
            sml2[0:1, 2:3], ALU.add, ALU.subtract), waits=[t3])
        th = V_(lambda e, h: e.tensor_copy(smlb[0:1, 0:1],
                                           sml2[0:1, 4:5]), waits=[t6])
        tl_ = V_(lambda e, h: e.scalar_tensor_tensor(
            smlb[0:1, 2:3], smlb[0:1, 0:1], -1.0, sml2[0:1, 4:5],
            ALU.mult, ALU.add), waits=[th])
        t7 = emit_bcast2([smlb[0:1, 0:2], smlb[0:1, 2:4]],
                         [sml2[:, 5:6], None], [tl_])
        return t7

    NLAM = sml2[:, 5:6]

    def l0_prep_side(side, wbuf, tw, rawbuf, outbuf, xn_dep):
        out_toks = []
        for tc in range(4):
            sl = PSAB[tc % 2][:, 0:512]
            pk = ("pab", tc % 2)
            mt = None
            for ic in range(8):
                mt = (T_ if ic == 7 else T0_)(
                    lambda e, h, ic=ic, tc=tc, sl=sl, wbuf=wbuf:
                        e.matmul(sl, wbuf[:, ic * 128:(ic + 1) * 128],
                                 xnT[:, ic * T + tc * 512:
                                     ic * T + tc * 512 + 512],
                                 start=(ic == 0), stop=(ic == 7)),
                        waits=(([tw] + xn_dep + ruse(pk))
                               if ic == 0 else []))
            if PREP_STAGE <= 0:
                rdone(pk, mt)
                out_toks.append(mt)
                continue
            t_raw = V_(lambda e, h, tc=tc, sl=sl, rawbuf=rawbuf:
                       e.tensor_copy(rawbuf[:, tc * 512:(tc + 1) * 512],
                                     sl), waits=[mt])
            sq = outbuf[:, tc * 512:(tc + 1) * 512]
            t_sq = S_(lambda e, h, sl=sl, sq=sq: e.activation(
                sq, sl, AF.Square), waits=[mt, t_raw])
            rdone(pk, [t_raw, t_sq])
            if PREP_STAGE <= 1:
                out_toks.append(t_sq)
                rdone("pcb", t_raw)
                continue
            t_r0 = T_(lambda e, h, sq=sq: e.matmul(
                psS[:, 0:512], onesq[0:64, 0:128], sq[0:64, :],
                start=True, stop=True),
                waits=[t_sq, t_blkc] + ruse("psS"))
            t_r1 = T_(lambda e, h, sq=sq: e.matmul(
                psR[:, 0:512], onesq[64:128, 0:128], sq[64:128, :],
                start=True, stop=True),
                waits=[t_sq] + ruse("psR"))
            bias_r = cstcol(-math.log(8.0)) if side == 0 else cstcol(0.0)
            tu0 = S_(lambda e, h: e.activation(
                rmsr[0:64, 0:512], psS[0:64, 0:512], AF.Ln,
                scale=1.0 / 64.0, bias=cstcol(RMS_EPS)[0:64, :]),
                waits=[t_r0] + ruse("rmsr"))
            rdone("psS", tu0)
            tu1 = S_(lambda e, h: e.activation(
                rmsr[64:128, 0:512], psR[64:128, 0:512], AF.Ln,
                scale=1.0 / 64.0, bias=cstcol(RMS_EPS)[64:128, :]),
                waits=[t_r1])
            rdone("psR", tu1)
            t_bc = S_(lambda e, h, bias_r=bias_r: e.activation(
                rmsb[:, 0:512], rmsr[:, 0:512], AF.Exp, scale=-0.5,
                bias=bias_r), waits=[tu0, tu1] + ruse("rmsb"))
            rdone("rmsr", t_bc)
            if PREP_STAGE <= 2:
                out_toks.append(t_bc)
                continue
            t_sw = T_(lambda e, h, tc=tc, rawbuf=rawbuf: e.matmul(
                psR[:, 0:512], perm0[:],
                rawbuf[:, tc * 512:(tc + 1) * 512], start=True, stop=True),
                waits=[t_raw, t_perm0] + ruse("psR"))
            tm1 = V_(lambda e, h, tc=tc, rawbuf=rawbuf: e.tensor_tensor(
                pcb[:, 0:512], rawbuf[:, tc * 512:(tc + 1) * 512],
                ropeC[:, tc * 512:(tc + 1) * 512], ALU.mult),
                waits=[t_raw, t_ropeC] + ruse("pcb"))
            ts2 = V_(lambda e, h, tc=tc: e.tensor_tensor(
                pcb[:, 512:1024], psR[:, 0:512],
                ropeS[:, tc * 512:(tc + 1) * 512], ALU.mult),
                waits=[t_sw, t_ropeS])
            rdone("psR", ts2)
            tsum = V_(lambda e, h: e.tensor_tensor(
                pcb[:, 0:512], pcb[:, 0:512], pcb[:, 512:1024], ALU.add),
                waits=[tm1, ts2])
            tq = V_(lambda e, h, tc=tc, outbuf=outbuf: e.tensor_tensor(
                outbuf[:, tc * 512:(tc + 1) * 512], pcb[:, 0:512],
                rmsb[:, 0:512], ALU.mult), waits=[tsum, t_bc])
            rdone("pcb", tq)
            rdone("rmsb", tq)
            radd("ropes", tq)
            out_toks.append(tq)
        return out_toks

    def l0_head(hd, nlam_tok, xn_dep, out_toks, pump=None):
        tw_q = DMAg(lambda e, h, hd=hd: e.dma_start(wbA[:], wq0[hd]),
                    waits=ruse("wbA"), sem="DA")
        tw_k = DMAg(lambda e, h, hd=hd: e.dma_start(wbB[:], wk0[hd]),
                    waits=ruse("wbB"), sem="DB")
        tw_v = DMAg(lambda e, h, hd=hd: e.dma_start(wbC[:], wv0[hd]),
                    waits=ruse("wbC"), sem="DC")
        tqs = l0_prep_side(0, wbA, tw_q, p0, qh, xn_dep)
        rdone("wbA", tqs[-1])
        tks = l0_prep_side(1, wbB, tw_k, p1, kh, xn_dep)
        rdone("wbB", tks[-1])
        if ATTN_STAGE <= 1:
            out_toks.extend(tqs + tks)
            return out_toks
        vt = []
        for qt in range(16):
            sl = PSCD[qt % 2][:, 0:128]
            pk = ("pcd", qt % 2)
            mt = None
            for ic in range(8):
                mt = (T_ if ic == 7 else T0_)(
                    lambda e, h, ic=ic, qt=qt, sl=sl: e.matmul(
                    sl, xnT[:, ic * T + qt * 128:ic * T + qt * 128 + 128],
                    wbC[:, ic * 128:(ic + 1) * 128],
                    start=(ic == 0), stop=(ic == 7)),
                    waits=(([tw_v] + xn_dep + ruse(pk))
                           if ic == 0 else []))
            vdst = vtok[:, qt * VS:qt * VS + 130].rearrange(
                "p (a b) -> p a b", b=65)[:, :, 0:64]
            vsrc = sl[:, 0:128].rearrange("p (a b) -> p a b", b=64)
            tv = V_(lambda e, h, vdst=vdst, vsrc=vsrc: e.tensor_copy(
                vdst, vsrc), waits=[mt])
            rdone(pk, tv)
            vt.append(tv)
        rdone("wbC", vt[-1])
        if ATTN_STAGE <= 2:
            out_toks.extend(vt)
            return out_toks

        def stageA(qt):
            par = qt % 2
            exp_by = []
            for side, pbuf in ((0, p0[:, par * T:par * T + T]),
                               (1, p1[:, par * T:par * T + T])):
                side_toks = []
                ng = (qt + 4) // 4
                for g in range(ng):
                    jn = min(4, qt + 1 - 4 * g)
                    sl = PSAB[g % 2][:, 0:512]
                    pk = ("pab", g % 2)
                    mlast = None
                    for j in range(jn):
                        kb = 4 * g + j
                        mlast = (T_ if j == jn - 1 else T0_)(
                            lambda e, h, j=j, kb=kb, side=side, qt=qt,
                            sl=sl: e.matmul(
                            sl[:, j * 128:(j + 1) * 128],
                            kh[64 * side:64 * side + 64,
                               kb * 128:(kb + 1) * 128],
                            qh[64 * side:64 * side + 64,
                               qt * 128:(qt + 1) * 128],
                            start=True, stop=True),
                            waits=(([tqs[-1], tks[-1]] + ruse(pk))
                                   if j == 0 else []))
                    wl = [mlast, t_mask]
                    if g == ng - 1:
                        jd = qt - 4 * g
                        tmask = V_(lambda e, h, jd=jd, sl=sl:
                                   e.tensor_tensor(
                            sl[:, jd * 128:(jd + 1) * 128],
                            sl[:, jd * 128:(jd + 1) * 128],
                            maskb[:], ALU.add), waits=wl)
                        wl = [tmask]
                    texp = S_(lambda e, h, jn=jn, g=g, pbuf=pbuf,
                              sl=sl: e.activation(
                        pbuf[:, g * 512:g * 512 + jn * 128],
                        sl[:, 0:jn * 128], AF.Exp, bias=cstcol(-8.0)),
                        waits=wl + ruse(("pex", side, par)))
                    rdone(pk, texp)
                    side_toks.append(texp)
                exp_by.append(side_toks)
            return exp_by

        def stageB(qt, exp_by):
            par = qt % 2
            pvps = PSCD[par]
            pvk = ("pcd", par)
            pvt = None
            for side, base in ((0, 0), (1, 256)):
                pbuf = (p0 if side == 0 else p1)[:, par * T:par * T + T]
                for kb in range(qt + 1):
                    pvt = (T_ if kb == qt else T0_)(
                        lambda e, h, kb=kb, qt=qt, base=base, pbuf=pbuf:
                        e.matmul(
                        pvps[:, base:base + 129],
                        pbuf[:, kb * 128:(kb + 1) * 128],
                        vtok[:, kb * VS:kb * VS + 129],
                        start=(kb == 0), stop=(kb == qt)),
                        waits=[exp_by[side][kb // 4], vt[kb]] +
                        (ruse(pvk) if kb == 0 and side == 0 else []))
                rdone(("pex", side, par), pvt)
            c8 = 8 + 16 * par
            zsrc = pvps[:, 0:512].rearrange(
                "p (a b) -> p a b", b=256)[:, :, 64:65]
            zdst = sml2[:, c8:c8 + 2].rearrange(
                "p (a b) -> p a b", b=1)
            rz = V_(lambda e, h, zsrc=zsrc, zdst=zdst: e.reciprocal(
                zdst, zsrc), waits=[pvt])
            pv1 = pvps[:, 256:386].rearrange(
                "p (a b) -> p a b", b=65)[:, :, 0:64]
            pv0 = pvps[:, 0:130].rearrange(
                "p (a b) -> p a b", b=65)[:, :, 0:64]
            tmp = pcb[:, 128:256].rearrange("p (a b) -> p a b", b=64)
            cmb = pcb[:, 0:128].rearrange("p (a b) -> p a b", b=64)
            t1 = V_(lambda e, h, c8=c8, pv1=pv1, tmp=tmp: e.tensor_scalar(
                tmp, pv1, sml2[:, c8 + 1:c8 + 2],
                NLAM, ALU.mult, ALU.mult),
                waits=[rz, nlam_tok] + ruse("pcb"))
            t2b = V_(lambda e, h, c8=c8, pv0=pv0, tmp=tmp, cmb=cmb:
                     e.scalar_tensor_tensor(
                cmb, pv0, sml2[:, c8:c8 + 1],
                tmp, ALU.mult, ALU.add), waits=[t1])
            rdone(pvk, t2b)
            tst = V_(lambda e, h: e.bn_stats(
                stats[:, 0:6], pcb[:, 0:128]),
                waits=[t2b] + ruse("aggr0", "stats"))
            tag = V_(lambda e, h: e.bn_aggr(aggr[:, 0:2], stats[:, 0:6]),
                     waits=[tst])
            rdone("stats", tag)
            tu = S_(lambda e, h: e.activation(sml2[:, 11:12], aggr[:, 1:2],
                                              AF.Ln, bias=cstcol(LN_EPS)),
                    waits=[tag] + ruse("lnsml"))
            tr_ = S_(lambda e, h: e.activation(
                sml2[:, 12:13], sml2[:, 11:12], AF.Exp, scale=-0.5,
                bias=cstcol(math.log(1.0 - LAMBDA_INIT))), waits=[tu])
            tln = V_(lambda e, h: e.tensor_scalar(
                pcb[:, 256:384], pcb[:, 0:128], aggr[:, 0:1],
                sml2[:, 12:13], ALU.subtract, ALU.mult),
                waits=[tr_])
            rdone("aggr0", tln)
            rdone("lnsml", tln)
            ttr = T_(lambda e, h: e.transpose(psU[:, 0:128],
                                              pcb[:, 256:384],
                                              ident[:]),
                     waits=[tln] + ruse(("psU", 0)))
            tev = V_(lambda e, h, hd=hd, qt=qt: e.tensor_copy(
                aT[:, hd * T + qt * 128:hd * T + qt * 128 + 128],
                psU[:, 0:128]), waits=[ttr])
            rdone(("psU", 0), tev)
            rdone("pcb", [tln, ttr, tev])
            out_toks.append(tev)

        prev = None
        for qt in range(16):
            eb = stageA(qt)
            if prev is not None:
                stageB(qt - 1, prev)
                if pump is not None:
                    pump(1)
            prev = eb
        stageB(15, prev)
        if pump is not None:
            pump(1)
        return out_toks

    def emit_out_proj(wdram, src, src_toks, bias_colst, n_oc, xadd_toks):
        for oc in range(n_oc):
            par = oc % 2
            wb = wbA if par == 0 else wbB
            tw = DMAg(lambda e, h, oc=oc, wb=wb: e.dma_start(
                wb[:], wdram[oc]),
                waits=ruse("wbA" if par == 0 else "wbB"),
                sem="DA" if par == 0 else "DB")
            mlast = None
            for tc in range(4):
                qd = tc % 2
                sl = PSAB[qd][:, 0:512]
                pk = ("pab", qd)
                mt = None
                for ic in range(8):
                    mt = (T_ if ic == 7 else T0_)(
                        lambda e, h, ic=ic, tc=tc, sl=sl, wb=wb:
                            e.matmul(sl,
                                     wb[:, ic * 128:(ic + 1) * 128],
                                     src[:, ic * T + tc * 512:
                                         ic * T + tc * 512 + 512],
                                     start=(ic == 0), stop=(ic == 7)),
                            waits=(([tw] + src_toks + ruse(pk))
                                   if ic == 0 else []))
                mlast = mt
                if bias_colst is not None:
                    tev = V_(lambda e, h, sl=sl, oc=oc: e.tensor_scalar(
                        pcb[:, 0:512], sl, bias_colst[:, oc:oc + 1], None,
                        ALU.add), waits=[mt] + ruse("pcb"))
                else:
                    tev = V_(lambda e, h, sl=sl: e.tensor_copy(
                        pcb[:, 0:512], sl), waits=[mt] + ruse("pcb"))
                rdone(pk, tev)
                lastadd = None
                for j in range(4):
                    ttr = T_(lambda e, h, j=j: e.transpose(
                        psU[:, 128:256], pcb[:, j * 128:(j + 1) * 128],
                        ident[:]), waits=[tev] + ruse(("psU", 1)))
                    tqi = tc * 4 + j
                    tadd = V_(lambda e, h, tqi=tqi, oc=oc:
                              e.scalar_tensor_tensor(
                        x_res[:, tqi * 1024 + oc * 128:
                              tqi * 1024 + oc * 128 + 128],
                        psU[:, 128:256], 0.0,
                        x_res[:, tqi * 1024 + oc * 128:
                              tqi * 1024 + oc * 128 + 128],
                        ALU.add, ALU.add), waits=[ttr])
                    rdone(("psU", 1), tadd)
                    lastadd = tadd
                    xadd_toks.append(tadd)
                rdone("pcb", lastadd)
            rdone("wbA" if par == 0 else "wbB", mlast)
        return xadd_toks

    def emit_ffn(l, xn_dep, xadd_toks, w2ready):
        tb2q, _ = emit_quant(b2cols[l][:, 0:8], 8, b2qcols[l][:, 0:8],
                             "fp16", skip_stage1=True, waits=[t_b2[l]])
        for blk in range(2):
            hq_toks = {}
            for hc in range(32):
                par = hc % 2
                wb = wbA if par == 0 else wbB
                tw = DMAg(lambda e, h, hc=hc, wb=wb, l=l: e.dma_start(
                    wb[:], wff[l]["w1"][hc]),
                    waits=ruse("wbA" if par == 0 else "wbB"),
                    sem="DA" if par == 0 else "DB")
                hts = []
                mm_l = None
                for half in range(2):
                    tok0 = blk * 1024 + half * 512
                    qd = (2 * hc + half) % 2
                    sl = PSAB[qd][:, 0:512]
                    pk = ("pab", qd)
                    mt = None
                    for ic in range(8):
                        mt = (T_ if ic == 7 else T0_)(
                            lambda e, h, ic=ic, sl=sl, wb=wb,
                                tok0=tok0: e.matmul(
                            sl, wb[:, ic * 128:(ic + 1) * 128],
                            xnT[:, ic * T + tok0:ic * T + tok0 + 512],
                            start=(ic == 0), stop=(ic == 7)),
                            waits=(([tw] + xn_dep + ruse(pk))
                                   if ic == 0 else []))
                    mm_l = mt
                    out_sl = hqv[:, hc * 1024 + half * 512:
                                 hc * 1024 + half * 512 + 512]
                    tqd, zd = emit_quant(sl, 512, out_sl, "e4m3",
                                         bias_col=b1cols[l][:, hc:hc + 1],
                                         relu=True,
                                         waits=[mt, t_b1[l]] +
                                         ruse(("hqw", hc)))
                    rdone(pk, zd)
                    hts.append(tqd)
                rdone("wbA" if par == 0 else "wbB", mm_l)
                hq_toks[hc] = hts
                rdone(("hqw", hc), hts)
            for tc2 in range(2):
                tok_in_blk = tc2 * 512
                for og in range(2):
                    mlast = [None] * 4
                    for hc in range(32):
                        par = hc % 2
                        tws = DMAw(lambda e, h, hc=hc, par=par, l=l:
                                   e.dma_start(
                            w2qs[:, par * 1024:par * 1024 + 1024],
                            w2q_sc[l][hc]),
                            waits=[w2ready] + ruse(("w2qsb", par),
                                                    "w2qs"),
                            sem="W2A" if par == 0 else "W2B")
                        for oc4 in range(4):
                            oc = og * 4 + oc4
                            mlast[oc4] = (T_ if (hc == 31 or oc4 == 3) else T0_)(
                                lambda e, h, oc4=oc4, oc=oc, par=par,
                                hc=hc, tok_in_blk=tok_in_blk: e.matmul(
                                    PS4[oc4][:, 0:512],
                                    w2qs[:, par * 1024 + oc * 128:
                                         par * 1024 + oc * 128 + 128],
                                    hqv[:, hc * 1024 + tok_in_blk:
                                        hc * 1024 + tok_in_blk + 512],
                                    start=(hc == 0), stop=(hc == 31)),
                                waits=([tws] + hq_toks[hc] +
                                       (ruse(("pab", 0), ("pab", 1),
                                             ("pcd", 0), ("pcd", 1))
                                        if hc == 0 else [])))
                        rdone(("w2qsb", par), mlast[3])
                        radd("w2qs", mlast[3])
                    for oc4 in range(4):
                        oc = og * 4 + oc4
                        sl = PS4[oc4][:, 0:512]
                        pk = [("pab", 0), ("pab", 1), ("pcd", 0),
                              ("pcd", 1)][oc4]
                        tqd, zd = emit_quant(
                            sl, 512, pcb[:, 0:512], "fp16",
                            bias_col=b2qcols[l][:, oc:oc + 1], relu=False,
                            waits=[mlast[oc4], tb2q] + ruse("pcb"))
                        rdone(pk, zd)
                        ttr = None
                        for j in range(4):
                            ttr = (T_ if j == 3 else T0_)(
                                lambda e, h, j=j: e.transpose(
                                psU[:, 128 + j * 128:256 + j * 128],
                                pcb[:, j * 128:(j + 1) * 128], ident[:]),
                                waits=(([tqd] + ruse(("psU", 1)))
                                       if j == 0 else []))
                        tq0 = blk * 8 + tc2 * 4
                        xrv = x_res[:, 0:16384].rearrange(
                            "p (a b) -> p a b", b=1024)[
                            :, tq0:tq0 + 4, oc * 128:oc * 128 + 128]
                        puv = psU[:, 128:640].rearrange(
                            "p (a b) -> p a b", b=128)
                        tadd = V_(lambda e, h, xrv=xrv, puv=puv:
                                  e.scalar_tensor_tensor(
                            xrv, puv, 0.0, xrv,
                            ALU.add, ALU.add), waits=[ttr])
                        rdone(("psU", 1), tadd)
                        xadd_toks.append(tadd)
                        rdone("pcb", tadd)
        return xadd_toks

    def pair_qk(ph, wdr, bcol, src, src_toks, out):
        tw = DMAg(lambda e, h, ph=ph, wdr=wdr: e.dma_start(
            wbA[:, 0:256], wdr[ph]), waits=ruse("wbA"), sem="DA")
        lt = None
        for tc in range(4):
            qd = tc % 2
            sl = PSAB[qd][:, 0:512]
            pk = ("pab", qd)
            mt = None
            for cc in range(2):
                mt = (T_ if cc == 1 else T0_)(
                    lambda e, h, cc=cc, tc=tc, sl=sl, src=src: e.matmul(
                    sl, wbA[:, cc * 128:(cc + 1) * 128],
                    src[:, cc * T + tc * 512:cc * T + tc * 512 + 512],
                    start=(cc == 0), stop=(cc == 1)),
                    waits=(([tw] + src_toks + ruse(pk))
                           if cc == 0 else []))
            te = S_(lambda e, h, sl=sl, tc=tc, out=out, bcol=bcol:
                    e.activation(out[:, tc * 512:(tc + 1) * 512], sl,
                                 AF.Gelu, bias=bcol), waits=[mt])
            rdone(pk, te)
            lt = te
        rdone("wbA", lt)
        return lt

    def rope_pair(wdr, bcol, out, xn_dep, trc, trs):
        tw = DMAg(lambda e, h, wdr=wdr: e.dma_start(wbA[:], wdr[0]),
                  waits=ruse("wbA"), sem="DA")
        lt = None
        for tc in range(4):
            qd = tc % 2
            sl = PSAB[qd][:, 0:512]
            pk = ("pab", qd)
            mt = None
            for ic in range(8):
                mt = (T_ if ic == 7 else T0_)(
                    lambda e, h, ic=ic, tc=tc, sl=sl: e.matmul(
                    sl, wbA[:, ic * 128:(ic + 1) * 128],
                    xnT[:, ic * T + tc * 512:ic * T + tc * 512 + 512],
                    start=(ic == 0), stop=(ic == 7)),
                    waits=(([tw] + xn_dep + ruse(pk))
                           if ic == 0 else []))
            traw = S_(lambda e, h, sl=sl, tc=tc, bcol=bcol: e.activation(
                p1[:, tc * 512:(tc + 1) * 512], sl, AF.Identity,
                bias=bcol), waits=[mt] + ruse("p1r"))
            rdone(pk, traw)
            tsw = T_(lambda e, h, tc=tc: e.matmul(
                psS[:, 0:512], perm1[:],
                p1[:, tc * 512:(tc + 1) * 512], start=True, stop=True),
                waits=[traw, t_perm1] + ruse("psS"))
            tm1 = V_(lambda e, h, tc=tc: e.tensor_tensor(
                pcb[:, 0:512], p1[:, tc * 512:(tc + 1) * 512],
                ropeC[:, tc * 512:(tc + 1) * 512], ALU.mult),
                waits=[traw, trc] + ruse("pcb"))
            ts2 = V_(lambda e, h, tc=tc: e.tensor_tensor(
                pcb[:, 512:1024], psS[:, 0:512],
                ropeS[:, tc * 512:(tc + 1) * 512], ALU.mult),
                waits=[tsw, trs])
            rdone("psS", ts2)
            tsum = V_(lambda e, h: e.tensor_tensor(
                pcb[:, 0:512], pcb[:, 0:512], pcb[:, 512:1024], ALU.add),
                waits=[tm1, ts2])
            tg = S_(lambda e, h, tc=tc, out=out: e.activation(
                out[:, tc * 512:(tc + 1) * 512], pcb[:, 0:512], AF.Gelu),
                waits=[tsum])
            rdone("pcb", tg)
            radd("p1r", tg)
            lt = tg
        rdone("wbA", lt)
        return lt

    def emit_bounds(ph):
        first = V_(lambda e, h: e.memset(sml[0:1, 16:32], 0.0),
                   waits=ruse("bnd"))
        tmx_toks = []
        for side, buf in ((0, qh), (1, kh)):
            for tc in range(4):
                sq = p1[:, tc * 512:(tc + 1) * 512]
                tsq = S_(lambda e, h, buf=buf, tc=tc, sq=sq:
                         e.activation(
                    sq, buf[:, tc * 512:(tc + 1) * 512], AF.Square),
                    waits=ruse("bsq", "p1r") + [first])
                for h2 in range(2):
                    ps_ = psS if h2 == 0 else psR
                    pk_ = "psS" if h2 == 0 else "psR"
                    tred = T_(lambda e, h, sq=sq, h2=h2, ps_=ps_:
                              e.matmul(
                        ps_[:, 0:512],
                        onesq[64 * h2:64 * h2 + 64, 0:128],
                        sq[64 * h2:64 * h2 + 64, :],
                        start=True, stop=True),
                        waits=[tsq, t_blkc] + ruse(pk_))
                    cix = 16 + side * 8 + h2 * 4 + tc
                    tmx = V_(lambda e, h, cix=cix, ps_=ps_: e.reduce_max(
                        sml[0:1, cix:cix + 1],
                        ps_[0:1, 0:512], AX.X), waits=[tred])
                    rdone(pk_, tmx)
                    tmx_toks.append(tmx)
                rdone("bsq", tmx_toks[-2:])
        tms = []
        for side in range(2):
            for h2 in range(2):
                c0 = 16 + side * 8 + h2 * 4
                tms.append(V_(lambda e, h, c0=c0, side=side, h2=h2:
                              e.reduce_max(
                    sml2[0:1, 16 + side * 2 + h2:17 + side * 2 + h2],
                    sml[0:1, c0:c0 + 4], AX.X), waits=tmx_toks))
        tp = V_(lambda e, h: e.tensor_tensor(
            sml2[0:1, 20:22], sml2[0:1, 16:18], sml2[0:1, 18:20],
            ALU.mult), waits=tms)
        tp = S_(lambda e, h: e.activation(sml2[0:1, 22:24],
                                          sml2[0:1, 20:22], AF.Ln,
                                          bias=cstcol(2.0 ** -60)[0:1, :]),
                waits=[tp])
        tp = S_(lambda e, h: e.activation(
            sml2[0:1, 24:26], sml2[0:1, 22:24], AF.Exp, scale=0.5,
            bias=cstcol(-math.log(8.0))[0:1, :]), waits=[tp])
        tp = V_(lambda e, h: e.tensor_scalar(
            smlb[0:1, 4:6], sml2[0:1, 24:26], 40.0, -1.0, ALU.min,
            ALU.mult), waits=[tp])
        t2 = emit_bcast2([smlb[0:1, 4:6]],
                         [bcols[:, 2 * ph:2 * ph + 1],
                          bcols[:, 2 * ph + 1:2 * ph + 2]], [tp])
        rdone("bnd", [t2])
        return t2

    def l1_attention(xn_dep, aT_toks):
        DMAg(lambda e, h: e.dma_start(ropeC[:], ropec1[0]),
             waits=ruse("ropes"), sem="Dr")
        trs = DMAg(lambda e, h: e.dma_start(ropeS[:], ropec1[1]),
                   sem="Dr")
        trc = trs
        cq_toks = []
        ckv_toks = []
        for (wdr, bc_sl, out, toks) in ((wq1, bias1[:, 0:2], cqb, cq_toks),
                                        (wkv1, bias2[:, 0:2], ckvb,
                                         ckv_toks)):
            for oc in range(2):
                tw = DMAg(lambda e, h, oc=oc, wdr=wdr: e.dma_start(
                    wbA[:], wdr[oc]), waits=ruse("wbA"), sem="DA")
                lt = None
                for tc in range(4):
                    qd = tc % 2
                    sl = PSAB[qd][:, 0:512]
                    pk = ("pab", qd)
                    mt = None
                    for ic in range(8):
                        mt = (T_ if ic == 7 else T0_)(
                            lambda e, h, ic=ic, tc=tc, sl=sl: e.matmul(
                            sl, wbA[:, ic * 128:(ic + 1) * 128],
                            xnT[:, ic * T + tc * 512:
                                ic * T + tc * 512 + 512],
                            start=(ic == 0), stop=(ic == 7)),
                            waits=(([tw] + xn_dep + t_bias +
                                    ruse(pk))
                                   if ic == 0 else []))
                    te = S_(lambda e, h, sl=sl, oc=oc, tc=tc, out=out,
                            bc_sl=bc_sl: e.activation(
                        out[:, oc * T + tc * 512:oc * T + tc * 512 + 512],
                        sl, AF.Gelu, bias=bc_sl[:, oc:oc + 1]),
                        waits=[mt])
                    rdone(pk, te)
                    toks.append(te)
                    lt = te
                rdone("wbA", lt)
        for ph in range(8):
            if ph < 7:
                tq_ = pair_qk(ph, wqc1, bias1[:, 2 + ph:3 + ph], cqb,
                              cq_toks, qh)
                tk_ = pair_qk(ph, wkc1, bias2[:, 2 + ph:3 + ph], ckvb,
                              ckv_toks, kh)
            else:
                tq_ = rope_pair(wqr1, bias1[:, 9:10], qh, xn_dep, trc, trs)
                tk_ = rope_pair(wkr1, bias2[:, 9:10], kh, xn_dep, trc, trs)
            twv = DMAg(lambda e, h, ph=ph: e.dma_start(wbC[:, 0:256],
                                                       wv1[ph]),
                       waits=ruse("wbC"), sem="DC")
            vt = []
            for qt in range(16):
                sl = PSCD[qt % 2][:, 0:128]
                pk = ("pcd", qt % 2)
                mt = None
                for cc in range(2):
                    mt = (T_ if cc == 1 else T0_)(
                        lambda e, h, cc=cc, qt=qt, sl=sl: e.matmul(
                        sl, ckvb[:, cc * T + qt * 128:
                                 cc * T + qt * 128 + 128],
                        wbC[:, cc * 128:(cc + 1) * 128],
                        start=(cc == 0), stop=(cc == 1)),
                        waits=(([twv] + ckv_toks + ruse(pk))
                               if cc == 0 else []))
                vdst = vtok[:, qt * VS:qt * VS + 130].rearrange(
                    "p (a b) -> p a b", b=65)[:, :, 0:64]
                vsrc = sl[:, 0:128].rearrange("p (a b) -> p a b", b=64)
                tv = V_(lambda e, h, vdst=vdst, vsrc=vsrc: e.tensor_copy(
                    vdst, vsrc), waits=[mt])
                rdone(pk, tv)
                vt.append(tv)
            rdone("wbC", vt[-1])
            bnd_tok = emit_bounds(ph)

            def stageA1(qt, h2):
                par = (2 * qt + h2) % 2
                po = p0[:, par * T:par * T + T]
                hh = 2 * ph + h2
                side_toks = []
                ng = (qt + 4) // 4
                for g in range(ng):
                    jn = min(4, qt + 1 - 4 * g)
                    sl = PSAB[g % 2][:, 0:512]
                    pk = ("pab", g % 2)
                    mlast = None
                    for j in range(jn):
                        kb = 4 * g + j
                        mlast = (T_ if j == jn - 1 else T0_)(
                            lambda e, h, j=j, kb=kb, h2=h2, qt=qt,
                            sl=sl: e.matmul(
                            sl[:, j * 128:(j + 1) * 128],
                            kh[64 * h2:64 * h2 + 64,
                               kb * 128:(kb + 1) * 128],
                            qh[64 * h2:64 * h2 + 64,
                               qt * 128:(qt + 1) * 128],
                            start=True, stop=True),
                            waits=(([tq_, tk_] + ruse(pk))
                                   if j == 0 else []))
                    wl = [mlast, t_mask]
                    if g == ng - 1:
                        jd = qt - 4 * g
                        tmask = V_(lambda e, h, jd=jd, sl=sl:
                                   e.tensor_tensor(
                            sl[:, jd * 128:(jd + 1) * 128],
                            sl[:, jd * 128:(jd + 1) * 128],
                            maskb[:], ALU.add), waits=wl)
                        wl = [tmask]
                    texp = S_(lambda e, h, jn=jn, g=g, hh=hh, sl=sl,
                              po=po: e.activation(
                        po[:, g * 512:g * 512 + jn * 128],
                        sl[:, 0:jn * 128], AF.Exp, scale=0.125,
                        bias=bcols[:, hh:hh + 1]),
                        waits=wl + [bnd_tok] + ruse(("pex", par)))
                    rdone(pk, texp)
                    side_toks.append(texp)
                return side_toks

            def stageB1(qt, h2, chunks):
                par = (2 * qt + h2) % 2
                po = p0[:, par * T:par * T + T]
                pvps = PSCD[qt % 2]
                pvk = ("pcd", qt % 2)
                base = 256 * h2
                pvt = None
                for kb in range(qt + 1):
                    pvt = (T_ if kb == qt else T0_)(
                        lambda e, h, kb=kb, qt=qt, h2=h2, base=base,
                        po=po: e.matmul(
                        pvps[:, base:base + 65],
                        po[:, kb * 128:(kb + 1) * 128],
                        vtok[:, kb * VS + 64 * h2:kb * VS + 64 * h2 + 65],
                        start=(kb == 0), stop=(kb == qt)),
                        waits=[chunks[kb // 4], vt[kb]] +
                        (ruse(pvk) if kb == 0 and h2 == 0 else []))
                rdone(("pex", par), pvt)
                return pvt

            def tail1(qt, pv0, pv1):
                pvps = PSCD[qt % 2]
                pvk = ("pcd", qt % 2)
                c8 = 8 + 2 * (qt % 2)
                rz0 = V_(lambda e, h, c8=c8, pvps=pvps: e.reciprocal(
                    sml2[:, c8:c8 + 1], pvps[:, 64:65]),
                    waits=[pv0, pv1])
                rz1 = V_(lambda e, h, c8=c8, pvps=pvps: e.reciprocal(
                    sml2[:, c8 + 1:c8 + 2], pvps[:, 256:257]),
                    waits=[pv1])
                tno0 = V_(lambda e, h, c8=c8, pvps=pvps: e.tensor_scalar(
                    pcb[:, 0:64], pvps[:, 0:64],
                    sml2[:, c8:c8 + 1], None, ALU.mult),
                    waits=[rz0] + ruse("pcb"))
                tno1 = V_(lambda e, h, c8=c8, pvps=pvps: e.tensor_scalar(
                    pcb[:, 64:128], pvps[:, 257:321],
                    sml2[:, c8 + 1:c8 + 2], None, ALU.mult),
                    waits=[rz1])
                rdone(pvk, tno1)
                ttr = T_(lambda e, h: e.transpose(
                    psU[:, 0:128], pcb[:, 0:128], ident[:]),
                    waits=[tno0, tno1] + ruse(("psU", 0)))
                rdone("pcb", ttr)
                tev = V_(lambda e, h, ph=ph, qt=qt: e.tensor_scalar(
                    aT[:, ph * T + qt * 128:ph * T + qt * 128 + 128],
                    psU[:, 0:128], bvcol[:, ph:ph + 1], None,
                    ALU.add), waits=[ttr])
                rdone(("psU", 0), tev)
                aT_toks.append(tev)

            units = [(qt, h2) for qt in range(16) for h2 in range(2)]
            pend = {}
            prevu = None
            for u in units:
                chunks = stageA1(*u)
                pend[u] = [chunks, None]
                if prevu is not None:
                    pend[prevu][1] = stageB1(prevu[0], prevu[1],
                                             pend[prevu][0])
                    if prevu[1] == 1:
                        q0 = prevu[0]
                        tail1(q0, pend[(q0, 0)][1], pend[(q0, 1)][1])
                        del pend[(q0, 0)], pend[(q0, 1)]
                prevu = u
            pend[prevu][1] = stageB1(prevu[0], prevu[1], pend[prevu][0])
            tail1(15, pend[(15, 0)][1], pend[(15, 1)][1])
        return aT_toks

    lvl = SCHED_LEVEL
    final_dep = []

    def gen_w2():
        for g in (w2q_gen(0), w2q_gen(1)):
            for _ in g:
                yield

    def make_pump(gen):
        state = [gen is None]

        def pump(n=1):
            if state[0]:
                return
            try:
                for _ in range(n):
                    next(gen)
            except StopIteration:
                state[0] = True

        def drain():
            if state[0]:
                return
            for _ in gen:
                pass
            state[0] = True
        return pump, drain

    gen0 = gen_w2() if lvl >= 2 else None
    pump0, drain0 = make_pump(gen0)
    pump0(6)
    ln0 = emit_layernorm(t_x + [t_ident])
    xn_dep0 = [ln0[-1]]
    final_dep = [ln0[-1]]
    if lvl >= 1:
        nlam_tok = emit_lambda() if LAMBDA_ON else None
        l0_toks = []
        for hd in range(N_HEADS_L0):
            l0_head(hd, nlam_tok, xn_dep0, l0_toks, pump=pump0)
        drain0()
        final_dep = l0_toks[-1:] or final_dep
        if PROJ_ON:
            xadd0 = []
            emit_out_proj(wproj0, aT, l0_toks[-1:], None, 8, xadd0)
            final_dep = [xadd0[-1]]
    if lvl >= 2:
        emit_ffn(0, xn_dep0, xadd0, W2R[0])
        final_dep = [xadd0[-1]]
    if lvl >= 3:
        ln1 = emit_layernorm([xadd0[-1]])
        xn_dep1 = [ln1[-1]]
        aT_toks1 = []
        l1_attention(xn_dep1, aT_toks1)
        xadd1 = []
        emit_out_proj(wo1, aT, [aT_toks1[-1]], bocol, 8, xadd1)
        final_dep = [xadd1[-1]]
    if lvl >= 4:
        emit_ffn(1, xn_dep1, xadd1, W2R[1])
        final_dep = [xadd1[-1]]

    for i in range(16):
        DMAo(lambda e, h, i=i: e.dma_start(
            y3[i], x_res[:, i * 1024:(i + 1) * 1024]), waits=final_dep)
    P.op("sync", lambda e, h: None,
         waits=[(s, P.sem_counts[s]) for s in
                ("Do", "Dc", "Dg", "DA", "DB", "DC", "W2A", "W2B",
                 "DZ0", "DZ1", "DS0", "DS1", "Dr")
                if P.sem_counts.get(s, 0) > 0])

    with nc.Block() as block:
        @block.tensor
        def _(eng):
            for f in P.th["tensor"]:
                f(eng, H)

        @block.vector
        def _(eng):
            for f in P.th["vector"]:
                f(eng, H)

        @block.scalar
        def _(eng):
            for f in P.th["scalar"]:
                f(eng, H)

        @block.gpsimd
        def _(eng):
            for f in P.th["gpsimd"]:
                f(eng, H)

        @block.sync
        def _(eng):
            for f in P.th["sync"]:
                f(eng, H)
    es.close()
    return nc


def prep_inputs(inputs):
    g = {k: np.asarray(v) for k, v in inputs.items()}
    d = {}
    d["wq0"] = pack_lhsT(np.ascontiguousarray(g["L0_attn_Wq"].T))
    d["wk0"] = pack_lhsT(np.ascontiguousarray(g["L0_attn_Wk"].T))
    d["wv0"] = pack_lhsT(np.ascontiguousarray(g["L0_attn_Wv"].T))
    d["wproj0"] = pack_lhsT(np.ascontiguousarray(g["L0_attn_Wproj"].T))
    for l in range(2):
        d[f"w1_{l}"] = pack_lhsT(np.ascontiguousarray(g[f"L{l}_ffn_w1"].T))
        d[f"w2_{l}"] = np.ascontiguousarray(
            g[f"L{l}_ffn_w2"].T.astype(np.float32)).reshape(32, 128, 1024)
        d[f"b1c_{l}"] = pack_bias_cols(g[f"L{l}_ffn_b1"])
        d[f"b2c_{l}"] = pack_bias_cols(g[f"L{l}_ffn_b2"])
    d["wq1"] = pack_lhsT(g["L1_attn_wq"])
    d["bq1"] = pack_bias_cols(g["L1_attn_bq"])
    d["wqc1"] = pack_lhsT(g["L1_attn_wqc"])
    d["bqc1"] = pack_bias_cols(g["L1_attn_bqc"])
    d["wqr1"] = pack_lhsT(g["L1_attn_wqr"])
    d["bqr1"] = pack_bias_cols(g["L1_attn_bqr"])
    d["wkv1"] = pack_lhsT(g["L1_attn_wkv"])
    d["bkv1"] = pack_bias_cols(g["L1_attn_bkv"])
    d["wkr1"] = pack_lhsT(g["L1_attn_wkr"])
    d["bkr1"] = pack_bias_cols(g["L1_attn_bkr"])
    d["wkc1"] = pack_lhsT(g["L1_attn_wkc"])
    d["bkc1"] = pack_bias_cols(g["L1_attn_bkc"])
    d["wv1"] = pack_lhsT(g["L1_attn_wv"])
    d["bv1"] = pack_bias_cols(g["L1_attn_bv"])
    d["wo1"] = pack_lhsT(g["L1_attn_wo"])
    d["bo1"] = pack_bias_cols(g["L1_attn_bo"])
    d["lamv"] = np.concatenate(
        [g["L0_attn_lq1"], g["L0_attn_lq2"], g["L0_attn_lk1"],
         g["L0_attn_lk2"]]).astype(np.float32).reshape(1, 256)
    d["identc"] = np.eye(128, dtype=np.float32)
    d["permc0"] = perm_mat(32)
    d["permc1"] = perm_mat(64)
    d["maskc"] = causal_maskbias()
    d["ropec0"] = np.stack(rope_tables_l0())
    d["ropec1"] = np.stack(rope_tables_l1())
    return d


def kernel(**inputs):
    nc = build_program()
    shared = prep_inputs(inputs)
    x = np.asarray(inputs["L0_attn_x"])
    in_maps = []
    for b in range(NB):
        m = dict(shared)
        m["x"] = np.ascontiguousarray(x[b])
        in_maps.append(m)
    res = run_bass_kernel_spmd(nc, in_maps, core_ids=list(range(NB)))
    return np.stack([np.asarray(r["y"])
                     for r in res.results]).astype(np.float32)


# version 21
# speedup vs baseline: 1.1070x; 1.0017x over previous
import math
from contextlib import ExitStack

import numpy as np

import concourse.bass as bass
import concourse.mybir as mybir
from concourse.bass_utils import run_bass_kernel_spmd

F32 = mybir.dt.float32
F32R = mybir.dt.float32r
BF16 = mybir.dt.bfloat16
F8 = mybir.dt.float8e4
I32 = mybir.dt.int32
AF = mybir.ActivationFunctionType
ALU = mybir.AluOpType
AX = mybir.AxisListType

T = 2048
NB = 8
LN2 = math.log(2.0)
INV_LN2 = 1.0 / LN2
LAMBDA_INIT = 0.8 - 0.6 * math.exp(-0.3 * 4)
RMS_EPS = float(np.finfo(np.float32).eps)
LN_EPS = 1e-5
SCHED_LEVEL = 4
N_HEADS_L0 = 8
ATTN_STAGE = 4
PROJ_ON = 1
LAMBDA_ON = 1
PREP_STAGE = 3



def pack_lhsT(W):
    I, O = W.shape
    return np.ascontiguousarray(
        W.reshape(I // 128, 128, O // 128, 128)
        .transpose(2, 1, 0, 3)
        .reshape(O // 128, 128, I).astype(np.float32))


def pack_bias_cols(b):
    n = b.shape[0] // 128
    return np.ascontiguousarray(np.asarray(b, np.float32).reshape(n, 128).T)


def rope_tables_l0():
    HD = 64
    inv_freq = 1.0 / 10000.0 ** (np.arange(0, HD, 2, dtype=np.float32) / HD)
    fr = np.outer(np.arange(T, dtype=np.float32), inv_freq)
    cos, sin = np.cos(fr).T, np.sin(fr).T
    Cf = np.empty((128, T), np.float32)
    Sf = np.empty((128, T), np.float32)
    for blk in range(4):
        Cf[blk * 32:(blk + 1) * 32] = cos
        Sf[blk * 32:(blk + 1) * 32] = sin if blk % 2 == 0 else -sin
    return Cf, Sf


def rope_tables_l1():
    DR = 128
    inv_freq = 1.0 / 10000.0 ** (np.arange(0, DR, 2, dtype=np.float32) / DR)
    fr = np.outer(np.arange(T, dtype=np.float32), inv_freq)
    cos, sin = np.cos(fr).T, np.sin(fr).T
    Cf = np.empty((128, T), np.float32)
    Sf = np.empty((128, T), np.float32)
    Cf[0:64] = cos
    Cf[64:128] = cos
    Sf[0:64] = -sin
    Sf[64:128] = sin
    return Cf, Sf


def perm_mat(n):
    P = np.zeros((128, 128), np.float32)
    for a in range(128):
        P[(a // n ^ 1) * n + (a % n), a] = 1.0
    return P


def causal_maskbias():
    m = np.zeros((128, 128), np.float32)
    m[np.tril_indices(128, -1)] = -60.0
    return m


class Prog:
    ENGINES = ("tensor", "vector", "scalar", "gpsimd", "sync")

    def __init__(self):
        self.th = {e: [] for e in self.ENGINES}
        self.sem_counts = {}
        self.hwm = {e: {} for e in self.ENGINES}
        self.default_waits = []

    def op(self, eng, fn, waits=(), inc=None, inc_step=1):
        tok = None
        if inc is not None:
            v = self.sem_counts.get(inc, 0) + inc_step
            self.sem_counts[inc] = v
            tok = (inc, v)
        rw = []
        for w in list(self.default_waits) + list(waits):
            if w is None:
                continue
            s, v = w
            if self.hwm[eng].get(s, 0) < v:
                self.hwm[eng][s] = v
                rw.append((s, v))
        step = inc_step

        def thunk(engine, H):
            for (s, v) in rw:
                engine.wait_ge(H[s], v)
            ins = fn(engine, H)
            if inc is not None:
                if ins is None:
                    engine.sem_inc(H[inc], step)
                else:
                    ins.then_inc(H[inc], step)
        self.th[eng].append(thunk)
        return tok


def build_program():
    nc = bass.Bass("TRN2", target_bir_lowering=False)
    P = Prog()

    def din(name, shape, dtype=F32):
        return nc.dram_tensor(name, list(shape), dtype,
                              kind="ExternalInput").ap()

    x_in = din("x", [T, 1024])
    y_out = nc.dram_tensor("y", [T, 1024], F32, kind="ExternalOutput").ap()

    wq0 = din("wq0", [8, 128, 1024])
    wk0 = din("wk0", [8, 128, 1024])
    wv0 = din("wv0", [8, 128, 1024])
    wproj0 = din("wproj0", [8, 128, 1024])
    wff = []
    for l in range(2):
        wff.append(dict(
            w1=din(f"w1_{l}", [32, 128, 1024]),
            w2=din(f"w2_{l}", [32, 128, 1024]),
            b1=din(f"b1c_{l}", [128, 32]),
            b2=din(f"b2c_{l}", [128, 8])))
    wq1 = din("wq1", [2, 128, 1024])
    bq1 = din("bq1", [128, 2])
    wqc1 = din("wqc1", [7, 128, 256])
    bqc1 = din("bqc1", [128, 7])
    wqr1 = din("wqr1", [1, 128, 1024])
    bqr1 = din("bqr1", [128, 1])
    wkv1 = din("wkv1", [2, 128, 1024])
    bkv1 = din("bkv1", [128, 2])
    wkr1 = din("wkr1", [1, 128, 1024])
    bkr1 = din("bkr1", [128, 1])
    wkc1 = din("wkc1", [7, 128, 256])
    bkc1 = din("bkc1", [128, 7])
    wv1 = din("wv1", [8, 128, 256])
    bv1 = din("bv1", [128, 8])
    wo1 = din("wo1", [8, 128, 1024])
    bo1 = din("bo1", [128, 8])
    lam_in = din("lamv", [1, 256])
    identc = din("identc", [128, 128])
    permc0 = din("permc0", [128, 128])
    permc1 = din("permc1", [128, 128])
    maskc = din("maskc", [128, 128])
    ropec0 = din("ropec0", [2, 128, T])
    ropec1 = din("ropec1", [2, 128, T])
    w2q_sc = [nc.dram_tensor(f"w2qsc_{l}", [32, 128, 1024], F8,
                             kind="Internal").ap() for l in range(2)]

    es = ExitStack()

    def sbt(name, shape, dtype):
        return es.enter_context(nc.sbuf_tensor(name, shape, dtype))

    x_res = sbt("x_res", [128, 16 * 1024], F32)
    xnT = sbt("xnT", [128, 8 * T], BF16)
    arA = sbt("arA", [128, 8 * T], BF16)
    aT = arA[:, 0:8 * T]
    hqv = arA.bitcast(F8)
    arB = sbt("arB", [128, 6 * T], BF16)
    cqb = arB[:, 0:2 * T]
    ckvb = arB[:, 2 * T:4 * T]
    qh = sbt("qh", [128, T], BF16)
    kh = sbt("kh", [128, T], BF16)
    VS = 136
    vtok = sbt("vtok", [128, 16 * VS], BF16)
    p0 = sbt("p0", [128, 2 * T], BF16)
    p1 = sbt("p1", [128, 2 * T], BF16)
    pcb = sbt("pcb", [128, 1024], BF16)
    NSCR = 8
    scr = arB.bitcast(F32)
    scri = arB.bitcast(I32)[:, 4096:5120]
    scrb = arB[:, 10240:11264]
    rmsr = sbt("rmsr", [128, 512], F32)
    rmsb = sbt("rmsb", [128, 512], F32)
    wbA = sbt("wbA", [128, 1024], BF16)
    wbB = sbt("wbB", [128, 1024], BF16)
    wbC = sbt("wbC", [128, 1024], BF16)
    w2qs = sbt("w2qs", [128, 2 * 1024], F8)
    ident = sbt("ident", [128, 128], BF16)
    perm0 = sbt("perm0", [128, 128], BF16)
    perm1 = sbt("perm1", [128, 128], BF16)
    maskb = sbt("maskb", [128, 128], BF16)
    ropeC = sbt("ropeC", [128, T], BF16)
    ropeS = sbt("ropeS", [128, T], BF16)
    onesq = sbt("onesq", [128, 128], BF16)
    b1cols = [sbt(f"b1col{l}", [128, 32], F32) for l in range(2)]
    b2cols = [sbt(f"b2col{l}", [128, 8], F32) for l in range(2)]
    b2qcols = [sbt(f"b2qcol{l}", [128, 8], F32) for l in range(2)]
    bias1 = sbt("bias1", [128, 12], F32)
    bias2 = sbt("bias2", [128, 12], F32)
    bvcol = sbt("bvcol", [128, 8], F32)
    bocol = sbt("bocol", [128, 8], F32)
    lamt = sbt("lamt", [1, 256], F32)
    smlb = sbt("smlb", [1, 8], BF16)
    sml = sbt("sml", [128, 128], F32)
    sml2 = sbt("sml2", [128, 32], F32)
    cst = sbt("cst", [128, 16], F32)
    stats = sbt("stats", [128, 12], F32)
    aggr = sbt("aggr", [128, 2 * 16], F32)
    rstd = sbt("rstd", [128, 16], F32)
    bcols = sbt("bcols", [128, 16], F32)

    psA = es.enter_context(nc.psum_tensor("psA", [128, 512], F32))
    psB = es.enter_context(nc.psum_tensor("psB", [128, 512], F32))
    psC = es.enter_context(nc.psum_tensor("psC", [128, 512], F32))
    psD = es.enter_context(nc.psum_tensor("psD", [128, 512], F32))
    psT = es.enter_context(nc.psum_tensor("psT", [128, 1024], BF16))
    psU = es.enter_context(nc.psum_tensor("psU", [128, 1024], BF16))
    psS = es.enter_context(nc.psum_tensor("psS", [128, 512], F32))
    psR = es.enter_context(nc.psum_tensor("psR", [128, 512], F32))
    PSAB = [psA, psB]
    PSCD = [psC, psD]
    PS4 = [psA, psB, psC, psD]

    sem_names = ["T", "V", "S", "G", "Dc", "Dg", "DA", "DB", "DC",
                 "W2A", "W2B", "DZ0", "DZ1", "DS0", "DS1",
                 "Dr", "Do"]
    H = {n: es.enter_context(nc.semaphore(n)) for n in sem_names}

    def T_(fn, waits=()):
        return P.op("tensor", fn, waits, inc="T")

    def T0_(fn, waits=()):
        return P.op("tensor", fn, waits, inc=None)

    def V_(fn, waits=()):
        return P.op("vector", fn, waits, inc="V")

    def S_(fn, waits=()):
        return P.op("scalar", fn, waits, inc="S")

    def G_(fn, waits=()):
        return P.op("gpsimd", fn, waits, inc="G")

    def DMA(eng, sem, fn, waits=()):
        return P.op(eng, fn, waits, inc=sem, inc_step=16)

    def DMAw(fn, waits=(), sem="Dc"):
        return P.op("sync", fn, waits, inc=sem, inc_step=16)

    def DMAg(fn, waits=(), sem="Dg"):
        return P.op("gpsimd", fn, waits, inc=sem, inc_step=16)

    def DMAo(fn, waits=()):
        return P.op("sync", fn, waits, inc="Do", inc_step=16)

    last_use = {}

    def ruse(*keys):
        out = []
        for k in keys:
            out.extend(last_use.get(k, []))
        return out

    def rdone(key, toks):
        if not isinstance(toks, list):
            toks = [toks]
        last_use[key] = [t for t in toks if t is not None]

    def radd(key, tok):
        if tok is not None:
            last_use.setdefault(key, []).append(tok)

    def emit_bcast2(srcs, dsts, waits):
        tm = None
        for i, s2 in enumerate(srcs):
            tm = T_(lambda e, h, s2=s2, i=i: e.matmul(
                psS[:, 0:2], onesq[0:1, 0:128], s2,
                start=(i == 0), stop=(i == len(srcs) - 1)),
                waits=list(waits) + [t_blkr] + (ruse("psS")
                                               if i == 0 else []))
        out = []
        for j, d in enumerate(dsts):
            if d is None:
                continue
            out.append(V_(lambda e, h, j=j, d=d: e.tensor_copy(
                d, psS[:, j:j + 1]), waits=[tm]))
        rdone("psS", out)
        return out[-1]

    CSTV = {}
    CSTV_FROZEN = [False]

    def cstcol(val):
        val = float(val)
        if val not in CSTV:
            assert not CSTV_FROZEN[0], f"const col {val} not preregistered"
            i = len(CSTV)
            assert i < 16, "too many const cols"
            CSTV[val] = i
            V_(lambda e, h, i=i, val=val: e.memset(cst[:, i:i + 1], val))
        return cst[:, CSTV[val]:CSTV[val] + 1]

    t_ident = DMAg(lambda e, h: e.dma_start(ident[:], identc))
    t_perm0 = DMAg(lambda e, h: e.dma_start(perm0[:], permc0))
    t_perm1 = DMAg(lambda e, h: e.dma_start(perm1[:], permc1))
    t_ropeC = DMAg(lambda e, h: e.dma_start(ropeC[:], ropec0[0]))
    t_ropeS = DMAg(lambda e, h: e.dma_start(ropeS[:], ropec0[1]))
    t_mask = DMAg(lambda e, h: e.dma_start(maskb[:], maskc))
    t_blkr = V_(lambda e, h: e.memset(onesq[:], 1.0))
    t_blkc = t_blkr
    V_(lambda e, h: e.memset(smlb[:], 0.0))
    t_lam = DMAw(lambda e, h: e.dma_start(lamt[:], lam_in))
    t_b1 = [DMAw(lambda e, h, l=l: e.dma_start(b1cols[l][:], wff[l]["b1"]))
            for l in range(2)]
    t_b2 = [DMAw(lambda e, h, l=l: e.dma_start(b2cols[l][:], wff[l]["b2"]))
            for l in range(2)]
    x3 = x_in.rearrange("(n p) c -> n p c", p=128)
    y3 = y_out.rearrange("(n p) c -> n p c", p=128)
    t_x = [DMAw(lambda e, h, i=i: e.dma_start(
        x_res[:, i * 1024:(i + 1) * 1024], x3[i])) for i in range(16)]
    t_bias_l = [
        DMAw(lambda e, h: e.dma_start(bias1[:, 0:2], bq1)),
        DMAw(lambda e, h: e.dma_start(bias1[:, 2:9], bqc1)),
        DMAw(lambda e, h: e.dma_start(bias1[:, 9:10], bqr1)),
        DMAw(lambda e, h: e.dma_start(bias2[:, 0:2], bkv1)),
        DMAw(lambda e, h: e.dma_start(bias2[:, 2:9], bkc1)),
        DMAw(lambda e, h: e.dma_start(bias2[:, 9:10], bkr1)),
        DMAw(lambda e, h: e.dma_start(bvcol[:], bv1)),
        DMAw(lambda e, h: e.dma_start(bocol[:], bo1)),
    ]
    for kb in range(16):
        V_(lambda e, h, kb=kb: e.memset(vtok[:, kb * VS + 64:kb * VS + 65],
                                        1.0))
    for _v in (0.0, 2.0, 4096.0, LN_EPS, RMS_EPS, 2.0 ** -60, -8.0,
               -math.log(8.0), math.log(1.0 - LAMBDA_INIT),
               3 * LN2, -3 * LN2, 4 * LN2, -4 * LN2):
        cstcol(_v)
    CSTV_FROZEN[0] = True
    tok_c = P.op("vector", lambda e, h: None,
                 waits=[("Dc", P.sem_counts["Dc"]),
                        ("Dg", P.sem_counts.get("Dg", 0))], inc="V")
    P.default_waits = [tok_c]
    t_ident = t_perm0 = t_perm1 = t_ropeC = t_ropeS = tok_c
    t_mask = t_blkc = t_blkr = t_lam = t_ones = tok_c
    t_b1 = [tok_c, tok_c]
    t_b2 = [tok_c, tok_c]
    t_x = [tok_c]
    t_bias = [tok_c]

    def emit_layernorm(dep):
        tA = []
        for i in range(16):
            xt = x_res[:, i * 1024:(i + 1) * 1024]
            tsa = V_(lambda e, h, xt=xt: e.bn_stats(stats[:, 0:6],
                                                    xt[:, 0:512]),
                     waits=list(dep) + ruse("aggr", "stats"))
            tsb = V_(lambda e, h, xt=xt: e.bn_stats(stats[:, 6:12],
                                                    xt[:, 512:1024]),
                     waits=[tsa])
            ta_ = V_(lambda e, h, i=i: e.bn_aggr(
                aggr[:, 2 * i:2 * i + 2], stats[:, 0:12]), waits=[tsb])
            rdone("stats", ta_)
            tA.append(ta_)
        var_view = aggr[:, 1:32:2]
        tl = S_(lambda e, h: e.activation(rstd[:], var_view, AF.Ln,
                                          bias=cstcol(LN_EPS)),
                waits=[tA[-1]])
        tl = S_(lambda e, h: e.activation(rstd[:], rstd[:], AF.Exp,
                                          scale=-0.5), waits=[tl])
        rdone("aggr", tl)
        toks = []
        for i in range(16):
            xt = x_res[:, i * 1024:(i + 1) * 1024]
            tv = V_(lambda e, h, xt=xt, i=i: e.tensor_scalar(
                pcb[:, 0:1024], xt, aggr[:, 2 * i:2 * i + 1],
                rstd[:, i:i + 1], ALU.subtract, ALU.mult),
                waits=[tl] + ruse("pcb"))
            last = None
            for c in range(8):
                pd, key = (psT, "psTg") if c % 2 == 0 \
                    else (psU, ("psU", 0))
                tt = T_(lambda e, h, c=c, pd=pd: e.transpose(
                    pd[:, 0:128], pcb[:, c * 128:(c + 1) * 128], ident[:]),
                    waits=[tv, t_ident] + ruse(key))
                tc = V_(lambda e, h, c=c, i=i, pd=pd: e.tensor_copy(
                    xnT[:, c * T + i * 128: c * T + i * 128 + 128],
                    pd[:, 0:128]), waits=[tt])
                rdone(key, tc)
                last = tc
                toks.append(tc)
            rdone("pcb", last)
        return toks

    qpar = [0]
    chain_done = {0: None, 1: None}

    def emit_quant(z_ap, w, out_ap, kind, bias_col=None, relu=False,
                   skip_stage1=False, waits=()):
        man, mine = (3, -6) if kind == "e4m3" else (10, -14)
        p = qpar[0]
        qpar[0] ^= 1

        def fs(j):
            return scr[:, (4 * p + j) * 512:(4 * p + j) * 512 + w]
        iA = scri[:, p * 512:p * 512 + w]
        SCB = scrb[:, p * 512:p * 512 + w]
        wl = list(waits)
        if chain_done[p] is not None:
            wl = wl + [chain_done[p]]
        if skip_stage1:
            A = z_ap
            z_done = None
            pre = wl
        else:
            bcol = bias_col if bias_col is not None else cstcol(0.0)
            t1 = S_(lambda e, h: e.activation(
                fs(0), z_ap, AF.Relu if relu else AF.Identity, bias=bcol),
                waits=wl)
            A = fs(0)
            z_done = t1
            pre = [t1]
        if relu:
            lin = A
        else:
            tab = S_(lambda e, h: e.activation(fs(1), A, AF.Abs), waits=pre)
            lin = fs(1)
            pre = [tab]
        tl = S_(lambda e, h: e.activation(fs(2), lin, AF.Ln,
                                          scale=float(2.0 ** 46),
                                          bias=cstcol(2.0)), waits=pre)
        ta = S_(lambda e, h: e.activation(iA, fs(2), AF.Identity,
                                          scale=INV_LN2), waits=[tl])
        tb = V_(lambda e, h: e.scalar_tensor_tensor(
            fs(1), iA, LN2, fs(2), ALU.mult, ALU.is_gt), waits=[ta])
        tn = V_(lambda e, h: e.scalar_tensor_tensor(
            fs(3), fs(1), float(46 - mine), iA, ALU.add, ALU.subtract),
            waits=[tb])
        tcl = V_(lambda e, h: e.tensor_scalar(
            fs(1), fs(3), float(-2 * mine), None, ALU.min), waits=[tn])
        tsc = S_(lambda e, h: e.activation(
            SCB, fs(1), AF.Exp, scale=-LN2,
            bias=cstcol((-mine - man) * LN2)), waits=[tcl])
        trs = S_(lambda e, h: e.activation(
            fs(3), fs(1), AF.Exp, scale=LN2,
            bias=cstcol((mine + man) * LN2)), waits=[tcl])
        tt_ = V_(lambda e, h: e.tensor_tensor(fs(2), A, fs(3), ALU.mult),
                 waits=[trs])
        z_done2 = tt_ if skip_stage1 else z_done
        ta2 = S_(lambda e, h: e.activation(iA, fs(2), AF.Identity,
                                           bias=cstcol(4096.0)),
                 waits=[tt_])
        tc2 = V_(lambda e, h: e.scalar_tensor_tensor(
            fs(1), iA, -4095.5, fs(2), ALU.add, ALU.is_le), waits=[ta2])
        tr = V_(lambda e, h: e.tensor_tensor(fs(3), iA, fs(1), ALU.add),
                waits=[tc2])
        tq = V_(lambda e, h: e.scalar_tensor_tensor(
            out_ap, fs(3), -4096.0, SCB, ALU.add, ALU.mult),
            waits=[tr, tsc])
        chain_done[p] = tq
        return tq, z_done2

    W2R = {}

    def w2q_gen(l):
        toks = []
        for hc in range(32):
            done = []
            for half in range(2):
                pn = qpar[0]
                zin = scr[:, (4 * pn) * 512:(4 * pn) * 512 + 512]
                wl = [chain_done[pn]] if chain_done[pn] is not None else []
                tw = DMAw(lambda e, h, hc=hc, half=half, zin=zin, l=l:
                          e.dma_start(zin, wff[l]["w2"][hc][:,
                                      half * 512:half * 512 + 512]),
                          waits=wl, sem="DZ0" if pn == 0 else "DZ1")
                tq, _ = emit_quant(
                    zin, 512,
                    w2qs[:, half * 512:half * 512 + 512],
                    "e4m3", skip_stage1=True,
                    waits=[tw] + ruse("w2qs"))
                done.append(tq)
            ts_ = DMAw(lambda e, h, hc=hc, l=l: e.dma_start(
                w2q_sc[l][hc], w2qs[:, 0:1024]), waits=done,
                sem="DS0" if l == 0 else "DS1")
            rdone("w2qs", ts_)
            toks.append(ts_)
            W2R[l] = ts_
            yield

    def emit_lambda():
        t1 = V_(lambda e, h: e.tensor_tensor(
            sml[0:1, 0:128], lamt[0:1, 0:128], lamt[0:1, 128:256],
            ALU.mult), waits=[t_lam])
        t2a = V_(lambda e, h: e.reduce_sum(sml2[0:1, 0:1], sml[0:1, 0:64],
                                           AX.X), waits=[t1])
        t2b = V_(lambda e, h: e.reduce_sum(sml2[0:1, 1:2],
                                           sml[0:1, 64:128], AX.X),
                 waits=[t1])
        t3 = S_(lambda e, h: e.activation(sml2[0:1, 2:4], sml2[0:1, 0:2],
                                          AF.Exp), waits=[t2a, t2b])
        t6 = V_(lambda e, h: e.scalar_tensor_tensor(
            sml2[0:1, 4:5], sml2[0:1, 3:4], -LAMBDA_INIT,
            sml2[0:1, 2:3], ALU.add, ALU.subtract), waits=[t3])
        th = V_(lambda e, h: e.tensor_copy(smlb[0:1, 0:1],
                                           sml2[0:1, 4:5]), waits=[t6])
        tl_ = V_(lambda e, h: e.scalar_tensor_tensor(
            smlb[0:1, 2:3], smlb[0:1, 0:1], -1.0, sml2[0:1, 4:5],
            ALU.mult, ALU.add), waits=[th])
        t7 = emit_bcast2([smlb[0:1, 0:2], smlb[0:1, 2:4]],
                         [sml2[:, 5:6], None], [tl_])
        return t7

    NLAM = sml2[:, 5:6]

    def l0_prep_side(side, wbuf, tw, rawbuf, outbuf, xn_dep):
        out_toks = []
        for tc in range(4):
            sl = PSAB[tc % 2][:, 0:512]
            pk = ("pab", tc % 2)
            mt = None
            for ic in range(8):
                mt = (T_ if ic == 7 else T0_)(
                    lambda e, h, ic=ic, tc=tc, sl=sl, wbuf=wbuf:
                        e.matmul(sl, wbuf[:, ic * 128:(ic + 1) * 128],
                                 xnT[:, ic * T + tc * 512:
                                     ic * T + tc * 512 + 512],
                                 start=(ic == 0), stop=(ic == 7)),
                        waits=(([tw] + xn_dep + ruse(pk))
                               if ic == 0 else []))
            if PREP_STAGE <= 0:
                rdone(pk, mt)
                out_toks.append(mt)
                continue
            t_raw = V_(lambda e, h, tc=tc, sl=sl, rawbuf=rawbuf:
                       e.tensor_copy(rawbuf[:, tc * 512:(tc + 1) * 512],
                                     sl), waits=[mt])
            sq = outbuf[:, tc * 512:(tc + 1) * 512]
            t_sq = S_(lambda e, h, sl=sl, sq=sq: e.activation(
                sq, sl, AF.Square), waits=[mt, t_raw])
            rdone(pk, [t_raw, t_sq])
            if PREP_STAGE <= 1:
                out_toks.append(t_sq)
                rdone("pcb", t_raw)
                continue
            t_r0 = T_(lambda e, h, sq=sq: e.matmul(
                psS[:, 0:512], onesq[0:64, 0:128], sq[0:64, :],
                start=True, stop=True),
                waits=[t_sq, t_blkc] + ruse("psS"))
            t_r1 = T_(lambda e, h, sq=sq: e.matmul(
                psR[:, 0:512], onesq[64:128, 0:128], sq[64:128, :],
                start=True, stop=True),
                waits=[t_sq] + ruse("psR"))
            bias_r = cstcol(-math.log(8.0)) if side == 0 else cstcol(0.0)
            tu0 = S_(lambda e, h: e.activation(
                rmsr[0:64, 0:512], psS[0:64, 0:512], AF.Ln,
                scale=1.0 / 64.0, bias=cstcol(RMS_EPS)[0:64, :]),
                waits=[t_r0] + ruse("rmsr"))
            rdone("psS", tu0)
            tu1 = S_(lambda e, h: e.activation(
                rmsr[64:128, 0:512], psR[64:128, 0:512], AF.Ln,
                scale=1.0 / 64.0, bias=cstcol(RMS_EPS)[64:128, :]),
                waits=[t_r1])
            rdone("psR", tu1)
            t_bc = S_(lambda e, h, bias_r=bias_r: e.activation(
                rmsb[:, 0:512], rmsr[:, 0:512], AF.Exp, scale=-0.5,
                bias=bias_r), waits=[tu0, tu1] + ruse("rmsb"))
            rdone("rmsr", t_bc)
            if PREP_STAGE <= 2:
                out_toks.append(t_bc)
                continue
            t_sw = T_(lambda e, h, tc=tc, rawbuf=rawbuf: e.matmul(
                psR[:, 0:512], perm0[:],
                rawbuf[:, tc * 512:(tc + 1) * 512], start=True, stop=True),
                waits=[t_raw, t_perm0] + ruse("psR"))
            tm1 = V_(lambda e, h, tc=tc, rawbuf=rawbuf: e.tensor_tensor(
                pcb[:, 0:512], rawbuf[:, tc * 512:(tc + 1) * 512],
                ropeC[:, tc * 512:(tc + 1) * 512], ALU.mult),
                waits=[t_raw, t_ropeC] + ruse("pcb"))
            ts2 = V_(lambda e, h, tc=tc: e.tensor_tensor(
                pcb[:, 512:1024], psR[:, 0:512],
                ropeS[:, tc * 512:(tc + 1) * 512], ALU.mult),
                waits=[t_sw, t_ropeS])
            rdone("psR", ts2)
            tsum = V_(lambda e, h: e.tensor_tensor(
                pcb[:, 0:512], pcb[:, 0:512], pcb[:, 512:1024], ALU.add),
                waits=[tm1, ts2])
            tq = V_(lambda e, h, tc=tc, outbuf=outbuf: e.tensor_tensor(
                outbuf[:, tc * 512:(tc + 1) * 512], pcb[:, 0:512],
                rmsb[:, 0:512], ALU.mult), waits=[tsum, t_bc])
            rdone("pcb", tq)
            rdone("rmsb", tq)
            radd("ropes", tq)
            out_toks.append(tq)
        return out_toks

    def l0_head(hd, nlam_tok, xn_dep, out_toks, pump=None):
        tw_q = DMAg(lambda e, h, hd=hd: e.dma_start(wbA[:], wq0[hd]),
                    waits=ruse("wbA"), sem="DA")
        tw_k = DMAg(lambda e, h, hd=hd: e.dma_start(wbB[:], wk0[hd]),
                    waits=ruse("wbB"), sem="DB")
        tw_v = DMAg(lambda e, h, hd=hd: e.dma_start(wbC[:], wv0[hd]),
                    waits=ruse("wbC"), sem="DC")
        tqs = l0_prep_side(0, wbA, tw_q, p0, qh, xn_dep)
        rdone("wbA", tqs[-1])
        tks = l0_prep_side(1, wbB, tw_k, p1, kh, xn_dep)
        rdone("wbB", tks[-1])
        if ATTN_STAGE <= 1:
            out_toks.extend(tqs + tks)
            return out_toks
        vt = []
        for qt in range(16):
            sl = PSCD[qt % 2][:, 0:128]
            pk = ("pcd", qt % 2)
            mt = None
            for ic in range(8):
                mt = (T_ if ic == 7 else T0_)(
                    lambda e, h, ic=ic, qt=qt, sl=sl: e.matmul(
                    sl, xnT[:, ic * T + qt * 128:ic * T + qt * 128 + 128],
                    wbC[:, ic * 128:(ic + 1) * 128],
                    start=(ic == 0), stop=(ic == 7)),
                    waits=(([tw_v] + xn_dep + ruse(pk))
                           if ic == 0 else []))
            vdst = vtok[:, qt * VS:qt * VS + 130].rearrange(
                "p (a b) -> p a b", b=65)[:, :, 0:64]
            vsrc = sl[:, 0:128].rearrange("p (a b) -> p a b", b=64)
            tv = V_(lambda e, h, vdst=vdst, vsrc=vsrc: e.tensor_copy(
                vdst, vsrc), waits=[mt])
            rdone(pk, tv)
            vt.append(tv)
        rdone("wbC", vt[-1])
        if ATTN_STAGE <= 2:
            out_toks.extend(vt)
            return out_toks

        def stageA(qt):
            par = qt % 2
            exp_by = []
            for side, pbuf in ((0, p0[:, par * T:par * T + T]),
                               (1, p1[:, par * T:par * T + T])):
                side_toks = []
                ng = (qt + 4) // 4
                for g in range(ng):
                    jn = min(4, qt + 1 - 4 * g)
                    sl = PSAB[g % 2][:, 0:512]
                    pk = ("pab", g % 2)
                    mlast = None
                    for j in range(jn):
                        kb = 4 * g + j
                        mlast = (T_ if j == jn - 1 else T0_)(
                            lambda e, h, j=j, kb=kb, side=side, qt=qt,
                            sl=sl: e.matmul(
                            sl[:, j * 128:(j + 1) * 128],
                            kh[64 * side:64 * side + 64,
                               kb * 128:(kb + 1) * 128],
                            qh[64 * side:64 * side + 64,
                               qt * 128:(qt + 1) * 128],
                            start=True, stop=True),
                            waits=(([tqs[-1], tks[-1]] + ruse(pk))
                                   if j == 0 else []))
                    wl = [mlast, t_mask]
                    if g == ng - 1:
                        jd = qt - 4 * g
                        tmask = V_(lambda e, h, jd=jd, sl=sl:
                                   e.tensor_tensor(
                            sl[:, jd * 128:(jd + 1) * 128],
                            sl[:, jd * 128:(jd + 1) * 128],
                            maskb[:], ALU.add), waits=wl)
                        wl = [tmask]
                    texp = S_(lambda e, h, jn=jn, g=g, pbuf=pbuf,
                              sl=sl: e.activation(
                        pbuf[:, g * 512:g * 512 + jn * 128],
                        sl[:, 0:jn * 128], AF.Exp, bias=cstcol(-8.0)),
                        waits=wl + ruse(("pex", side, par)))
                    rdone(pk, texp)
                    side_toks.append(texp)
                exp_by.append(side_toks)
            return exp_by

        def stageB(qt, exp_by):
            par = qt % 2
            pvps = PSCD[par]
            pvk = ("pcd", par)
            pvt = None
            for side, base in ((0, 0), (1, 256)):
                pbuf = (p0 if side == 0 else p1)[:, par * T:par * T + T]
                for kb in range(qt + 1):
                    pvt = (T_ if kb == qt else T0_)(
                        lambda e, h, kb=kb, qt=qt, base=base, pbuf=pbuf:
                        e.matmul(
                        pvps[:, base:base + 129],
                        pbuf[:, kb * 128:(kb + 1) * 128],
                        vtok[:, kb * VS:kb * VS + 129],
                        start=(kb == 0), stop=(kb == qt)),
                        waits=[exp_by[side][kb // 4], vt[kb]] +
                        (ruse(pvk) if kb == 0 and side == 0 else []))
                rdone(("pex", side, par), pvt)
            c8 = 8 + 16 * par
            zsrc = pvps[:, 0:512].rearrange(
                "p (a b) -> p a b", b=256)[:, :, 64:65]
            zdst = sml2[:, c8:c8 + 2].rearrange(
                "p (a b) -> p a b", b=1)
            rz = V_(lambda e, h, zsrc=zsrc, zdst=zdst: e.reciprocal(
                zdst, zsrc), waits=[pvt])
            pv1 = pvps[:, 256:386].rearrange(
                "p (a b) -> p a b", b=65)[:, :, 0:64]
            pv0 = pvps[:, 0:130].rearrange(
                "p (a b) -> p a b", b=65)[:, :, 0:64]
            tmp = pcb[:, 128:256].rearrange("p (a b) -> p a b", b=64)
            cmb = pcb[:, 0:128].rearrange("p (a b) -> p a b", b=64)
            t1 = V_(lambda e, h, c8=c8, pv1=pv1, tmp=tmp: e.tensor_scalar(
                tmp, pv1, sml2[:, c8 + 1:c8 + 2],
                NLAM, ALU.mult, ALU.mult),
                waits=[rz, nlam_tok] + ruse("pcb"))
            t2b = V_(lambda e, h, c8=c8, pv0=pv0, tmp=tmp, cmb=cmb:
                     e.scalar_tensor_tensor(
                cmb, pv0, sml2[:, c8:c8 + 1],
                tmp, ALU.mult, ALU.add), waits=[t1])
            rdone(pvk, t2b)
            tst = V_(lambda e, h: e.bn_stats(
                stats[:, 0:6], pcb[:, 0:128]),
                waits=[t2b] + ruse("aggr0", "stats"))
            tag = V_(lambda e, h: e.bn_aggr(aggr[:, 0:2], stats[:, 0:6]),
                     waits=[tst])
            rdone("stats", tag)
            tu = S_(lambda e, h: e.activation(sml2[:, 11:12], aggr[:, 1:2],
                                              AF.Ln, bias=cstcol(LN_EPS)),
                    waits=[tag] + ruse("lnsml"))
            tr_ = S_(lambda e, h: e.activation(
                sml2[:, 12:13], sml2[:, 11:12], AF.Exp, scale=-0.5,
                bias=cstcol(math.log(1.0 - LAMBDA_INIT))), waits=[tu])
            tln = V_(lambda e, h: e.tensor_scalar(
                pcb[:, 256:384], pcb[:, 0:128], aggr[:, 0:1],
                sml2[:, 12:13], ALU.subtract, ALU.mult),
                waits=[tr_])
            rdone("aggr0", tln)
            rdone("lnsml", tln)
            ttr = T_(lambda e, h: e.transpose(psU[:, 0:128],
                                              pcb[:, 256:384],
                                              ident[:]),
                     waits=[tln] + ruse(("psU", 0)))
            tev = V_(lambda e, h, hd=hd, qt=qt: e.tensor_copy(
                aT[:, hd * T + qt * 128:hd * T + qt * 128 + 128],
                psU[:, 0:128]), waits=[ttr])
            rdone(("psU", 0), tev)
            rdone("pcb", [tln, ttr, tev])
            out_toks.append(tev)

        prev = None
        for qt in range(16):
            eb = stageA(qt)
            if prev is not None:
                stageB(qt - 1, prev)
                if pump is not None and qt % 2 == 0:
                    pump(1)
            prev = eb
        stageB(15, prev)
        if pump is not None:
            pump(1)
        return out_toks

    def emit_out_proj(wdram, src, src_toks, bias_colst, n_oc, xadd_toks):
        for oc in range(n_oc):
            par = oc % 2
            wb = wbA if par == 0 else wbB
            tw = DMAg(lambda e, h, oc=oc, wb=wb: e.dma_start(
                wb[:], wdram[oc]),
                waits=ruse("wbA" if par == 0 else "wbB"),
                sem="DA" if par == 0 else "DB")
            mlast = None
            for tc in range(4):
                qd = tc % 2
                sl = PSAB[qd][:, 0:512]
                pk = ("pab", qd)
                mt = None
                for ic in range(8):
                    mt = (T_ if ic == 7 else T0_)(
                        lambda e, h, ic=ic, tc=tc, sl=sl, wb=wb:
                            e.matmul(sl,
                                     wb[:, ic * 128:(ic + 1) * 128],
                                     src[:, ic * T + tc * 512:
                                         ic * T + tc * 512 + 512],
                                     start=(ic == 0), stop=(ic == 7)),
                            waits=(([tw] + src_toks + ruse(pk))
                                   if ic == 0 else []))
                mlast = mt
                if bias_colst is not None:
                    tev = V_(lambda e, h, sl=sl, oc=oc: e.tensor_scalar(
                        pcb[:, 0:512], sl, bias_colst[:, oc:oc + 1], None,
                        ALU.add), waits=[mt] + ruse("pcb"))
                else:
                    tev = V_(lambda e, h, sl=sl: e.tensor_copy(
                        pcb[:, 0:512], sl), waits=[mt] + ruse("pcb"))
                rdone(pk, tev)
                lastadd = None
                for j in range(4):
                    ttr = T_(lambda e, h, j=j: e.transpose(
                        psU[:, 128:256], pcb[:, j * 128:(j + 1) * 128],
                        ident[:]), waits=[tev] + ruse(("psU", 1)))
                    tqi = tc * 4 + j
                    tadd = V_(lambda e, h, tqi=tqi, oc=oc:
                              e.scalar_tensor_tensor(
                        x_res[:, tqi * 1024 + oc * 128:
                              tqi * 1024 + oc * 128 + 128],
                        psU[:, 128:256], 0.0,
                        x_res[:, tqi * 1024 + oc * 128:
                              tqi * 1024 + oc * 128 + 128],
                        ALU.add, ALU.add), waits=[ttr])
                    rdone(("psU", 1), tadd)
                    lastadd = tadd
                    xadd_toks.append(tadd)
                rdone("pcb", lastadd)
            rdone("wbA" if par == 0 else "wbB", mlast)
        return xadd_toks

    def emit_ffn(l, xn_dep, xadd_toks, w2ready):
        tb2q, _ = emit_quant(b2cols[l][:, 0:8], 8, b2qcols[l][:, 0:8],
                             "fp16", skip_stage1=True, waits=[t_b2[l]])
        for blk in range(2):
            hq_toks = {}
            for hc in range(32):
                par = hc % 2
                wb = wbA if par == 0 else wbB
                tw = DMAg(lambda e, h, hc=hc, wb=wb, l=l: e.dma_start(
                    wb[:], wff[l]["w1"][hc]),
                    waits=ruse("wbA" if par == 0 else "wbB"),
                    sem="DA" if par == 0 else "DB")
                hts = []
                mm_l = None
                for half in range(2):
                    tok0 = blk * 1024 + half * 512
                    qd = (2 * hc + half) % 2
                    sl = PSAB[qd][:, 0:512]
                    pk = ("pab", qd)
                    mt = None
                    for ic in range(8):
                        mt = (T_ if ic == 7 else T0_)(
                            lambda e, h, ic=ic, sl=sl, wb=wb,
                                tok0=tok0: e.matmul(
                            sl, wb[:, ic * 128:(ic + 1) * 128],
                            xnT[:, ic * T + tok0:ic * T + tok0 + 512],
                            start=(ic == 0), stop=(ic == 7)),
                            waits=(([tw] + xn_dep + ruse(pk))
                                   if ic == 0 else []))
                    mm_l = mt
                    out_sl = hqv[:, hc * 1024 + half * 512:
                                 hc * 1024 + half * 512 + 512]
                    tqd, zd = emit_quant(sl, 512, out_sl, "e4m3",
                                         bias_col=b1cols[l][:, hc:hc + 1],
                                         relu=True,
                                         waits=[mt, t_b1[l]] +
                                         ruse(("hqw", hc)))
                    rdone(pk, zd)
                    hts.append(tqd)
                rdone("wbA" if par == 0 else "wbB", mm_l)
                hq_toks[hc] = hts
                rdone(("hqw", hc), hts)
            for tc2 in range(2):
                tok_in_blk = tc2 * 512
                for og in range(2):
                    mlast = [None] * 4
                    for hc in range(32):
                        par = hc % 2
                        tws = DMAw(lambda e, h, hc=hc, par=par, l=l:
                                   e.dma_start(
                            w2qs[:, par * 1024:par * 1024 + 1024],
                            w2q_sc[l][hc]),
                            waits=[w2ready] + ruse(("w2qsb", par),
                                                    "w2qs"),
                            sem="W2A" if par == 0 else "W2B")
                        for oc4 in range(4):
                            oc = og * 4 + oc4
                            mlast[oc4] = (T_ if (hc == 31 or oc4 == 3) else T0_)(
                                lambda e, h, oc4=oc4, oc=oc, par=par,
                                hc=hc, tok_in_blk=tok_in_blk: e.matmul(
                                    PS4[oc4][:, 0:512],
                                    w2qs[:, par * 1024 + oc * 128:
                                         par * 1024 + oc * 128 + 128],
                                    hqv[:, hc * 1024 + tok_in_blk:
                                        hc * 1024 + tok_in_blk + 512],
                                    start=(hc == 0), stop=(hc == 31)),
                                waits=([tws] + hq_toks[hc] +
                                       (ruse(("pab", 0), ("pab", 1),
                                             ("pcd", 0), ("pcd", 1))
                                        if hc == 0 else [])))
                        rdone(("w2qsb", par), mlast[3])
                        radd("w2qs", mlast[3])
                    for oc4 in range(4):
                        oc = og * 4 + oc4
                        sl = PS4[oc4][:, 0:512]
                        pk = [("pab", 0), ("pab", 1), ("pcd", 0),
                              ("pcd", 1)][oc4]
                        tqd, zd = emit_quant(
                            sl, 512, pcb[:, 0:512], "fp16",
                            bias_col=b2qcols[l][:, oc:oc + 1], relu=False,
                            waits=[mlast[oc4], tb2q] + ruse("pcb"))
                        rdone(pk, zd)
                        ttr = None
                        for j in range(4):
                            ttr = (T_ if j == 3 else T0_)(
                                lambda e, h, j=j: e.transpose(
                                psU[:, 128 + j * 128:256 + j * 128],
                                pcb[:, j * 128:(j + 1) * 128], ident[:]),
                                waits=(([tqd] + ruse(("psU", 1)))
                                       if j == 0 else []))
                        tq0 = blk * 8 + tc2 * 4
                        xrv = x_res[:, 0:16384].rearrange(
                            "p (a b) -> p a b", b=1024)[
                            :, tq0:tq0 + 4, oc * 128:oc * 128 + 128]
                        puv = psU[:, 128:640].rearrange(
                            "p (a b) -> p a b", b=128)
                        tadd = V_(lambda e, h, xrv=xrv, puv=puv:
                                  e.scalar_tensor_tensor(
                            xrv, puv, 0.0, xrv,
                            ALU.add, ALU.add), waits=[ttr])
                        rdone(("psU", 1), tadd)
                        xadd_toks.append(tadd)
                        rdone("pcb", tadd)
        return xadd_toks

    def pair_qk(ph, wdr, bcol, src, src_toks, out):
        tw = DMAg(lambda e, h, ph=ph, wdr=wdr: e.dma_start(
            wbA[:, 0:256], wdr[ph]), waits=ruse("wbA"), sem="DA")
        lt = None
        for tc in range(4):
            qd = tc % 2
            sl = PSAB[qd][:, 0:512]
            pk = ("pab", qd)
            mt = None
            for cc in range(2):
                mt = (T_ if cc == 1 else T0_)(
                    lambda e, h, cc=cc, tc=tc, sl=sl, src=src: e.matmul(
                    sl, wbA[:, cc * 128:(cc + 1) * 128],
                    src[:, cc * T + tc * 512:cc * T + tc * 512 + 512],
                    start=(cc == 0), stop=(cc == 1)),
                    waits=(([tw] + src_toks + ruse(pk))
                           if cc == 0 else []))
            te = S_(lambda e, h, sl=sl, tc=tc, out=out, bcol=bcol:
                    e.activation(out[:, tc * 512:(tc + 1) * 512], sl,
                                 AF.Gelu, bias=bcol), waits=[mt])
            rdone(pk, te)
            lt = te
        rdone("wbA", lt)
        return lt

    def rope_pair(wdr, bcol, out, xn_dep, trc, trs):
        tw = DMAg(lambda e, h, wdr=wdr: e.dma_start(wbA[:], wdr[0]),
                  waits=ruse("wbA"), sem="DA")
        lt = None
        for tc in range(4):
            qd = tc % 2
            sl = PSAB[qd][:, 0:512]
            pk = ("pab", qd)
            mt = None
            for ic in range(8):
                mt = (T_ if ic == 7 else T0_)(
                    lambda e, h, ic=ic, tc=tc, sl=sl: e.matmul(
                    sl, wbA[:, ic * 128:(ic + 1) * 128],
                    xnT[:, ic * T + tc * 512:ic * T + tc * 512 + 512],
                    start=(ic == 0), stop=(ic == 7)),
                    waits=(([tw] + xn_dep + ruse(pk))
                           if ic == 0 else []))
            traw = S_(lambda e, h, sl=sl, tc=tc, bcol=bcol: e.activation(
                p1[:, tc * 512:(tc + 1) * 512], sl, AF.Identity,
                bias=bcol), waits=[mt] + ruse("p1r"))
            rdone(pk, traw)
            tsw = T_(lambda e, h, tc=tc: e.matmul(
                psS[:, 0:512], perm1[:],
                p1[:, tc * 512:(tc + 1) * 512], start=True, stop=True),
                waits=[traw, t_perm1] + ruse("psS"))
            tm1 = V_(lambda e, h, tc=tc: e.tensor_tensor(
                pcb[:, 0:512], p1[:, tc * 512:(tc + 1) * 512],
                ropeC[:, tc * 512:(tc + 1) * 512], ALU.mult),
                waits=[traw, trc] + ruse("pcb"))
            ts2 = V_(lambda e, h, tc=tc: e.tensor_tensor(
                pcb[:, 512:1024], psS[:, 0:512],
                ropeS[:, tc * 512:(tc + 1) * 512], ALU.mult),
                waits=[tsw, trs])
            rdone("psS", ts2)
            tsum = V_(lambda e, h: e.tensor_tensor(
                pcb[:, 0:512], pcb[:, 0:512], pcb[:, 512:1024], ALU.add),
                waits=[tm1, ts2])
            tg = S_(lambda e, h, tc=tc, out=out: e.activation(
                out[:, tc * 512:(tc + 1) * 512], pcb[:, 0:512], AF.Gelu),
                waits=[tsum])
            rdone("pcb", tg)
            radd("p1r", tg)
            lt = tg
        rdone("wbA", lt)
        return lt

    def emit_bounds(ph):
        first = V_(lambda e, h: e.memset(sml[0:1, 16:32], 0.0),
                   waits=ruse("bnd"))
        tmx_toks = []
        for side, buf in ((0, qh), (1, kh)):
            for tc in range(4):
                sq = p1[:, tc * 512:(tc + 1) * 512]
                tsq = S_(lambda e, h, buf=buf, tc=tc, sq=sq:
                         e.activation(
                    sq, buf[:, tc * 512:(tc + 1) * 512], AF.Square),
                    waits=ruse("bsq", "p1r") + [first])
                for h2 in range(2):
                    ps_ = psS if h2 == 0 else psR
                    pk_ = "psS" if h2 == 0 else "psR"
                    tred = T_(lambda e, h, sq=sq, h2=h2, ps_=ps_:
                              e.matmul(
                        ps_[:, 0:512],
                        onesq[64 * h2:64 * h2 + 64, 0:128],
                        sq[64 * h2:64 * h2 + 64, :],
                        start=True, stop=True),
                        waits=[tsq, t_blkc] + ruse(pk_))
                    cix = 16 + side * 8 + h2 * 4 + tc
                    tmx = V_(lambda e, h, cix=cix, ps_=ps_: e.reduce_max(
                        sml[0:1, cix:cix + 1],
                        ps_[0:1, 0:512], AX.X), waits=[tred])
                    rdone(pk_, tmx)
                    tmx_toks.append(tmx)
                rdone("bsq", tmx_toks[-2:])
        tms = []
        for side in range(2):
            for h2 in range(2):
                c0 = 16 + side * 8 + h2 * 4
                tms.append(V_(lambda e, h, c0=c0, side=side, h2=h2:
                              e.reduce_max(
                    sml2[0:1, 16 + side * 2 + h2:17 + side * 2 + h2],
                    sml[0:1, c0:c0 + 4], AX.X), waits=tmx_toks))
        tp = V_(lambda e, h: e.tensor_tensor(
            sml2[0:1, 20:22], sml2[0:1, 16:18], sml2[0:1, 18:20],
            ALU.mult), waits=tms)
        tp = S_(lambda e, h: e.activation(sml2[0:1, 22:24],
                                          sml2[0:1, 20:22], AF.Ln,
                                          bias=cstcol(2.0 ** -60)[0:1, :]),
                waits=[tp])
        tp = S_(lambda e, h: e.activation(
            sml2[0:1, 24:26], sml2[0:1, 22:24], AF.Exp, scale=0.5,
            bias=cstcol(-math.log(8.0))[0:1, :]), waits=[tp])
        tp = V_(lambda e, h: e.tensor_scalar(
            smlb[0:1, 4:6], sml2[0:1, 24:26], 40.0, -1.0, ALU.min,
            ALU.mult), waits=[tp])
        t2 = emit_bcast2([smlb[0:1, 4:6]],
                         [bcols[:, 2 * ph:2 * ph + 1],
                          bcols[:, 2 * ph + 1:2 * ph + 2]], [tp])
        rdone("bnd", [t2])
        return t2

    def l1_attention(xn_dep, aT_toks, pump=None):
        DMAg(lambda e, h: e.dma_start(ropeC[:], ropec1[0]),
             waits=ruse("ropes"), sem="Dr")
        trs = DMAg(lambda e, h: e.dma_start(ropeS[:], ropec1[1]),
                   sem="Dr")
        trc = trs
        cq_toks = []
        ckv_toks = []
        for (wdr, bc_sl, out, toks) in ((wq1, bias1[:, 0:2], cqb, cq_toks),
                                        (wkv1, bias2[:, 0:2], ckvb,
                                         ckv_toks)):
            for oc in range(2):
                tw = DMAg(lambda e, h, oc=oc, wdr=wdr: e.dma_start(
                    wbA[:], wdr[oc]), waits=ruse("wbA"), sem="DA")
                lt = None
                for tc in range(4):
                    qd = tc % 2
                    sl = PSAB[qd][:, 0:512]
                    pk = ("pab", qd)
                    mt = None
                    for ic in range(8):
                        mt = (T_ if ic == 7 else T0_)(
                            lambda e, h, ic=ic, tc=tc, sl=sl: e.matmul(
                            sl, wbA[:, ic * 128:(ic + 1) * 128],
                            xnT[:, ic * T + tc * 512:
                                ic * T + tc * 512 + 512],
                            start=(ic == 0), stop=(ic == 7)),
                            waits=(([tw] + xn_dep + t_bias +
                                    ruse(pk))
                                   if ic == 0 else []))
                    te = S_(lambda e, h, sl=sl, oc=oc, tc=tc, out=out,
                            bc_sl=bc_sl: e.activation(
                        out[:, oc * T + tc * 512:oc * T + tc * 512 + 512],
                        sl, AF.Gelu, bias=bc_sl[:, oc:oc + 1]),
                        waits=[mt])
                    rdone(pk, te)
                    toks.append(te)
                    lt = te
                rdone("wbA", lt)
        for ph in range(8):
            if ph < 7:
                tq_ = pair_qk(ph, wqc1, bias1[:, 2 + ph:3 + ph], cqb,
                              cq_toks, qh)
                tk_ = pair_qk(ph, wkc1, bias2[:, 2 + ph:3 + ph], ckvb,
                              ckv_toks, kh)
            else:
                tq_ = rope_pair(wqr1, bias1[:, 9:10], qh, xn_dep, trc, trs)
                tk_ = rope_pair(wkr1, bias2[:, 9:10], kh, xn_dep, trc, trs)
            twv = DMAg(lambda e, h, ph=ph: e.dma_start(wbC[:, 0:256],
                                                       wv1[ph]),
                       waits=ruse("wbC"), sem="DC")
            vt = []
            for qt in range(16):
                sl = PSCD[qt % 2][:, 0:128]
                pk = ("pcd", qt % 2)
                mt = None
                for cc in range(2):
                    mt = (T_ if cc == 1 else T0_)(
                        lambda e, h, cc=cc, qt=qt, sl=sl: e.matmul(
                        sl, ckvb[:, cc * T + qt * 128:
                                 cc * T + qt * 128 + 128],
                        wbC[:, cc * 128:(cc + 1) * 128],
                        start=(cc == 0), stop=(cc == 1)),
                        waits=(([twv] + ckv_toks + ruse(pk))
                               if cc == 0 else []))
                vdst = vtok[:, qt * VS:qt * VS + 130].rearrange(
                    "p (a b) -> p a b", b=65)[:, :, 0:64]
                vsrc = sl[:, 0:128].rearrange("p (a b) -> p a b", b=64)
                tv = V_(lambda e, h, vdst=vdst, vsrc=vsrc: e.tensor_copy(
                    vdst, vsrc), waits=[mt])
                rdone(pk, tv)
                vt.append(tv)
            rdone("wbC", vt[-1])
            bnd_tok = emit_bounds(ph)

            def stageA1(qt, h2):
                par = (2 * qt + h2) % 2
                po = p0[:, par * T:par * T + T]
                hh = 2 * ph + h2
                side_toks = []
                ng = (qt + 4) // 4
                for g in range(ng):
                    jn = min(4, qt + 1 - 4 * g)
                    sl = PSAB[g % 2][:, 0:512]
                    pk = ("pab", g % 2)
                    mlast = None
                    for j in range(jn):
                        kb = 4 * g + j
                        mlast = (T_ if j == jn - 1 else T0_)(
                            lambda e, h, j=j, kb=kb, h2=h2, qt=qt,
                            sl=sl: e.matmul(
                            sl[:, j * 128:(j + 1) * 128],
                            kh[64 * h2:64 * h2 + 64,
                               kb * 128:(kb + 1) * 128],
                            qh[64 * h2:64 * h2 + 64,
                               qt * 128:(qt + 1) * 128],
                            start=True, stop=True),
                            waits=(([tq_, tk_] + ruse(pk))
                                   if j == 0 else []))
                    wl = [mlast, t_mask]
                    if g == ng - 1:
                        jd = qt - 4 * g
                        tmask = V_(lambda e, h, jd=jd, sl=sl:
                                   e.tensor_tensor(
                            sl[:, jd * 128:(jd + 1) * 128],
                            sl[:, jd * 128:(jd + 1) * 128],
                            maskb[:], ALU.add), waits=wl)
                        wl = [tmask]
                    texp = S_(lambda e, h, jn=jn, g=g, hh=hh, sl=sl,
                              po=po: e.activation(
                        po[:, g * 512:g * 512 + jn * 128],
                        sl[:, 0:jn * 128], AF.Exp, scale=0.125,
                        bias=bcols[:, hh:hh + 1]),
                        waits=wl + [bnd_tok] + ruse(("pex", par)))
                    rdone(pk, texp)
                    side_toks.append(texp)
                return side_toks

            def stageB1(qt, h2, chunks):
                par = (2 * qt + h2) % 2
                po = p0[:, par * T:par * T + T]
                pvps = PSCD[qt % 2]
                pvk = ("pcd", qt % 2)
                base = 256 * h2
                pvt = None
                for kb in range(qt + 1):
                    pvt = (T_ if kb == qt else T0_)(
                        lambda e, h, kb=kb, qt=qt, h2=h2, base=base,
                        po=po: e.matmul(
                        pvps[:, base:base + 65],
                        po[:, kb * 128:(kb + 1) * 128],
                        vtok[:, kb * VS + 64 * h2:kb * VS + 64 * h2 + 65],
                        start=(kb == 0), stop=(kb == qt)),
                        waits=[chunks[kb // 4], vt[kb]] +
                        (ruse(pvk) if kb == 0 and h2 == 0 else []))
                rdone(("pex", par), pvt)
                return pvt

            def tail1(qt, pv0, pv1):
                pvps = PSCD[qt % 2]
                pvk = ("pcd", qt % 2)
                c8 = 8 + 2 * (qt % 2)
                rz0 = V_(lambda e, h, c8=c8, pvps=pvps: e.reciprocal(
                    sml2[:, c8:c8 + 1], pvps[:, 64:65]),
                    waits=[pv0, pv1])
                rz1 = V_(lambda e, h, c8=c8, pvps=pvps: e.reciprocal(
                    sml2[:, c8 + 1:c8 + 2], pvps[:, 256:257]),
                    waits=[pv1])
                tno0 = V_(lambda e, h, c8=c8, pvps=pvps: e.tensor_scalar(
                    pcb[:, 0:64], pvps[:, 0:64],
                    sml2[:, c8:c8 + 1], None, ALU.mult),
                    waits=[rz0] + ruse("pcb"))
                tno1 = V_(lambda e, h, c8=c8, pvps=pvps: e.tensor_scalar(
                    pcb[:, 64:128], pvps[:, 257:321],
                    sml2[:, c8 + 1:c8 + 2], None, ALU.mult),
                    waits=[rz1])
                rdone(pvk, tno1)
                ttr = T_(lambda e, h: e.transpose(
                    psU[:, 0:128], pcb[:, 0:128], ident[:]),
                    waits=[tno0, tno1] + ruse(("psU", 0)))
                rdone("pcb", ttr)
                tev = V_(lambda e, h, ph=ph, qt=qt: e.tensor_scalar(
                    aT[:, ph * T + qt * 128:ph * T + qt * 128 + 128],
                    psU[:, 0:128], bvcol[:, ph:ph + 1], None,
                    ALU.add), waits=[ttr])
                rdone(("psU", 0), tev)
                aT_toks.append(tev)

            units = [(qt, h2) for qt in range(16) for h2 in range(2)]
            pend = {}
            prevu = None
            for u in units:
                chunks = stageA1(*u)
                pend[u] = [chunks, None]
                if prevu is not None:
                    pend[prevu][1] = stageB1(prevu[0], prevu[1],
                                             pend[prevu][0])
                    if pump is not None:
                        pump(1)
                    if prevu[1] == 1:
                        q0 = prevu[0]
                        tail1(q0, pend[(q0, 0)][1], pend[(q0, 1)][1])
                        del pend[(q0, 0)], pend[(q0, 1)]
                prevu = u
            pend[prevu][1] = stageB1(prevu[0], prevu[1], pend[prevu][0])
            tail1(15, pend[(15, 0)][1], pend[(15, 1)][1])
        return aT_toks

    lvl = SCHED_LEVEL
    final_dep = []

    def gen_w2():
        for g in (w2q_gen(0), w2q_gen(1)):
            for _ in g:
                yield

    def make_pump(gen):
        state = [gen is None]

        def pump(n=1):
            if state[0]:
                return
            try:
                for _ in range(n):
                    next(gen)
            except StopIteration:
                state[0] = True

        def drain():
            if state[0]:
                return
            for _ in gen:
                pass
            state[0] = True
        return pump, drain

    gen0 = gen_w2() if lvl >= 2 else None
    pump0, drain0 = make_pump(gen0)
    pump0(6)
    ln0 = emit_layernorm(t_x + [t_ident])
    xn_dep0 = [ln0[-1]]
    final_dep = [ln0[-1]]
    if lvl >= 1:
        nlam_tok = emit_lambda() if LAMBDA_ON else None
        l0_toks = []
        for hd in range(N_HEADS_L0):
            l0_head(hd, nlam_tok, xn_dep0, l0_toks, pump=pump0)
        drain0()
        final_dep = l0_toks[-1:] or final_dep
        if PROJ_ON:
            xadd0 = []
            emit_out_proj(wproj0, aT, l0_toks[-1:], None, 8, xadd0)
            final_dep = [xadd0[-1]]
    if lvl >= 2:
        emit_ffn(0, xn_dep0, xadd0, W2R[0])
        final_dep = [xadd0[-1]]
    if lvl >= 3:
        ln1 = emit_layernorm([xadd0[-1]])
        xn_dep1 = [ln1[-1]]
        aT_toks1 = []
        l1_attention(xn_dep1, aT_toks1)
        xadd1 = []
        emit_out_proj(wo1, aT, [aT_toks1[-1]], bocol, 8, xadd1)
        final_dep = [xadd1[-1]]
    if lvl >= 4:
        emit_ffn(1, xn_dep1, xadd1, W2R[1])
        final_dep = [xadd1[-1]]

    for i in range(16):
        DMAo(lambda e, h, i=i: e.dma_start(
            y3[i], x_res[:, i * 1024:(i + 1) * 1024]), waits=final_dep)
    P.op("sync", lambda e, h: None,
         waits=[(s, P.sem_counts[s]) for s in
                ("Do", "Dc", "Dg", "DA", "DB", "DC", "W2A", "W2B",
                 "DZ0", "DZ1", "DS0", "DS1", "Dr")
                if P.sem_counts.get(s, 0) > 0])

    with nc.Block() as block:
        @block.tensor
        def _(eng):
            for f in P.th["tensor"]:
                f(eng, H)

        @block.vector
        def _(eng):
            for f in P.th["vector"]:
                f(eng, H)

        @block.scalar
        def _(eng):
            for f in P.th["scalar"]:
                f(eng, H)

        @block.gpsimd
        def _(eng):
            for f in P.th["gpsimd"]:
                f(eng, H)

        @block.sync
        def _(eng):
            for f in P.th["sync"]:
                f(eng, H)
    es.close()
    return nc


def prep_inputs(inputs):
    g = {k: np.asarray(v) for k, v in inputs.items()}
    d = {}
    d["wq0"] = pack_lhsT(np.ascontiguousarray(g["L0_attn_Wq"].T))
    d["wk0"] = pack_lhsT(np.ascontiguousarray(g["L0_attn_Wk"].T))
    d["wv0"] = pack_lhsT(np.ascontiguousarray(g["L0_attn_Wv"].T))
    d["wproj0"] = pack_lhsT(np.ascontiguousarray(g["L0_attn_Wproj"].T))
    for l in range(2):
        d[f"w1_{l}"] = pack_lhsT(np.ascontiguousarray(g[f"L{l}_ffn_w1"].T))
        d[f"w2_{l}"] = np.ascontiguousarray(
            g[f"L{l}_ffn_w2"].T.astype(np.float32)).reshape(32, 128, 1024)
        d[f"b1c_{l}"] = pack_bias_cols(g[f"L{l}_ffn_b1"])
        d[f"b2c_{l}"] = pack_bias_cols(g[f"L{l}_ffn_b2"])
    d["wq1"] = pack_lhsT(g["L1_attn_wq"])
    d["bq1"] = pack_bias_cols(g["L1_attn_bq"])
    d["wqc1"] = pack_lhsT(g["L1_attn_wqc"])
    d["bqc1"] = pack_bias_cols(g["L1_attn_bqc"])
    d["wqr1"] = pack_lhsT(g["L1_attn_wqr"])
    d["bqr1"] = pack_bias_cols(g["L1_attn_bqr"])
    d["wkv1"] = pack_lhsT(g["L1_attn_wkv"])
    d["bkv1"] = pack_bias_cols(g["L1_attn_bkv"])
    d["wkr1"] = pack_lhsT(g["L1_attn_wkr"])
    d["bkr1"] = pack_bias_cols(g["L1_attn_bkr"])
    d["wkc1"] = pack_lhsT(g["L1_attn_wkc"])
    d["bkc1"] = pack_bias_cols(g["L1_attn_bkc"])
    d["wv1"] = pack_lhsT(g["L1_attn_wv"])
    d["bv1"] = pack_bias_cols(g["L1_attn_bv"])
    d["wo1"] = pack_lhsT(g["L1_attn_wo"])
    d["bo1"] = pack_bias_cols(g["L1_attn_bo"])
    d["lamv"] = np.concatenate(
        [g["L0_attn_lq1"], g["L0_attn_lq2"], g["L0_attn_lk1"],
         g["L0_attn_lk2"]]).astype(np.float32).reshape(1, 256)
    d["identc"] = np.eye(128, dtype=np.float32)
    d["permc0"] = perm_mat(32)
    d["permc1"] = perm_mat(64)
    d["maskc"] = causal_maskbias()
    d["ropec0"] = np.stack(rope_tables_l0())
    d["ropec1"] = np.stack(rope_tables_l1())
    return d


def kernel(**inputs):
    nc = build_program()
    shared = prep_inputs(inputs)
    x = np.asarray(inputs["L0_attn_x"])
    in_maps = []
    for b in range(NB):
        m = dict(shared)
        m["x"] = np.ascontiguousarray(x[b])
        in_maps.append(m)
    res = run_bass_kernel_spmd(nc, in_maps, core_ids=list(range(NB)))
    return np.stack([np.asarray(r["y"])
                     for r in res.results]).astype(np.float32)


# version 22 (final)
# speedup vs baseline: 1.1103x; 1.0030x over previous
import math
from contextlib import ExitStack

import numpy as np

import concourse.bass as bass
import concourse.mybir as mybir
from concourse.bass_utils import run_bass_kernel_spmd

F32 = mybir.dt.float32
F32R = mybir.dt.float32r
BF16 = mybir.dt.bfloat16
F8 = mybir.dt.float8e4
I32 = mybir.dt.int32
AF = mybir.ActivationFunctionType
ALU = mybir.AluOpType
AX = mybir.AxisListType

T = 2048
NB = 8
LN2 = math.log(2.0)
INV_LN2 = 1.0 / LN2
LAMBDA_INIT = 0.8 - 0.6 * math.exp(-0.3 * 4)
RMS_EPS = float(np.finfo(np.float32).eps)
LN_EPS = 1e-5
SCHED_LEVEL = 4
N_HEADS_L0 = 8
ATTN_STAGE = 4
PROJ_ON = 1
LAMBDA_ON = 1
PREP_STAGE = 3



def pack_lhsT(W):
    I, O = W.shape
    return np.ascontiguousarray(
        W.reshape(I // 128, 128, O // 128, 128)
        .transpose(2, 1, 0, 3)
        .reshape(O // 128, 128, I).astype(np.float32))


def pack_bias_cols(b):
    n = b.shape[0] // 128
    return np.ascontiguousarray(np.asarray(b, np.float32).reshape(n, 128).T)


def rope_tables_l0():
    HD = 64
    inv_freq = 1.0 / 10000.0 ** (np.arange(0, HD, 2, dtype=np.float32) / HD)
    fr = np.outer(np.arange(T, dtype=np.float32), inv_freq)
    cos, sin = np.cos(fr).T, np.sin(fr).T
    Cf = np.empty((128, T), np.float32)
    Sf = np.empty((128, T), np.float32)
    for blk in range(4):
        Cf[blk * 32:(blk + 1) * 32] = cos
        Sf[blk * 32:(blk + 1) * 32] = sin if blk % 2 == 0 else -sin
    return Cf, Sf


def rope_tables_l1():
    DR = 128
    inv_freq = 1.0 / 10000.0 ** (np.arange(0, DR, 2, dtype=np.float32) / DR)
    fr = np.outer(np.arange(T, dtype=np.float32), inv_freq)
    cos, sin = np.cos(fr).T, np.sin(fr).T
    Cf = np.empty((128, T), np.float32)
    Sf = np.empty((128, T), np.float32)
    Cf[0:64] = cos
    Cf[64:128] = cos
    Sf[0:64] = -sin
    Sf[64:128] = sin
    return Cf, Sf


def perm_mat(n):
    P = np.zeros((128, 128), np.float32)
    for a in range(128):
        P[(a // n ^ 1) * n + (a % n), a] = 1.0
    return P


def causal_maskbias():
    m = np.zeros((128, 128), np.float32)
    m[np.tril_indices(128, -1)] = -60.0
    return m


class Prog:
    ENGINES = ("tensor", "vector", "scalar", "gpsimd", "sync")

    def __init__(self):
        self.th = {e: [] for e in self.ENGINES}
        self.sem_counts = {}
        self.hwm = {e: {} for e in self.ENGINES}
        self.default_waits = []

    def op(self, eng, fn, waits=(), inc=None, inc_step=1):
        tok = None
        if inc is not None:
            v = self.sem_counts.get(inc, 0) + inc_step
            self.sem_counts[inc] = v
            tok = (inc, v)
        rw = []
        for w in list(self.default_waits) + list(waits):
            if w is None:
                continue
            s, v = w
            if self.hwm[eng].get(s, 0) < v:
                self.hwm[eng][s] = v
                rw.append((s, v))
        step = inc_step

        def thunk(engine, H):
            for (s, v) in rw:
                engine.wait_ge(H[s], v)
            ins = fn(engine, H)
            if inc is not None:
                if ins is None:
                    engine.sem_inc(H[inc], step)
                else:
                    ins.then_inc(H[inc], step)
        self.th[eng].append(thunk)
        return tok


def build_program():
    nc = bass.Bass("TRN2", target_bir_lowering=False)
    P = Prog()

    def din(name, shape, dtype=F32):
        return nc.dram_tensor(name, list(shape), dtype,
                              kind="ExternalInput").ap()

    x_in = din("x", [T, 1024])
    y_out = nc.dram_tensor("y", [T, 1024], F32, kind="ExternalOutput").ap()

    wq0 = din("wq0", [8, 128, 1024])
    wk0 = din("wk0", [8, 128, 1024])
    wv0 = din("wv0", [8, 128, 1024])
    wproj0 = din("wproj0", [8, 128, 1024])
    wff = []
    for l in range(2):
        wff.append(dict(
            w1=din(f"w1_{l}", [32, 128, 1024]),
            w2=din(f"w2_{l}", [32, 128, 1024]),
            b1=din(f"b1c_{l}", [128, 32]),
            b2=din(f"b2c_{l}", [128, 8])))
    wq1 = din("wq1", [2, 128, 1024])
    bq1 = din("bq1", [128, 2])
    wqc1 = din("wqc1", [7, 128, 256])
    bqc1 = din("bqc1", [128, 7])
    wqr1 = din("wqr1", [1, 128, 1024])
    bqr1 = din("bqr1", [128, 1])
    wkv1 = din("wkv1", [2, 128, 1024])
    bkv1 = din("bkv1", [128, 2])
    wkr1 = din("wkr1", [1, 128, 1024])
    bkr1 = din("bkr1", [128, 1])
    wkc1 = din("wkc1", [7, 128, 256])
    bkc1 = din("bkc1", [128, 7])
    wv1 = din("wv1", [8, 128, 256])
    bv1 = din("bv1", [128, 8])
    wo1 = din("wo1", [8, 128, 1024])
    bo1 = din("bo1", [128, 8])
    lam_in = din("lamv", [1, 256])
    identc = din("identc", [128, 128])
    permc0 = din("permc0", [128, 128])
    permc1 = din("permc1", [128, 128])
    maskc = din("maskc", [128, 128])
    ropec0 = din("ropec0", [2, 128, T])
    ropec1 = din("ropec1", [2, 128, T])
    w2q_sc = [nc.dram_tensor(f"w2qsc_{l}", [32, 128, 1024], F8,
                             kind="Internal").ap() for l in range(2)]

    es = ExitStack()

    def sbt(name, shape, dtype):
        return es.enter_context(nc.sbuf_tensor(name, shape, dtype))

    x_res = sbt("x_res", [128, 16 * 1024], F32)
    xnT = sbt("xnT", [128, 8 * T], BF16)
    arA = sbt("arA", [128, 8 * T], BF16)
    aT = arA[:, 0:8 * T]
    hqv = arA.bitcast(F8)
    arB = sbt("arB", [128, 6 * T], BF16)
    cqb = arB[:, 0:2 * T]
    ckvb = arB[:, 2 * T:4 * T]
    qh = sbt("qh", [128, T], BF16)
    kh = sbt("kh", [128, T], BF16)
    VS = 136
    vtok = sbt("vtok", [128, 16 * VS], BF16)
    p0 = sbt("p0", [128, 2 * T], BF16)
    p1 = sbt("p1", [128, 2 * T], BF16)
    pcb = sbt("pcb", [128, 1024], BF16)
    NSCR = 8
    scr = arB.bitcast(F32)
    scri = arB.bitcast(I32)[:, 4096:5120]
    scrb = arB[:, 10240:11264]
    rmsr = sbt("rmsr", [128, 512], F32)
    rmsb = sbt("rmsb", [128, 512], F32)
    wbA = sbt("wbA", [128, 1024], BF16)
    wbB = sbt("wbB", [128, 1024], BF16)
    wbC = sbt("wbC", [128, 1024], BF16)
    w2qs = sbt("w2qs", [128, 2 * 1024], F8)
    ident = sbt("ident", [128, 128], BF16)
    perm0 = sbt("perm0", [128, 128], BF16)
    perm1 = sbt("perm1", [128, 128], BF16)
    maskb = sbt("maskb", [128, 128], BF16)
    ropeC = sbt("ropeC", [128, T], BF16)
    ropeS = sbt("ropeS", [128, T], BF16)
    onesq = sbt("onesq", [128, 128], BF16)
    b1cols = [sbt(f"b1col{l}", [128, 32], F32) for l in range(2)]
    b2cols = [sbt(f"b2col{l}", [128, 8], F32) for l in range(2)]
    b2qcols = [sbt(f"b2qcol{l}", [128, 8], F32) for l in range(2)]
    bias1 = sbt("bias1", [128, 12], F32)
    bias2 = sbt("bias2", [128, 12], F32)
    bvcol = sbt("bvcol", [128, 8], F32)
    bocol = sbt("bocol", [128, 8], F32)
    lamt = sbt("lamt", [1, 256], F32)
    smlb = sbt("smlb", [1, 8], BF16)
    sml = sbt("sml", [128, 128], F32)
    sml2 = sbt("sml2", [128, 32], F32)
    cst = sbt("cst", [128, 16], F32)
    stats = sbt("stats", [128, 12], F32)
    aggr = sbt("aggr", [128, 2 * 16], F32)
    rstd = sbt("rstd", [128, 16], F32)
    bcols = sbt("bcols", [128, 16], F32)

    psA = es.enter_context(nc.psum_tensor("psA", [128, 512], F32))
    psB = es.enter_context(nc.psum_tensor("psB", [128, 512], F32))
    psC = es.enter_context(nc.psum_tensor("psC", [128, 512], F32))
    psD = es.enter_context(nc.psum_tensor("psD", [128, 512], F32))
    psT = es.enter_context(nc.psum_tensor("psT", [128, 1024], BF16))
    psU = es.enter_context(nc.psum_tensor("psU", [128, 1024], BF16))
    psS = es.enter_context(nc.psum_tensor("psS", [128, 512], F32))
    psR = es.enter_context(nc.psum_tensor("psR", [128, 512], F32))
    PSAB = [psA, psB]
    PSCD = [psC, psD]
    PS4 = [psA, psB, psC, psD]

    sem_names = ["T", "V", "S", "G", "Dc", "Dg", "DA", "DB", "DC",
                 "W2A", "W2B", "DZ0", "DZ1", "DS0", "DS1",
                 "Dr", "Do"]
    H = {n: es.enter_context(nc.semaphore(n)) for n in sem_names}

    def T_(fn, waits=()):
        return P.op("tensor", fn, waits, inc="T")

    def T0_(fn, waits=()):
        return P.op("tensor", fn, waits, inc=None)

    def V_(fn, waits=()):
        return P.op("vector", fn, waits, inc="V")

    def S_(fn, waits=()):
        return P.op("scalar", fn, waits, inc="S")

    def G_(fn, waits=()):
        return P.op("gpsimd", fn, waits, inc="G")

    def DMA(eng, sem, fn, waits=()):
        return P.op(eng, fn, waits, inc=sem, inc_step=16)

    def DMAw(fn, waits=(), sem="Dc"):
        return P.op("sync", fn, waits, inc=sem, inc_step=16)

    def DMAg(fn, waits=(), sem="Dg"):
        return P.op("gpsimd", fn, waits, inc=sem, inc_step=16)

    def DMAo(fn, waits=()):
        return P.op("sync", fn, waits, inc="Do", inc_step=16)

    last_use = {}

    def ruse(*keys):
        out = []
        for k in keys:
            out.extend(last_use.get(k, []))
        return out

    def rdone(key, toks):
        if not isinstance(toks, list):
            toks = [toks]
        last_use[key] = [t for t in toks if t is not None]

    def radd(key, tok):
        if tok is not None:
            last_use.setdefault(key, []).append(tok)

    def emit_bcast2(srcs, dsts, waits):
        tm = None
        for i, s2 in enumerate(srcs):
            tm = T_(lambda e, h, s2=s2, i=i: e.matmul(
                psS[:, 0:2], onesq[0:1, 0:128], s2,
                start=(i == 0), stop=(i == len(srcs) - 1)),
                waits=list(waits) + [t_blkr] + (ruse("psS")
                                               if i == 0 else []))
        out = []
        for j, d in enumerate(dsts):
            if d is None:
                continue
            out.append(V_(lambda e, h, j=j, d=d: e.tensor_copy(
                d, psS[:, j:j + 1]), waits=[tm]))
        rdone("psS", out)
        return out[-1]

    CSTV = {}
    CSTV_FROZEN = [False]

    def cstcol(val):
        val = float(val)
        if val not in CSTV:
            assert not CSTV_FROZEN[0], f"const col {val} not preregistered"
            i = len(CSTV)
            assert i < 16, "too many const cols"
            CSTV[val] = i
            V_(lambda e, h, i=i, val=val: e.memset(cst[:, i:i + 1], val))
        return cst[:, CSTV[val]:CSTV[val] + 1]

    t_ident = DMAg(lambda e, h: e.dma_start(ident[:], identc))
    t_perm0 = DMAg(lambda e, h: e.dma_start(perm0[:], permc0))
    t_perm1 = DMAg(lambda e, h: e.dma_start(perm1[:], permc1))
    t_ropeC = DMAg(lambda e, h: e.dma_start(ropeC[:], ropec0[0]))
    t_ropeS = DMAg(lambda e, h: e.dma_start(ropeS[:], ropec0[1]))
    t_mask = DMAg(lambda e, h: e.dma_start(maskb[:], maskc))
    t_blkr = V_(lambda e, h: e.memset(onesq[:], 1.0))
    t_blkc = t_blkr
    V_(lambda e, h: e.memset(smlb[:], 0.0))
    t_lam = DMAw(lambda e, h: e.dma_start(lamt[:], lam_in))
    t_b1 = [DMAw(lambda e, h, l=l: e.dma_start(b1cols[l][:], wff[l]["b1"]))
            for l in range(2)]
    t_b2 = [DMAw(lambda e, h, l=l: e.dma_start(b2cols[l][:], wff[l]["b2"]))
            for l in range(2)]
    x3 = x_in.rearrange("(n p) c -> n p c", p=128)
    y3 = y_out.rearrange("(n p) c -> n p c", p=128)
    t_x = [DMAw(lambda e, h, i=i: e.dma_start(
        x_res[:, i * 1024:(i + 1) * 1024], x3[i])) for i in range(16)]
    t_bias_l = [
        DMAw(lambda e, h: e.dma_start(bias1[:, 0:2], bq1)),
        DMAw(lambda e, h: e.dma_start(bias1[:, 2:9], bqc1)),
        DMAw(lambda e, h: e.dma_start(bias1[:, 9:10], bqr1)),
        DMAw(lambda e, h: e.dma_start(bias2[:, 0:2], bkv1)),
        DMAw(lambda e, h: e.dma_start(bias2[:, 2:9], bkc1)),
        DMAw(lambda e, h: e.dma_start(bias2[:, 9:10], bkr1)),
        DMAw(lambda e, h: e.dma_start(bvcol[:], bv1)),
        DMAw(lambda e, h: e.dma_start(bocol[:], bo1)),
    ]
    for kb in range(16):
        V_(lambda e, h, kb=kb: e.memset(vtok[:, kb * VS + 64:kb * VS + 65],
                                        1.0))
    for _v in (0.0, 2.0, 4096.0, LN_EPS, RMS_EPS, 2.0 ** -60, -8.0,
               -math.log(8.0), math.log(1.0 - LAMBDA_INIT),
               3 * LN2, -3 * LN2, 4 * LN2, -4 * LN2):
        cstcol(_v)
    CSTV_FROZEN[0] = True
    tok_c = P.op("vector", lambda e, h: None,
                 waits=[("Dc", P.sem_counts["Dc"]),
                        ("Dg", P.sem_counts.get("Dg", 0))], inc="V")
    P.default_waits = [tok_c]
    t_ident = t_perm0 = t_perm1 = t_ropeC = t_ropeS = tok_c
    t_mask = t_blkc = t_blkr = t_lam = t_ones = tok_c
    t_b1 = [tok_c, tok_c]
    t_b2 = [tok_c, tok_c]
    t_x = [tok_c]
    t_bias = [tok_c]

    def emit_layernorm(dep):
        tA = []
        for i in range(16):
            xt = x_res[:, i * 1024:(i + 1) * 1024]
            tsa = V_(lambda e, h, xt=xt: e.bn_stats(stats[:, 0:6],
                                                    xt[:, 0:512]),
                     waits=list(dep) + ruse("aggr", "stats"))
            tsb = V_(lambda e, h, xt=xt: e.bn_stats(stats[:, 6:12],
                                                    xt[:, 512:1024]),
                     waits=[tsa])
            ta_ = V_(lambda e, h, i=i: e.bn_aggr(
                aggr[:, 2 * i:2 * i + 2], stats[:, 0:12]), waits=[tsb])
            rdone("stats", ta_)
            tA.append(ta_)
        var_view = aggr[:, 1:32:2]
        tl = S_(lambda e, h: e.activation(rstd[:], var_view, AF.Ln,
                                          bias=cstcol(LN_EPS)),
                waits=[tA[-1]])
        tl = S_(lambda e, h: e.activation(rstd[:], rstd[:], AF.Exp,
                                          scale=-0.5), waits=[tl])
        rdone("aggr", tl)
        toks = []
        for i in range(16):
            xt = x_res[:, i * 1024:(i + 1) * 1024]
            tv = V_(lambda e, h, xt=xt, i=i: e.tensor_scalar(
                pcb[:, 0:1024], xt, aggr[:, 2 * i:2 * i + 1],
                rstd[:, i:i + 1], ALU.subtract, ALU.mult),
                waits=[tl] + ruse("pcb"))
            last = None
            last = None
            for g in range(2):
                tt = None
                for j in range(4):
                    c = 2 * j + g
                    tt = (T_ if j == 3 else T0_)(
                        lambda e, h, c=c, j=j, g=g: e.transpose(
                        psT[:, g * 512 + j * 128:g * 512 + j * 128 + 128],
                        pcb[:, c * 128:(c + 1) * 128], ident[:]),
                        waits=(([tv, t_ident] + ruse("psTg"))
                               if j == 0 else []))
                xv = xnT[:, 0:16384].rearrange(
                    "p (a b) -> p a b", b=4096)[
                    :, :, g * 2048 + i * 128:g * 2048 + i * 128 + 128]
                pv = psT[:, g * 512:g * 512 + 512].rearrange(
                    "p (a b) -> p a b", b=128)
                tc = V_(lambda e, h, xv=xv, pv=pv: e.tensor_copy(
                    xv, pv), waits=[tt])
                rdone("psTg", tc)
                last = tc
                toks.append(tc)
            rdone("pcb", last)
        return toks

    qpar = [0]
    chain_done = {0: None, 1: None}

    def emit_quant(z_ap, w, out_ap, kind, bias_col=None, relu=False,
                   skip_stage1=False, waits=()):
        man, mine = (3, -6) if kind == "e4m3" else (10, -14)
        p = qpar[0]
        qpar[0] ^= 1

        def fs(j):
            return scr[:, (4 * p + j) * 512:(4 * p + j) * 512 + w]
        iA = scri[:, p * 512:p * 512 + w]
        SCB = scrb[:, p * 512:p * 512 + w]
        wl = list(waits)
        if chain_done[p] is not None:
            wl = wl + [chain_done[p]]
        if skip_stage1:
            A = z_ap
            z_done = None
            pre = wl
        else:
            bcol = bias_col if bias_col is not None else cstcol(0.0)
            t1 = S_(lambda e, h: e.activation(
                fs(0), z_ap, AF.Relu if relu else AF.Identity, bias=bcol),
                waits=wl)
            A = fs(0)
            z_done = t1
            pre = [t1]
        if relu:
            lin = A
        else:
            tab = S_(lambda e, h: e.activation(fs(1), A, AF.Abs), waits=pre)
            lin = fs(1)
            pre = [tab]
        tl = S_(lambda e, h: e.activation(fs(2), lin, AF.Ln,
                                          scale=float(2.0 ** 46),
                                          bias=cstcol(2.0)), waits=pre)
        ta = S_(lambda e, h: e.activation(iA, fs(2), AF.Identity,
                                          scale=INV_LN2), waits=[tl])
        tb = V_(lambda e, h: e.scalar_tensor_tensor(
            fs(1), iA, LN2, fs(2), ALU.mult, ALU.is_gt), waits=[ta])
        tn = V_(lambda e, h: e.scalar_tensor_tensor(
            fs(3), fs(1), float(46 - mine), iA, ALU.add, ALU.subtract),
            waits=[tb])
        tcl = V_(lambda e, h: e.tensor_scalar(
            fs(1), fs(3), float(-2 * mine), None, ALU.min), waits=[tn])
        tsc = S_(lambda e, h: e.activation(
            SCB, fs(1), AF.Exp, scale=-LN2,
            bias=cstcol((-mine - man) * LN2)), waits=[tcl])
        trs = S_(lambda e, h: e.activation(
            fs(3), fs(1), AF.Exp, scale=LN2,
            bias=cstcol((mine + man) * LN2)), waits=[tcl])
        tt_ = V_(lambda e, h: e.tensor_tensor(fs(2), A, fs(3), ALU.mult),
                 waits=[trs])
        z_done2 = tt_ if skip_stage1 else z_done
        ta2 = S_(lambda e, h: e.activation(iA, fs(2), AF.Identity,
                                           bias=cstcol(4096.0)),
                 waits=[tt_])
        tc2 = V_(lambda e, h: e.scalar_tensor_tensor(
            fs(1), iA, -4095.5, fs(2), ALU.add, ALU.is_le), waits=[ta2])
        tr = V_(lambda e, h: e.tensor_tensor(fs(3), iA, fs(1), ALU.add),
                waits=[tc2])
        tq = V_(lambda e, h: e.scalar_tensor_tensor(
            out_ap, fs(3), -4096.0, SCB, ALU.add, ALU.mult),
            waits=[tr, tsc])
        chain_done[p] = tq
        return tq, z_done2

    W2R = {}

    def w2q_gen(l):
        toks = []
        for hc in range(32):
            done = []
            for half in range(2):
                pn = qpar[0]
                zin = scr[:, (4 * pn) * 512:(4 * pn) * 512 + 512]
                wl = [chain_done[pn]] if chain_done[pn] is not None else []
                tw = DMAw(lambda e, h, hc=hc, half=half, zin=zin, l=l:
                          e.dma_start(zin, wff[l]["w2"][hc][:,
                                      half * 512:half * 512 + 512]),
                          waits=wl, sem="DZ0" if pn == 0 else "DZ1")
                tq, _ = emit_quant(
                    zin, 512,
                    w2qs[:, half * 512:half * 512 + 512],
                    "e4m3", skip_stage1=True,
                    waits=[tw] + ruse("w2qs"))
                done.append(tq)
            ts_ = DMAw(lambda e, h, hc=hc, l=l: e.dma_start(
                w2q_sc[l][hc], w2qs[:, 0:1024]), waits=done,
                sem="DS0" if l == 0 else "DS1")
            rdone("w2qs", ts_)
            toks.append(ts_)
            W2R[l] = ts_
            yield

    def emit_lambda():
        t1 = V_(lambda e, h: e.tensor_tensor(
            sml[0:1, 0:128], lamt[0:1, 0:128], lamt[0:1, 128:256],
            ALU.mult), waits=[t_lam])
        t2a = V_(lambda e, h: e.reduce_sum(sml2[0:1, 0:1], sml[0:1, 0:64],
                                           AX.X), waits=[t1])
        t2b = V_(lambda e, h: e.reduce_sum(sml2[0:1, 1:2],
                                           sml[0:1, 64:128], AX.X),
                 waits=[t1])
        t3 = S_(lambda e, h: e.activation(sml2[0:1, 2:4], sml2[0:1, 0:2],
                                          AF.Exp), waits=[t2a, t2b])
        t6 = V_(lambda e, h: e.scalar_tensor_tensor(
            sml2[0:1, 4:5], sml2[0:1, 3:4], -LAMBDA_INIT,
            sml2[0:1, 2:3], ALU.add, ALU.subtract), waits=[t3])
        th = V_(lambda e, h: e.tensor_copy(smlb[0:1, 0:1],
                                           sml2[0:1, 4:5]), waits=[t6])
        tl_ = V_(lambda e, h: e.scalar_tensor_tensor(
            smlb[0:1, 2:3], smlb[0:1, 0:1], -1.0, sml2[0:1, 4:5],
            ALU.mult, ALU.add), waits=[th])
        t7 = emit_bcast2([smlb[0:1, 0:2], smlb[0:1, 2:4]],
                         [sml2[:, 5:6], None], [tl_])
        return t7

    NLAM = sml2[:, 5:6]

    def l0_prep_side(side, wbuf, tw, rawbuf, outbuf, xn_dep):
        out_toks = []
        for tc in range(4):
            sl = PSAB[tc % 2][:, 0:512]
            pk = ("pab", tc % 2)
            mt = None
            for ic in range(8):
                mt = (T_ if ic == 7 else T0_)(
                    lambda e, h, ic=ic, tc=tc, sl=sl, wbuf=wbuf:
                        e.matmul(sl, wbuf[:, ic * 128:(ic + 1) * 128],
                                 xnT[:, ic * T + tc * 512:
                                     ic * T + tc * 512 + 512],
                                 start=(ic == 0), stop=(ic == 7)),
                        waits=(([tw] + xn_dep + ruse(pk))
                               if ic == 0 else []))
            if PREP_STAGE <= 0:
                rdone(pk, mt)
                out_toks.append(mt)
                continue
            t_raw = V_(lambda e, h, tc=tc, sl=sl, rawbuf=rawbuf:
                       e.tensor_copy(rawbuf[:, tc * 512:(tc + 1) * 512],
                                     sl), waits=[mt])
            sq = outbuf[:, tc * 512:(tc + 1) * 512]
            t_sq = S_(lambda e, h, sl=sl, sq=sq: e.activation(
                sq, sl, AF.Square), waits=[mt, t_raw])
            rdone(pk, [t_raw, t_sq])
            if PREP_STAGE <= 1:
                out_toks.append(t_sq)
                rdone("pcb", t_raw)
                continue
            t_r0 = T_(lambda e, h, sq=sq: e.matmul(
                psS[:, 0:512], onesq[0:64, 0:128], sq[0:64, :],
                start=True, stop=True),
                waits=[t_sq, t_blkc] + ruse("psS"))
            t_r1 = T_(lambda e, h, sq=sq: e.matmul(
                psR[:, 0:512], onesq[64:128, 0:128], sq[64:128, :],
                start=True, stop=True),
                waits=[t_sq] + ruse("psR"))
            bias_r = cstcol(-math.log(8.0)) if side == 0 else cstcol(0.0)
            tu0 = S_(lambda e, h: e.activation(
                rmsr[0:64, 0:512], psS[0:64, 0:512], AF.Ln,
                scale=1.0 / 64.0, bias=cstcol(RMS_EPS)[0:64, :]),
                waits=[t_r0] + ruse("rmsr"))
            rdone("psS", tu0)
            tu1 = S_(lambda e, h: e.activation(
                rmsr[64:128, 0:512], psR[64:128, 0:512], AF.Ln,
                scale=1.0 / 64.0, bias=cstcol(RMS_EPS)[64:128, :]),
                waits=[t_r1])
            rdone("psR", tu1)
            t_bc = S_(lambda e, h, bias_r=bias_r: e.activation(
                rmsb[:, 0:512], rmsr[:, 0:512], AF.Exp, scale=-0.5,
                bias=bias_r), waits=[tu0, tu1] + ruse("rmsb"))
            rdone("rmsr", t_bc)
            if PREP_STAGE <= 2:
                out_toks.append(t_bc)
                continue
            t_sw = T_(lambda e, h, tc=tc, rawbuf=rawbuf: e.matmul(
                psR[:, 0:512], perm0[:],
                rawbuf[:, tc * 512:(tc + 1) * 512], start=True, stop=True),
                waits=[t_raw, t_perm0] + ruse("psR"))
            tm1 = V_(lambda e, h, tc=tc, rawbuf=rawbuf: e.tensor_tensor(
                pcb[:, 0:512], rawbuf[:, tc * 512:(tc + 1) * 512],
                ropeC[:, tc * 512:(tc + 1) * 512], ALU.mult),
                waits=[t_raw, t_ropeC] + ruse("pcb"))
            ts2 = V_(lambda e, h, tc=tc: e.tensor_tensor(
                pcb[:, 512:1024], psR[:, 0:512],
                ropeS[:, tc * 512:(tc + 1) * 512], ALU.mult),
                waits=[t_sw, t_ropeS])
            rdone("psR", ts2)
            tsum = V_(lambda e, h: e.tensor_tensor(
                pcb[:, 0:512], pcb[:, 0:512], pcb[:, 512:1024], ALU.add),
                waits=[tm1, ts2])
            tq = V_(lambda e, h, tc=tc, outbuf=outbuf: e.tensor_tensor(
                outbuf[:, tc * 512:(tc + 1) * 512], pcb[:, 0:512],
                rmsb[:, 0:512], ALU.mult), waits=[tsum, t_bc])
            rdone("pcb", tq)
            rdone("rmsb", tq)
            radd("ropes", tq)
            out_toks.append(tq)
        return out_toks

    def l0_head(hd, nlam_tok, xn_dep, out_toks, pump=None):
        tw_q = DMAg(lambda e, h, hd=hd: e.dma_start(wbA[:], wq0[hd]),
                    waits=ruse("wbA"), sem="DA")
        tw_k = DMAg(lambda e, h, hd=hd: e.dma_start(wbB[:], wk0[hd]),
                    waits=ruse("wbB"), sem="DB")
        tw_v = DMAg(lambda e, h, hd=hd: e.dma_start(wbC[:], wv0[hd]),
                    waits=ruse("wbC"), sem="DC")
        tqs = l0_prep_side(0, wbA, tw_q, p0, qh, xn_dep)
        rdone("wbA", tqs[-1])
        tks = l0_prep_side(1, wbB, tw_k, p1, kh, xn_dep)
        rdone("wbB", tks[-1])
        if ATTN_STAGE <= 1:
            out_toks.extend(tqs + tks)
            return out_toks
        vt = []
        for qt in range(16):
            sl = PSCD[qt % 2][:, 0:128]
            pk = ("pcd", qt % 2)
            mt = None
            for ic in range(8):
                mt = (T_ if ic == 7 else T0_)(
                    lambda e, h, ic=ic, qt=qt, sl=sl: e.matmul(
                    sl, xnT[:, ic * T + qt * 128:ic * T + qt * 128 + 128],
                    wbC[:, ic * 128:(ic + 1) * 128],
                    start=(ic == 0), stop=(ic == 7)),
                    waits=(([tw_v] + xn_dep + ruse(pk))
                           if ic == 0 else []))
            vdst = vtok[:, qt * VS:qt * VS + 130].rearrange(
                "p (a b) -> p a b", b=65)[:, :, 0:64]
            vsrc = sl[:, 0:128].rearrange("p (a b) -> p a b", b=64)
            tv = V_(lambda e, h, vdst=vdst, vsrc=vsrc: e.tensor_copy(
                vdst, vsrc), waits=[mt])
            rdone(pk, tv)
            vt.append(tv)
        rdone("wbC", vt[-1])
        if ATTN_STAGE <= 2:
            out_toks.extend(vt)
            return out_toks

        def stageA(qt):
            par = qt % 2
            exp_by = []
            for side, pbuf in ((0, p0[:, par * T:par * T + T]),
                               (1, p1[:, par * T:par * T + T])):
                side_toks = []
                ng = (qt + 4) // 4
                for g in range(ng):
                    jn = min(4, qt + 1 - 4 * g)
                    sl = PSAB[g % 2][:, 0:512]
                    pk = ("pab", g % 2)
                    mlast = None
                    for j in range(jn):
                        kb = 4 * g + j
                        mlast = (T_ if j == jn - 1 else T0_)(
                            lambda e, h, j=j, kb=kb, side=side, qt=qt,
                            sl=sl: e.matmul(
                            sl[:, j * 128:(j + 1) * 128],
                            kh[64 * side:64 * side + 64,
                               kb * 128:(kb + 1) * 128],
                            qh[64 * side:64 * side + 64,
                               qt * 128:(qt + 1) * 128],
                            start=True, stop=True),
                            waits=(([tqs[-1], tks[-1]] + ruse(pk))
                                   if j == 0 else []))
                    wl = [mlast, t_mask]
                    if g == ng - 1:
                        jd = qt - 4 * g
                        tmask = V_(lambda e, h, jd=jd, sl=sl:
                                   e.tensor_tensor(
                            sl[:, jd * 128:(jd + 1) * 128],
                            sl[:, jd * 128:(jd + 1) * 128],
                            maskb[:], ALU.add), waits=wl)
                        wl = [tmask]
                    texp = S_(lambda e, h, jn=jn, g=g, pbuf=pbuf,
                              sl=sl: e.activation(
                        pbuf[:, g * 512:g * 512 + jn * 128],
                        sl[:, 0:jn * 128], AF.Exp, bias=cstcol(-8.0)),
                        waits=wl + ruse(("pex", side, par)))
                    rdone(pk, texp)
                    side_toks.append(texp)
                exp_by.append(side_toks)
            return exp_by

        def stageB(qt, exp_by):
            par = qt % 2
            pvps = PSCD[par]
            pvk = ("pcd", par)
            pvt = None
            for side, base in ((0, 0), (1, 256)):
                pbuf = (p0 if side == 0 else p1)[:, par * T:par * T + T]
                for kb in range(qt + 1):
                    pvt = (T_ if kb == qt else T0_)(
                        lambda e, h, kb=kb, qt=qt, base=base, pbuf=pbuf:
                        e.matmul(
                        pvps[:, base:base + 129],
                        pbuf[:, kb * 128:(kb + 1) * 128],
                        vtok[:, kb * VS:kb * VS + 129],
                        start=(kb == 0), stop=(kb == qt)),
                        waits=[exp_by[side][kb // 4], vt[kb]] +
                        (ruse(pvk) if kb == 0 and side == 0 else []))
                rdone(("pex", side, par), pvt)
            c8 = 8 + 16 * par
            zsrc = pvps[:, 0:512].rearrange(
                "p (a b) -> p a b", b=256)[:, :, 64:65]
            zdst = sml2[:, c8:c8 + 2].rearrange(
                "p (a b) -> p a b", b=1)
            rz = V_(lambda e, h, zsrc=zsrc, zdst=zdst: e.reciprocal(
                zdst, zsrc), waits=[pvt])
            pv1 = pvps[:, 256:386].rearrange(
                "p (a b) -> p a b", b=65)[:, :, 0:64]
            pv0 = pvps[:, 0:130].rearrange(
                "p (a b) -> p a b", b=65)[:, :, 0:64]
            tmp = pcb[:, 128:256].rearrange("p (a b) -> p a b", b=64)
            cmb = pcb[:, 0:128].rearrange("p (a b) -> p a b", b=64)
            t1 = V_(lambda e, h, c8=c8, pv1=pv1, tmp=tmp: e.tensor_scalar(
                tmp, pv1, sml2[:, c8 + 1:c8 + 2],
                NLAM, ALU.mult, ALU.mult),
                waits=[rz, nlam_tok] + ruse("pcb"))
            t2b = V_(lambda e, h, c8=c8, pv0=pv0, tmp=tmp, cmb=cmb:
                     e.scalar_tensor_tensor(
                cmb, pv0, sml2[:, c8:c8 + 1],
                tmp, ALU.mult, ALU.add), waits=[t1])
            rdone(pvk, t2b)
            tst = V_(lambda e, h: e.bn_stats(
                stats[:, 0:6], pcb[:, 0:128]),
                waits=[t2b] + ruse("aggr0", "stats"))
            tag = V_(lambda e, h: e.bn_aggr(aggr[:, 0:2], stats[:, 0:6]),
                     waits=[tst])
            rdone("stats", tag)
            tu = S_(lambda e, h: e.activation(sml2[:, 11:12], aggr[:, 1:2],
                                              AF.Ln, bias=cstcol(LN_EPS)),
                    waits=[tag] + ruse("lnsml"))
            tr_ = S_(lambda e, h: e.activation(
                sml2[:, 12:13], sml2[:, 11:12], AF.Exp, scale=-0.5,
                bias=cstcol(math.log(1.0 - LAMBDA_INIT))), waits=[tu])
            tln = V_(lambda e, h: e.tensor_scalar(
                pcb[:, 256:384], pcb[:, 0:128], aggr[:, 0:1],
                sml2[:, 12:13], ALU.subtract, ALU.mult),
                waits=[tr_])
            rdone("aggr0", tln)
            rdone("lnsml", tln)
            ttr = T_(lambda e, h: e.transpose(psU[:, 0:128],
                                              pcb[:, 256:384],
                                              ident[:]),
                     waits=[tln] + ruse(("psU", 0)))
            tev = V_(lambda e, h, hd=hd, qt=qt: e.tensor_copy(
                aT[:, hd * T + qt * 128:hd * T + qt * 128 + 128],
                psU[:, 0:128]), waits=[ttr])
            rdone(("psU", 0), tev)
            rdone("pcb", [tln, ttr, tev])
            out_toks.append(tev)

        prev = None
        for qt in range(16):
            eb = stageA(qt)
            if prev is not None:
                stageB(qt - 1, prev)
                if pump is not None and qt % 2 == 0:
                    pump(1)
            prev = eb
        stageB(15, prev)
        if pump is not None:
            pump(1)
        return out_toks

    def emit_out_proj(wdram, src, src_toks, bias_colst, n_oc, xadd_toks):
        for oc in range(n_oc):
            par = oc % 2
            wb = wbA if par == 0 else wbB
            tw = DMAg(lambda e, h, oc=oc, wb=wb: e.dma_start(
                wb[:], wdram[oc]),
                waits=ruse("wbA" if par == 0 else "wbB"),
                sem="DA" if par == 0 else "DB")
            mlast = None
            for tc in range(4):
                qd = tc % 2
                sl = PSAB[qd][:, 0:512]
                pk = ("pab", qd)
                mt = None
                for ic in range(8):
                    mt = (T_ if ic == 7 else T0_)(
                        lambda e, h, ic=ic, tc=tc, sl=sl, wb=wb:
                            e.matmul(sl,
                                     wb[:, ic * 128:(ic + 1) * 128],
                                     src[:, ic * T + tc * 512:
                                         ic * T + tc * 512 + 512],
                                     start=(ic == 0), stop=(ic == 7)),
                            waits=(([tw] + src_toks + ruse(pk))
                                   if ic == 0 else []))
                mlast = mt
                if bias_colst is not None:
                    tev = V_(lambda e, h, sl=sl, oc=oc: e.tensor_scalar(
                        pcb[:, 0:512], sl, bias_colst[:, oc:oc + 1], None,
                        ALU.add), waits=[mt] + ruse("pcb"))
                else:
                    tev = V_(lambda e, h, sl=sl: e.tensor_copy(
                        pcb[:, 0:512], sl), waits=[mt] + ruse("pcb"))
                rdone(pk, tev)
                lastadd = None
                for j in range(4):
                    ttr = T_(lambda e, h, j=j: e.transpose(
                        psU[:, 128:256], pcb[:, j * 128:(j + 1) * 128],
                        ident[:]), waits=[tev] + ruse(("psU", 1)))
                    tqi = tc * 4 + j
                    tadd = V_(lambda e, h, tqi=tqi, oc=oc:
                              e.scalar_tensor_tensor(
                        x_res[:, tqi * 1024 + oc * 128:
                              tqi * 1024 + oc * 128 + 128],
                        psU[:, 128:256], 0.0,
                        x_res[:, tqi * 1024 + oc * 128:
                              tqi * 1024 + oc * 128 + 128],
                        ALU.add, ALU.add), waits=[ttr])
                    rdone(("psU", 1), tadd)
                    lastadd = tadd
                    xadd_toks.append(tadd)
                rdone("pcb", lastadd)
            rdone("wbA" if par == 0 else "wbB", mlast)
        return xadd_toks

    def emit_ffn(l, xn_dep, xadd_toks, w2ready):
        tb2q, _ = emit_quant(b2cols[l][:, 0:8], 8, b2qcols[l][:, 0:8],
                             "fp16", skip_stage1=True, waits=[t_b2[l]])
        for blk in range(2):
            hq_toks = {}
            for hc in range(32):
                par = hc % 2
                wb = wbA if par == 0 else wbB
                tw = DMAg(lambda e, h, hc=hc, wb=wb, l=l: e.dma_start(
                    wb[:], wff[l]["w1"][hc]),
                    waits=ruse("wbA" if par == 0 else "wbB"),
                    sem="DA" if par == 0 else "DB")
                hts = []
                mm_l = None
                for half in range(2):
                    tok0 = blk * 1024 + half * 512
                    qd = (2 * hc + half) % 2
                    sl = PSAB[qd][:, 0:512]
                    pk = ("pab", qd)
                    mt = None
                    for ic in range(8):
                        mt = (T_ if ic == 7 else T0_)(
                            lambda e, h, ic=ic, sl=sl, wb=wb,
                                tok0=tok0: e.matmul(
                            sl, wb[:, ic * 128:(ic + 1) * 128],
                            xnT[:, ic * T + tok0:ic * T + tok0 + 512],
                            start=(ic == 0), stop=(ic == 7)),
                            waits=(([tw] + xn_dep + ruse(pk))
                                   if ic == 0 else []))
                    mm_l = mt
                    out_sl = hqv[:, hc * 1024 + half * 512:
                                 hc * 1024 + half * 512 + 512]
                    tqd, zd = emit_quant(sl, 512, out_sl, "e4m3",
                                         bias_col=b1cols[l][:, hc:hc + 1],
                                         relu=True,
                                         waits=[mt, t_b1[l]] +
                                         ruse(("hqw", hc)))
                    rdone(pk, zd)
                    hts.append(tqd)
                rdone("wbA" if par == 0 else "wbB", mm_l)
                hq_toks[hc] = hts
                rdone(("hqw", hc), hts)
            for tc2 in range(2):
                tok_in_blk = tc2 * 512
                for og in range(2):
                    mlast = [None] * 4
                    for hc in range(32):
                        par = hc % 2
                        tws = DMAw(lambda e, h, hc=hc, par=par, l=l:
                                   e.dma_start(
                            w2qs[:, par * 1024:par * 1024 + 1024],
                            w2q_sc[l][hc]),
                            waits=[w2ready] + ruse(("w2qsb", par),
                                                    "w2qs"),
                            sem="W2A" if par == 0 else "W2B")
                        for oc4 in range(4):
                            oc = og * 4 + oc4
                            mlast[oc4] = (T_ if (hc == 31 or oc4 == 3) else T0_)(
                                lambda e, h, oc4=oc4, oc=oc, par=par,
                                hc=hc, tok_in_blk=tok_in_blk: e.matmul(
                                    PS4[oc4][:, 0:512],
                                    w2qs[:, par * 1024 + oc * 128:
                                         par * 1024 + oc * 128 + 128],
                                    hqv[:, hc * 1024 + tok_in_blk:
                                        hc * 1024 + tok_in_blk + 512],
                                    start=(hc == 0), stop=(hc == 31)),
                                waits=([tws] + hq_toks[hc] +
                                       (ruse(("pab", 0), ("pab", 1),
                                             ("pcd", 0), ("pcd", 1))
                                        if hc == 0 else [])))
                        rdone(("w2qsb", par), mlast[3])
                        radd("w2qs", mlast[3])
                    for oc4 in range(4):
                        oc = og * 4 + oc4
                        sl = PS4[oc4][:, 0:512]
                        pk = [("pab", 0), ("pab", 1), ("pcd", 0),
                              ("pcd", 1)][oc4]
                        tqd, zd = emit_quant(
                            sl, 512, pcb[:, 0:512], "fp16",
                            bias_col=b2qcols[l][:, oc:oc + 1], relu=False,
                            waits=[mlast[oc4], tb2q] + ruse("pcb"))
                        rdone(pk, zd)
                        ttr = None
                        for j in range(4):
                            ttr = (T_ if j == 3 else T0_)(
                                lambda e, h, j=j: e.transpose(
                                psU[:, 128 + j * 128:256 + j * 128],
                                pcb[:, j * 128:(j + 1) * 128], ident[:]),
                                waits=(([tqd] + ruse(("psU", 1)))
                                       if j == 0 else []))
                        tq0 = blk * 8 + tc2 * 4
                        xrv = x_res[:, 0:16384].rearrange(
                            "p (a b) -> p a b", b=1024)[
                            :, tq0:tq0 + 4, oc * 128:oc * 128 + 128]
                        puv = psU[:, 128:640].rearrange(
                            "p (a b) -> p a b", b=128)
                        tadd = V_(lambda e, h, xrv=xrv, puv=puv:
                                  e.scalar_tensor_tensor(
                            xrv, puv, 0.0, xrv,
                            ALU.add, ALU.add), waits=[ttr])
                        rdone(("psU", 1), tadd)
                        xadd_toks.append(tadd)
                        rdone("pcb", tadd)
        return xadd_toks

    def pair_qk(ph, wdr, bcol, src, src_toks, out):
        tw = DMAg(lambda e, h, ph=ph, wdr=wdr: e.dma_start(
            wbA[:, 0:256], wdr[ph]), waits=ruse("wbA"), sem="DA")
        lt = None
        for tc in range(4):
            qd = tc % 2
            sl = PSAB[qd][:, 0:512]
            pk = ("pab", qd)
            mt = None
            for cc in range(2):
                mt = (T_ if cc == 1 else T0_)(
                    lambda e, h, cc=cc, tc=tc, sl=sl, src=src: e.matmul(
                    sl, wbA[:, cc * 128:(cc + 1) * 128],
                    src[:, cc * T + tc * 512:cc * T + tc * 512 + 512],
                    start=(cc == 0), stop=(cc == 1)),
                    waits=(([tw] + src_toks + ruse(pk))
                           if cc == 0 else []))
            te = S_(lambda e, h, sl=sl, tc=tc, out=out, bcol=bcol:
                    e.activation(out[:, tc * 512:(tc + 1) * 512], sl,
                                 AF.Gelu, bias=bcol), waits=[mt])
            rdone(pk, te)
            lt = te
        rdone("wbA", lt)
        return lt

    def rope_pair(wdr, bcol, out, xn_dep, trc, trs):
        tw = DMAg(lambda e, h, wdr=wdr: e.dma_start(wbA[:], wdr[0]),
                  waits=ruse("wbA"), sem="DA")
        lt = None
        for tc in range(4):
            qd = tc % 2
            sl = PSAB[qd][:, 0:512]
            pk = ("pab", qd)
            mt = None
            for ic in range(8):
                mt = (T_ if ic == 7 else T0_)(
                    lambda e, h, ic=ic, tc=tc, sl=sl: e.matmul(
                    sl, wbA[:, ic * 128:(ic + 1) * 128],
                    xnT[:, ic * T + tc * 512:ic * T + tc * 512 + 512],
                    start=(ic == 0), stop=(ic == 7)),
                    waits=(([tw] + xn_dep + ruse(pk))
                           if ic == 0 else []))
            traw = S_(lambda e, h, sl=sl, tc=tc, bcol=bcol: e.activation(
                p1[:, tc * 512:(tc + 1) * 512], sl, AF.Identity,
                bias=bcol), waits=[mt] + ruse("p1r"))
            rdone(pk, traw)
            tsw = T_(lambda e, h, tc=tc: e.matmul(
                psS[:, 0:512], perm1[:],
                p1[:, tc * 512:(tc + 1) * 512], start=True, stop=True),
                waits=[traw, t_perm1] + ruse("psS"))
            tm1 = V_(lambda e, h, tc=tc: e.tensor_tensor(
                pcb[:, 0:512], p1[:, tc * 512:(tc + 1) * 512],
                ropeC[:, tc * 512:(tc + 1) * 512], ALU.mult),
                waits=[traw, trc] + ruse("pcb"))
            ts2 = V_(lambda e, h, tc=tc: e.tensor_tensor(
                pcb[:, 512:1024], psS[:, 0:512],
                ropeS[:, tc * 512:(tc + 1) * 512], ALU.mult),
                waits=[tsw, trs])
            rdone("psS", ts2)
            tsum = V_(lambda e, h: e.tensor_tensor(
                pcb[:, 0:512], pcb[:, 0:512], pcb[:, 512:1024], ALU.add),
                waits=[tm1, ts2])
            tg = S_(lambda e, h, tc=tc, out=out: e.activation(
                out[:, tc * 512:(tc + 1) * 512], pcb[:, 0:512], AF.Gelu),
                waits=[tsum])
            rdone("pcb", tg)
            radd("p1r", tg)
            lt = tg
        rdone("wbA", lt)
        return lt

    def emit_bounds(ph):
        first = V_(lambda e, h: e.memset(sml[0:1, 16:32], 0.0),
                   waits=ruse("bnd"))
        tmx_toks = []
        for side, buf in ((0, qh), (1, kh)):
            for tc in range(4):
                sq = p1[:, tc * 512:(tc + 1) * 512]
                tsq = S_(lambda e, h, buf=buf, tc=tc, sq=sq:
                         e.activation(
                    sq, buf[:, tc * 512:(tc + 1) * 512], AF.Square),
                    waits=ruse("bsq", "p1r") + [first])
                for h2 in range(2):
                    ps_ = psS if h2 == 0 else psR
                    pk_ = "psS" if h2 == 0 else "psR"
                    tred = T_(lambda e, h, sq=sq, h2=h2, ps_=ps_:
                              e.matmul(
                        ps_[:, 0:512],
                        onesq[64 * h2:64 * h2 + 64, 0:128],
                        sq[64 * h2:64 * h2 + 64, :],
                        start=True, stop=True),
                        waits=[tsq, t_blkc] + ruse(pk_))
                    cix = 16 + side * 8 + h2 * 4 + tc
                    tmx = V_(lambda e, h, cix=cix, ps_=ps_: e.reduce_max(
                        sml[0:1, cix:cix + 1],
                        ps_[0:1, 0:512], AX.X), waits=[tred])
                    rdone(pk_, tmx)
                    tmx_toks.append(tmx)
                rdone("bsq", tmx_toks[-2:])
        tms = []
        for side in range(2):
            for h2 in range(2):
                c0 = 16 + side * 8 + h2 * 4
                tms.append(V_(lambda e, h, c0=c0, side=side, h2=h2:
                              e.reduce_max(
                    sml2[0:1, 16 + side * 2 + h2:17 + side * 2 + h2],
                    sml[0:1, c0:c0 + 4], AX.X), waits=tmx_toks))
        tp = V_(lambda e, h: e.tensor_tensor(
            sml2[0:1, 20:22], sml2[0:1, 16:18], sml2[0:1, 18:20],
            ALU.mult), waits=tms)
        tp = S_(lambda e, h: e.activation(sml2[0:1, 22:24],
                                          sml2[0:1, 20:22], AF.Ln,
                                          bias=cstcol(2.0 ** -60)[0:1, :]),
                waits=[tp])
        tp = S_(lambda e, h: e.activation(
            sml2[0:1, 24:26], sml2[0:1, 22:24], AF.Exp, scale=0.5,
            bias=cstcol(-math.log(8.0))[0:1, :]), waits=[tp])
        tp = V_(lambda e, h: e.tensor_scalar(
            smlb[0:1, 4:6], sml2[0:1, 24:26], 40.0, -1.0, ALU.min,
            ALU.mult), waits=[tp])
        t2 = emit_bcast2([smlb[0:1, 4:6]],
                         [bcols[:, 2 * ph:2 * ph + 1],
                          bcols[:, 2 * ph + 1:2 * ph + 2]], [tp])
        rdone("bnd", [t2])
        return t2

    def l1_attention(xn_dep, aT_toks, pump=None):
        DMAg(lambda e, h: e.dma_start(ropeC[:], ropec1[0]),
             waits=ruse("ropes"), sem="Dr")
        trs = DMAg(lambda e, h: e.dma_start(ropeS[:], ropec1[1]),
                   sem="Dr")
        trc = trs
        cq_toks = []
        ckv_toks = []
        for (wdr, bc_sl, out, toks) in ((wq1, bias1[:, 0:2], cqb, cq_toks),
                                        (wkv1, bias2[:, 0:2], ckvb,
                                         ckv_toks)):
            for oc in range(2):
                tw = DMAg(lambda e, h, oc=oc, wdr=wdr: e.dma_start(
                    wbA[:], wdr[oc]), waits=ruse("wbA"), sem="DA")
                lt = None
                for tc in range(4):
                    qd = tc % 2
                    sl = PSAB[qd][:, 0:512]
                    pk = ("pab", qd)
                    mt = None
                    for ic in range(8):
                        mt = (T_ if ic == 7 else T0_)(
                            lambda e, h, ic=ic, tc=tc, sl=sl: e.matmul(
                            sl, wbA[:, ic * 128:(ic + 1) * 128],
                            xnT[:, ic * T + tc * 512:
                                ic * T + tc * 512 + 512],
                            start=(ic == 0), stop=(ic == 7)),
                            waits=(([tw] + xn_dep + t_bias +
                                    ruse(pk))
                                   if ic == 0 else []))
                    te = S_(lambda e, h, sl=sl, oc=oc, tc=tc, out=out,
                            bc_sl=bc_sl: e.activation(
                        out[:, oc * T + tc * 512:oc * T + tc * 512 + 512],
                        sl, AF.Gelu, bias=bc_sl[:, oc:oc + 1]),
                        waits=[mt])
                    rdone(pk, te)
                    toks.append(te)
                    lt = te
                rdone("wbA", lt)
        for ph in range(8):
            if ph < 7:
                tq_ = pair_qk(ph, wqc1, bias1[:, 2 + ph:3 + ph], cqb,
                              cq_toks, qh)
                tk_ = pair_qk(ph, wkc1, bias2[:, 2 + ph:3 + ph], ckvb,
                              ckv_toks, kh)
            else:
                tq_ = rope_pair(wqr1, bias1[:, 9:10], qh, xn_dep, trc, trs)
                tk_ = rope_pair(wkr1, bias2[:, 9:10], kh, xn_dep, trc, trs)
            twv = DMAg(lambda e, h, ph=ph: e.dma_start(wbC[:, 0:256],
                                                       wv1[ph]),
                       waits=ruse("wbC"), sem="DC")
            vt = []
            for qt in range(16):
                sl = PSCD[qt % 2][:, 0:128]
                pk = ("pcd", qt % 2)
                mt = None
                for cc in range(2):
                    mt = (T_ if cc == 1 else T0_)(
                        lambda e, h, cc=cc, qt=qt, sl=sl: e.matmul(
                        sl, ckvb[:, cc * T + qt * 128:
                                 cc * T + qt * 128 + 128],
                        wbC[:, cc * 128:(cc + 1) * 128],
                        start=(cc == 0), stop=(cc == 1)),
                        waits=(([twv] + ckv_toks + ruse(pk))
                               if cc == 0 else []))
                vdst = vtok[:, qt * VS:qt * VS + 130].rearrange(
                    "p (a b) -> p a b", b=65)[:, :, 0:64]
                vsrc = sl[:, 0:128].rearrange("p (a b) -> p a b", b=64)
                tv = V_(lambda e, h, vdst=vdst, vsrc=vsrc: e.tensor_copy(
                    vdst, vsrc), waits=[mt])
                rdone(pk, tv)
                vt.append(tv)
            rdone("wbC", vt[-1])
            bnd_tok = emit_bounds(ph)

            def stageA1(qt, h2):
                par = (2 * qt + h2) % 2
                po = p0[:, par * T:par * T + T]
                hh = 2 * ph + h2
                side_toks = []
                ng = (qt + 4) // 4
                for g in range(ng):
                    jn = min(4, qt + 1 - 4 * g)
                    sl = PSAB[g % 2][:, 0:512]
                    pk = ("pab", g % 2)
                    mlast = None
                    for j in range(jn):
                        kb = 4 * g + j
                        mlast = (T_ if j == jn - 1 else T0_)(
                            lambda e, h, j=j, kb=kb, h2=h2, qt=qt,
                            sl=sl: e.matmul(
                            sl[:, j * 128:(j + 1) * 128],
                            kh[64 * h2:64 * h2 + 64,
                               kb * 128:(kb + 1) * 128],
                            qh[64 * h2:64 * h2 + 64,
                               qt * 128:(qt + 1) * 128],
                            start=True, stop=True),
                            waits=(([tq_, tk_] + ruse(pk))
                                   if j == 0 else []))
                    wl = [mlast, t_mask]
                    if g == ng - 1:
                        jd = qt - 4 * g
                        tmask = V_(lambda e, h, jd=jd, sl=sl:
                                   e.tensor_tensor(
                            sl[:, jd * 128:(jd + 1) * 128],
                            sl[:, jd * 128:(jd + 1) * 128],
                            maskb[:], ALU.add), waits=wl)
                        wl = [tmask]
                    texp = S_(lambda e, h, jn=jn, g=g, hh=hh, sl=sl,
                              po=po: e.activation(
                        po[:, g * 512:g * 512 + jn * 128],
                        sl[:, 0:jn * 128], AF.Exp, scale=0.125,
                        bias=bcols[:, hh:hh + 1]),
                        waits=wl + [bnd_tok] + ruse(("pex", par)))
                    rdone(pk, texp)
                    side_toks.append(texp)
                return side_toks

            def stageB1(qt, h2, chunks):
                par = (2 * qt + h2) % 2
                po = p0[:, par * T:par * T + T]
                pvps = PSCD[qt % 2]
                pvk = ("pcd", qt % 2)
                base = 256 * h2
                pvt = None
                for kb in range(qt + 1):
                    pvt = (T_ if kb == qt else T0_)(
                        lambda e, h, kb=kb, qt=qt, h2=h2, base=base,
                        po=po: e.matmul(
                        pvps[:, base:base + 65],
                        po[:, kb * 128:(kb + 1) * 128],
                        vtok[:, kb * VS + 64 * h2:kb * VS + 64 * h2 + 65],
                        start=(kb == 0), stop=(kb == qt)),
                        waits=[chunks[kb // 4], vt[kb]] +
                        (ruse(pvk) if kb == 0 and h2 == 0 else []))
                rdone(("pex", par), pvt)
                return pvt

            def tail1(qt, pv0, pv1):
                pvps = PSCD[qt % 2]
                pvk = ("pcd", qt % 2)
                c8 = 8 + 2 * (qt % 2)
                rz0 = V_(lambda e, h, c8=c8, pvps=pvps: e.reciprocal(
                    sml2[:, c8:c8 + 1], pvps[:, 64:65]),
                    waits=[pv0, pv1])
                rz1 = V_(lambda e, h, c8=c8, pvps=pvps: e.reciprocal(
                    sml2[:, c8 + 1:c8 + 2], pvps[:, 256:257]),
                    waits=[pv1])
                tno0 = V_(lambda e, h, c8=c8, pvps=pvps: e.tensor_scalar(
                    pcb[:, 0:64], pvps[:, 0:64],
                    sml2[:, c8:c8 + 1], None, ALU.mult),
                    waits=[rz0] + ruse("pcb"))
                tno1 = V_(lambda e, h, c8=c8, pvps=pvps: e.tensor_scalar(
                    pcb[:, 64:128], pvps[:, 257:321],
                    sml2[:, c8 + 1:c8 + 2], None, ALU.mult),
                    waits=[rz1])
                rdone(pvk, tno1)
                ttr = T_(lambda e, h: e.transpose(
                    psU[:, 0:128], pcb[:, 0:128], ident[:]),
                    waits=[tno0, tno1] + ruse(("psU", 0)))
                rdone("pcb", ttr)
                tev = V_(lambda e, h, ph=ph, qt=qt: e.tensor_scalar(
                    aT[:, ph * T + qt * 128:ph * T + qt * 128 + 128],
                    psU[:, 0:128], bvcol[:, ph:ph + 1], None,
                    ALU.add), waits=[ttr])
                rdone(("psU", 0), tev)
                aT_toks.append(tev)

            units = [(qt, h2) for qt in range(16) for h2 in range(2)]
            pend = {}
            prevu = None
            for u in units:
                chunks = stageA1(*u)
                pend[u] = [chunks, None]
                if prevu is not None:
                    pend[prevu][1] = stageB1(prevu[0], prevu[1],
                                             pend[prevu][0])
                    if pump is not None:
                        pump(1)
                    if prevu[1] == 1:
                        q0 = prevu[0]
                        tail1(q0, pend[(q0, 0)][1], pend[(q0, 1)][1])
                        del pend[(q0, 0)], pend[(q0, 1)]
                prevu = u
            pend[prevu][1] = stageB1(prevu[0], prevu[1], pend[prevu][0])
            tail1(15, pend[(15, 0)][1], pend[(15, 1)][1])
        return aT_toks

    lvl = SCHED_LEVEL
    final_dep = []

    def gen_w2():
        for g in (w2q_gen(0), w2q_gen(1)):
            for _ in g:
                yield

    def make_pump(gen):
        state = [gen is None]

        def pump(n=1):
            if state[0]:
                return
            try:
                for _ in range(n):
                    next(gen)
            except StopIteration:
                state[0] = True

        def drain():
            if state[0]:
                return
            for _ in gen:
                pass
            state[0] = True
        return pump, drain

    gen0 = gen_w2() if lvl >= 2 else None
    pump0, drain0 = make_pump(gen0)
    pump0(6)
    ln0 = emit_layernorm(t_x + [t_ident])
    xn_dep0 = [ln0[-1]]
    final_dep = [ln0[-1]]
    if lvl >= 1:
        nlam_tok = emit_lambda() if LAMBDA_ON else None
        l0_toks = []
        for hd in range(N_HEADS_L0):
            l0_head(hd, nlam_tok, xn_dep0, l0_toks, pump=pump0)
        drain0()
        final_dep = l0_toks[-1:] or final_dep
        if PROJ_ON:
            xadd0 = []
            emit_out_proj(wproj0, aT, l0_toks[-1:], None, 8, xadd0)
            final_dep = [xadd0[-1]]
    if lvl >= 2:
        emit_ffn(0, xn_dep0, xadd0, W2R[0])
        final_dep = [xadd0[-1]]
    if lvl >= 3:
        ln1 = emit_layernorm([xadd0[-1]])
        xn_dep1 = [ln1[-1]]
        aT_toks1 = []
        l1_attention(xn_dep1, aT_toks1)
        xadd1 = []
        emit_out_proj(wo1, aT, [aT_toks1[-1]], bocol, 8, xadd1)
        final_dep = [xadd1[-1]]
    if lvl >= 4:
        emit_ffn(1, xn_dep1, xadd1, W2R[1])
        final_dep = [xadd1[-1]]

    for i in range(16):
        DMAo(lambda e, h, i=i: e.dma_start(
            y3[i], x_res[:, i * 1024:(i + 1) * 1024]), waits=final_dep)
    P.op("sync", lambda e, h: None,
         waits=[(s, P.sem_counts[s]) for s in
                ("Do", "Dc", "Dg", "DA", "DB", "DC", "W2A", "W2B",
                 "DZ0", "DZ1", "DS0", "DS1", "Dr")
                if P.sem_counts.get(s, 0) > 0])

    with nc.Block() as block:
        @block.tensor
        def _(eng):
            for f in P.th["tensor"]:
                f(eng, H)

        @block.vector
        def _(eng):
            for f in P.th["vector"]:
                f(eng, H)

        @block.scalar
        def _(eng):
            for f in P.th["scalar"]:
                f(eng, H)

        @block.gpsimd
        def _(eng):
            for f in P.th["gpsimd"]:
                f(eng, H)

        @block.sync
        def _(eng):
            for f in P.th["sync"]:
                f(eng, H)
    es.close()
    return nc


def prep_inputs(inputs):
    g = {k: np.asarray(v) for k, v in inputs.items()}
    d = {}
    d["wq0"] = pack_lhsT(np.ascontiguousarray(g["L0_attn_Wq"].T))
    d["wk0"] = pack_lhsT(np.ascontiguousarray(g["L0_attn_Wk"].T))
    d["wv0"] = pack_lhsT(np.ascontiguousarray(g["L0_attn_Wv"].T))
    d["wproj0"] = pack_lhsT(np.ascontiguousarray(g["L0_attn_Wproj"].T))
    for l in range(2):
        d[f"w1_{l}"] = pack_lhsT(np.ascontiguousarray(g[f"L{l}_ffn_w1"].T))
        d[f"w2_{l}"] = np.ascontiguousarray(
            g[f"L{l}_ffn_w2"].T.astype(np.float32)).reshape(32, 128, 1024)
        d[f"b1c_{l}"] = pack_bias_cols(g[f"L{l}_ffn_b1"])
        d[f"b2c_{l}"] = pack_bias_cols(g[f"L{l}_ffn_b2"])
    d["wq1"] = pack_lhsT(g["L1_attn_wq"])
    d["bq1"] = pack_bias_cols(g["L1_attn_bq"])
    d["wqc1"] = pack_lhsT(g["L1_attn_wqc"])
    d["bqc1"] = pack_bias_cols(g["L1_attn_bqc"])
    d["wqr1"] = pack_lhsT(g["L1_attn_wqr"])
    d["bqr1"] = pack_bias_cols(g["L1_attn_bqr"])
    d["wkv1"] = pack_lhsT(g["L1_attn_wkv"])
    d["bkv1"] = pack_bias_cols(g["L1_attn_bkv"])
    d["wkr1"] = pack_lhsT(g["L1_attn_wkr"])
    d["bkr1"] = pack_bias_cols(g["L1_attn_bkr"])
    d["wkc1"] = pack_lhsT(g["L1_attn_wkc"])
    d["bkc1"] = pack_bias_cols(g["L1_attn_bkc"])
    d["wv1"] = pack_lhsT(g["L1_attn_wv"])
    d["bv1"] = pack_bias_cols(g["L1_attn_bv"])
    d["wo1"] = pack_lhsT(g["L1_attn_wo"])
    d["bo1"] = pack_bias_cols(g["L1_attn_bo"])
    d["lamv"] = np.concatenate(
        [g["L0_attn_lq1"], g["L0_attn_lq2"], g["L0_attn_lk1"],
         g["L0_attn_lk2"]]).astype(np.float32).reshape(1, 256)
    d["identc"] = np.eye(128, dtype=np.float32)
    d["permc0"] = perm_mat(32)
    d["permc1"] = perm_mat(64)
    d["maskc"] = causal_maskbias()
    d["ropec0"] = np.stack(rope_tables_l0())
    d["ropec1"] = np.stack(rope_tables_l1())
    return d


def kernel(**inputs):
    nc = build_program()
    shared = prep_inputs(inputs)
    x = np.asarray(inputs["L0_attn_x"])
    in_maps = []
    for b in range(NB):
        m = dict(shared)
        m["x"] = np.ascontiguousarray(x[b])
        in_maps.append(m)
    res = run_bass_kernel_spmd(nc, in_maps, core_ids=list(range(NB)))
    return np.stack([np.asarray(r["y"])
                     for r in res.results]).astype(np.float32)
